# version 30
# baseline (speedup 1.0000x reference)
import math
import contextlib
import numpy as np
import concourse.bass as bass
import concourse.mybir as mybir
from concourse.bass_utils import run_bass_kernel_spmd

F32 = mybir.dt.float32
BF16 = mybir.dt.bfloat16
I32 = mybir.dt.int32
AF = mybir.ActivationFunctionType
ALU = mybir.AluOpType
AX = mybir.AxisListType

T = 2048
DM = 1024
NEXP = 36
NTOK = T // 128
NDC = DM // 128
TWO_PI = 2.0 * math.pi


class Sem:
    def __init__(self, nc, es, name):
        self.h = es.enter_context(nc.semaphore(name))
        self.n = 0

    def bump(self, k=1):
        self.n += k
        return self.n


def _w(e, waits):
    for s, v in waits:
        if v > 0:
            e.wait_ge(s.h, v)


def build_program():
    nc = bass.Bass("TRN2", target_bir_lowering=False)
    es = contextlib.ExitStack()

    dx = nc.dram_tensor("x", [T, DM], F32, kind="ExternalInput").ap()
    dout = nc.dram_tensor("out", [T, DM], F32, kind="ExternalOutput").ap()
    dgup = nc.dram_tensor("gupT", [NEXP, 128, 2048], F32, kind="ExternalInput").ap()
    ddown = nc.dram_tensor("downT", [NEXP, 128, 1024], F32, kind="ExternalInput").ap()
    drout = nc.dram_tensor("routerT", [128, 256], F32, kind="ExternalInput").ap()
    dbias = nc.dram_tensor("rbias", [1, 32], F32, kind="ExternalInput").ap()
    dB = {d: nc.dram_tensor(f"B_{d}", [128, 4096], F32, kind="ExternalInput").ap() for d in "fb"}
    dCT = {d: nc.dram_tensor(f"CT_{d}", [128, 4096], F32, kind="ExternalInput").ap() for d in "fb"}
    dvec = {d: nc.dram_tensor(f"vec_{d}", [128, 6], F32, kind="ExternalInput").ap() for d in "fb"}
    dD = {d: nc.dram_tensor(f"D_{d}", [128, 8], F32, kind="ExternalInput").ap() for d in "fb"}
    hscr = nc.dram_tensor("hscr", [NEXP, 4, 128, 512], BF16, kind="Internal").ap()

    identf_d = nc.inline_tensor(np.eye(128, dtype=np.float32), name="identf_dram").ap()
    oh32_d = nc.inline_tensor(np.eye(32, dtype=np.float32), name="oh32f_dram").ap()
    onesf_d = nc.inline_tensor(np.ones((1, 128), dtype=np.float32), name="onesf_dram").ap()

    sb = lambda n, sh, dt: nc.alloc_sbuf_tensor(n, sh, dt).ap()
    xnT = sb("xnT", [128, NDC * T], BF16)
    xnTv = xnT.rearrange("p (dc t) -> p dc t", dc=NDC)
    xsT = {(d, ri): sb(f"xs_{d}{ri}", [128, 2 * T], BF16) for d in "fb" for ri in "ri"}
    CTb = {(d, ri): sb(f"CTb_{d}{ri}", [128, 2048], BF16) for d in "fb" for ri in "ri"}
    diagD = {d: sb(f"diagD_{d}", [128, 1024], BF16) for d in "fb"}
    routerTb = sb("routerTb", [128, 256], BF16)
    bias_bc = sb("bias_bc", [128, 32], F32)
    wTb = sb("wTb", [32, T], BF16)
    identf = sb("identf", [128, 128], F32)
    identb = sb("identb", [128, 128], BF16)
    oh32b = sb("oh32b", [32, 32], BF16)
    onesf = sb("onesf", [1, 128], F32)
    biasrow = sb("biasrow", [1, 32], F32)
    c_hpi = sb("c_hpi", [128, 1], F32)
    c_eps = sb("c_eps", [128, 1], F32)
    c_nh = sb("c_nh", [128, 1], F32)
    vt = {}
    for d in "fb":
        vt[d] = {k: sb(f"v_{d}_{k}", [128, 2], F32)
                 for k in ["delta", "elr", "ngd", "rr", "theta", "kf", "thr", "half",
                           "sh", "ch", "sinth", "costh", "lbr", "lbi", "l2m", "onem",
                           "lsr", "lsi", "dinv", "nr", "scr", "sci", "tmp"]}
        vt[d]["vecs"] = sb(f"vecs_{d}", [128, 6], F32)
        vt[d]["Dt"] = sb(f"Dt_{d}", [128, 8], F32)
        vt[d]["kint"] = sb(f"kint_{d}", [128, 2], I32)
        vt[d]["AB"] = sb(f"AB_{d}", [128, 44], F32)

    routerTb_lo = sb("routerTb_lo", [128, 256], BF16)
    xrl = [sb(f"xrl{i}", [128, 1024], F32) for i in range(2)]

    es_ab = contextlib.ExitStack()
    sbt = lambda st, n, sh, dt: st.enter_context(nc.sbuf_tensor(n, sh, dt)).ap()
    es_aonly = contextlib.ExitStack()
    sbr = lambda st, n, sh, dt: st.enter_context(nc.sbuf_tensor(n, sh, dt, side="right")).ap()
    stage = sbr(es_aonly, "stage", [128, 1024], F32)
    stage2 = sbr(es_aonly, "stage2", [128, 1024], F32)
    Bbar2 = sbr(es_aonly, "Bbar2", [128, 2048], BF16)
    xn_tok = [sbr(es_aonly, f"xn_tok{i}", [128, 1024], BF16) for i in range(2)]
    xnf_tok = sbr(es_aonly, "xnf_tok", [128, 1024], F32)
    xnlo_tok = [sbr(es_aonly, f"xnlo_tok{i}", [128, 1024], BF16) for i in range(2)]
    xnTlo = [sbr(es_aonly, f"xnTlo{i}", [128, 1024], BF16) for i in range(2)]
    routstage = sbr(es_aonly, "routstage", [128, 256], F32)
    r_lg = [sbr(es_aonly, f"r_lg{i}", [128, 32], F32) for i in range(2)]
    r_e = [sbr(es_aonly, f"r_e{i}", [128, 32], F32) for i in range(2)]
    r_cur = [sbr(es_aonly, f"r_cur{i}", [128, 32], F32) for i in range(2)]
    r_msk = [sbr(es_aonly, f"r_msk{i}", [128, 32], F32) for i in range(2)]
    r_wb = [sbr(es_aonly, f"r_wb{i}", [128, 32], BF16) for i in range(2)]
    r_m = [sbr(es_aonly, f"r_m{i}", [128, 1], F32) for i in range(2)]
    r_s1 = [sbr(es_aonly, f"r_s1{i}", [128, 1], F32) for i in range(2)]
    r_s2 = [sbr(es_aonly, f"r_s2{i}", [128, 1], F32) for i in range(2)]
    ln_bn = [sbr(es_aonly, f"ln_bn{i}", [128, 12], F32) for i in range(2)]
    ln_mv = [sbr(es_aonly, f"ln_mv{i}", [128, 2], F32) for i in range(2)]
    ln_rstd = [sbr(es_aonly, f"ln_rstd{i}", [128, 1], F32) for i in range(2)]
    Bbar1 = sbt(es_ab, "Bbar1", [128, 2048], BF16)
    BTb = {ri: sbt(es_ab, f"BTb_{ri}", [128, 2048], BF16) for ri in "ri"}
    BTbv = {ri: BTb[ri].rearrange("p (dc cc c) -> p dc cc c", dc=8, cc=2) for ri in "ri"}
    cosT = sbt(es_ab, "cosT", [128, T], F32)
    sinT = sbt(es_ab, "sinT", [128, T], F32)
    cmb = {ri: sbt(es_ab, f"cmb_{ri}", [128, T], F32) for ri in "ri"}
    bq = {ri: sbt(es_ab, f"bq_{ri}", [128, T], F32) for ri in "ri"}
    uu = {ri: sbt(es_ab, f"u_{ri}", [128, T], F32) for ri in "ri"}

    dM = Sem(nc, es, "dM")
    dX = [Sem(nc, es, f"dX{i}") for i in range(2)]
    dST = Sem(nc, es, "dST")
    dST2 = Sem(nc, es, "dST2")
    dBU = [Sem(nc, es, f"dBU{i}") for i in range(2)]
    dGU = [Sem(nc, es, f"dGU{i}") for i in range(2)]
    dDW = [Sem(nc, es, f"dDW{i}") for i in range(3)]
    dHL = [Sem(nc, es, f"dHL{i}") for i in range(2)]
    dHS = [Sem(nc, es, f"dHS{i}") for i in range(2)]
    dOS = [Sem(nc, es, f"dOS{i}") for i in range(2)]
    sP = Sem(nc, es, "sP")
    sV = Sem(nc, es, "sV")
    sA = Sem(nc, es, "sA")
    sG = Sem(nc, es, "sG")

    PL = {k: [] for k in ["sync", "scalar", "vector", "tensor", "gpsimd"]}

    def q(key, fn):
        PL[key].append(fn)

    def load(sem, out_ap, in_ap, waits=()):
        waits = tuple(waits)

        def f(e, sem=sem, out_ap=out_ap, in_ap=in_ap, waits=waits):
            _w(e, waits)
            e.dma_start(out=out_ap, in_=in_ap).then_inc(sem.h, 16)
        q("sync", f)
        return sem.bump(16)

    def g_dma(sem, out_ap, in_ap, waits=()):
        waits = tuple(waits)

        def f(e, sem=sem, out_ap=out_ap, in_ap=in_ap, waits=waits):
            _w(e, waits)
            e.dma_start(out=out_ap, in_=in_ap).then_inc(sem.h, 16)
        q("gpsimd", f)
        return sem.bump(16)

    def st_dma(sem, out_ap, in_ap, waits=()):
        waits = tuple(waits)

        def f(e, sem=sem, out_ap=out_ap, in_ap=in_ap, waits=waits):
            _w(e, waits)
            e.dma_start(out=out_ap, in_=in_ap).then_inc(sem.h, 16)
        q("scalar", f)
        return sem.bump(16)

    def _mk(key, sem):
        def op(fn, waits=(), inc=True, chain=False):
            waits = tuple(waits) + (((sem, sem.n),) if chain else ())

            def f(e, fn=fn, waits=waits):
                _w(e, waits)
                inst = fn(e)
                if inc:
                    inst.then_inc(sem.h, 1)
            q(key, f)
            return sem.bump() if inc else sem.n
        return op

    v_op = _mk("vector", sV)
    a_op = _mk("scalar", sA)
    g_op = _mk("gpsimd", sG)
    t_op = _mk("tensor", sP)

    es_psAB = contextlib.ExitStack()
    tp = es_psAB.enter_context(nc.psum_tensor("tp", [128, 4096], BF16)).ap()
    tpv = tp.rearrange("p (b c) -> p b c", b=4)
    tpv8 = tp.rearrange("p (b k c) -> p b k c", b=4, k=8)
    r_ps = [es_psAB.enter_context(nc.psum_tensor(f"r_ps{i}", [128, 32], F32)).ap()
            for i in range(2)]
    wt_ps = [es_psAB.enter_context(nc.psum_tensor(f"wt_ps{i}", [32, 128], BF16)).ap()
             for i in range(2)]

    load(dM, identf[:], identf_d)
    load(dM, onesf[:], onesf_d)
    load(dM, biasrow[:], dbias)
    load(dM, routstage[:], drout)
    for d in "fb":
        load(dM, vt[d]["vecs"][:], dvec[d])
        load(dM, vt[d]["Dt"][:], dD[d])

    m_hpi = v_op(lambda e: e.memset(c_hpi[:], math.pi / 2.0))
    m_eps = v_op(lambda e: e.memset(c_eps[:], 1e-5))
    m_nh = v_op(lambda e: e.memset(c_nh[:], -0.5))
    load(dM, r_lg[0][0:32, 0:32], oh32_d)
    m_misc = dM.n
    m_identb = v_op(lambda e: e.tensor_copy(out=identb[:], in_=identf[:]),
                    waits=[(dM, m_misc)])
    m_oh = v_op(lambda e: e.tensor_copy(out=oh32b[:], in_=r_lg[0][0:32, 0:32]),
                waits=[(dM, m_misc)])
    m_routb = g_op(lambda e: e.tensor_copy(out=routerTb[:], in_=routstage[:]),
                   waits=[(dM, m_misc)])
    m_routlo = g_op(lambda e: e.tensor_tensor(out=routerTb_lo[:], in0=routstage[:],
                                              in1=routerTb[:], op=ALU.subtract),
                    waits=[(sG, m_routb)])
    m_biasmm = t_op(lambda e: e.matmul(r_ps[0][:], onesf[:], biasrow[:],
                                       start=True, stop=True),
                    waits=[(dM, m_misc)])
    m_biasbc = v_op(lambda e: e.tensor_copy(out=bias_bc[:], in_=r_ps[0][:]),
                    waits=[(sP, m_biasmm)])

    INV_D = 1.0 / DM
    ln_v_marks = [0] * NTOK
    ln_tp_marks = [0] * NTOK
    ln_lo_evac = [0] * NTOK
    rmm_marks = [0] * NTOK
    m_wq_arr = [0] * NTOK
    ln_rstd_m = [0, 0]
    tp_evac = [0, 0]
    w_evac = [0, 0]
    m_e_par = [0, 0]
    m_wt_par = [0, 0]
    m_lg_par = [0, 0]
    norm_par = [0, 0]
    def ln_bn_part(i):
        p2 = i % 2
        buf = xrl[p2]
        war = ([(sV, ln_v_marks[i - 2])] if i >= 2 else [])
        m_ld = load(dX[p2], buf[:], dx[i * 128:(i + 1) * 128, :], waits=war)
        v_op(lambda e, b=buf, p2=p2: e.bn_stats(out=ln_bn[p2][:, 0:6], in_=b[:, 0:512]),
             waits=[(dX[p2], m_ld), (sV, norm_par[p2])])
        v_op(lambda e, b=buf, p2=p2: e.bn_stats(out=ln_bn[p2][:, 6:12], in_=b[:, 512:1024]),
             chain=True)
        v_op(lambda e, p2=p2: e.bn_aggr(out=ln_mv[p2][:], in_=ln_bn[p2][:]), chain=True)
        m_var = sV.n
        a_op(lambda e, p2=p2: e.activation(out=ln_rstd[p2][:],
                                           in_=ln_mv[p2][:, 1:2], func=AF.Ln,
                                           bias=c_eps[:]),
             waits=[(sV, m_var), (sV, m_nh)])
        ln_rstd_m[p2] = a_op(lambda e, p2=p2: e.activation(out=ln_rstd[p2][:],
                                                           in_=ln_rstd[p2][:],
                                                           func=AF.Exp, scale=c_nh[:]),
                             chain=True)

    def ln_xnf_part(i):
        p2 = i % 2
        buf = xrl[p2]
        v_op(lambda e, b=buf, p2=p2: e.tensor_scalar(
            out=xnf_tok[:], in0=b[:], scalar1=ln_mv[p2][:, 0:1], scalar2=ln_rstd[p2][:],
            op0=ALU.subtract, op1=ALU.mult),
            waits=[(sA, ln_rstd_m[p2])] + ([(sP, ln_tp_marks[i - 2])] if i >= 2 else []),
            chain=True)
        v_op(lambda e, p2=p2: e.tensor_copy(out=xn_tok[p2][:], in_=xnf_tok[:]), chain=True)
        m_norm = v_op(lambda e, p2=p2: e.tensor_tensor(out=xnlo_tok[p2][:], in0=xnf_tok[:],
                                                       in1=xn_tok[p2][:], op=ALU.subtract),
                      chain=True)
        ln_v_marks[i] = m_norm
        norm_par[p2] = m_norm

    def ln_tp_part(i):
        p2 = i % 2
        m_norm = ln_v_marks[i]
        lo_v = xnTlo[p2].rearrange("p (dc t) -> p dc t", dc=8)
        war_lo = [(sP, rmm_marks[i - 2])] if i >= 2 else []
        for half in range(2):
            src_t = xn_tok[p2] if half == 0 else xnlo_tok[p2]
            def tfn(e, src_t=src_t, half=half):
                last = None
                for k in range(8):
                    last = e.transpose(
                        tp[:, half * 1024 + k * 128: half * 1024 + (k + 1) * 128],
                        src_t[:, k * 128:(k + 1) * 128], identb[:])
                return last
            m_tp = t_op(tfn, waits=[(sV, m_norm), (sV, m_identb), (sA, tp_evac[half])]
                             + (war_lo if half == 1 else []))
            ln_tp_marks[i] = m_tp
            def efn(e, i=i, half=half, lo_v=lo_v):
                if half == 1:
                    return e.copy(out=lo_v[:, 0:8, :], in_=tpv8[:, 1, 0:8, 0:128])
                return e.copy(out=xnTv[:, 0:8, i * 128:(i + 1) * 128],
                              in_=tpv8[:, 0, 0:8, 0:128])
            tp_evac[half] = a_op(efn, waits=[(sP, m_tp)])
        ln_lo_evac[i] = tp_evac[1]


    def router_mm_part(i):
        p2 = i % 2
        def rmm(e, i=i, p2=p2):
            last = None
            lo_t = xnTlo[p2]
            for dc in range(NDC):
                last = e.matmul(r_ps[p2][:],
                                xnT[:, dc * T + i * 128: dc * T + i * 128 + 128],
                                routerTb[:, dc * 32:(dc + 1) * 32],
                                start=(dc == 0), stop=False)
                last = e.matmul(r_ps[p2][:], lo_t[:, dc * 128:(dc + 1) * 128],
                                routerTb[:, dc * 32:(dc + 1) * 32],
                                start=False, stop=False)
                last = e.matmul(r_ps[p2][:],
                                xnT[:, dc * T + i * 128: dc * T + i * 128 + 128],
                                routerTb_lo[:, dc * 32:(dc + 1) * 32],
                                start=False, stop=(dc == NDC - 1))
            return last
        rmm_marks[i] = t_op(rmm, waits=[(sA, ln_lo_evac[i]), (sG, m_routlo),
                                        (sV, m_lg_par[p2])])

    def router_head_part(i):
        p2 = i % 2
        m_lg = v_op(lambda e, p2=p2: e.tensor_tensor(out=r_lg[p2][:], in0=r_ps[p2][:],
                                                     in1=bias_bc[:], op=ALU.add),
                    waits=[(sP, rmm_marks[i]), (sV, m_biasbc), (sA, m_e_par[p2])])
        m_lg_par[p2] = m_lg
        m_mx = v_op(lambda e, p2=p2: e.reduce_max(out=r_m[p2][:], in_=r_lg[p2][:],
                                                  axis=AX.X, negate=True), chain=True)
        m_e = a_op(lambda e, p2=p2: e.activation(out=r_e[p2][:], in_=r_lg[p2][:],
                                                 func=AF.Exp, bias=r_m[p2][:]),
                   waits=[(sV, m_mx)])
        m_e_par[p2] = m_e

    def router_tail_part(i):
        p2 = i % 2
        v_op(lambda e, p2=p2: e.tensor_copy(out=r_cur[p2][:], in_=r_e[p2][:]),
             waits=[(sA, m_e_par[p2])])
        for k in range(4):
            v_op(lambda e, p2=p2: e.reduce_max(out=r_m[p2][:], in_=r_cur[p2][:], axis=AX.X),
                 chain=True)
            if k < 3:
                v_op(lambda e, p2=p2: e.tensor_scalar(out=r_msk[p2][:], in0=r_cur[p2][:],
                                                      scalar1=r_m[p2][:], scalar2=None,
                                                      op0=ALU.is_ge), chain=True)
                v_op(lambda e, p2=p2: e.scalar_tensor_tensor(out=r_cur[p2][:],
                                                             in0=r_msk[p2][:],
                                                             scalar=-1e30, in1=r_cur[p2][:],
                                                             op0=ALU.mult, op1=ALU.add),
                     chain=True)
        v_op(lambda e, p2=p2: e.tensor_scalar(out=r_msk[p2][:], in0=r_e[p2][:],
                                              scalar1=r_m[p2][:], scalar2=None,
                                              op0=ALU.is_ge), chain=True)
        v_op(lambda e, p2=p2: e.tensor_tensor(out=r_cur[p2][:], in0=r_e[p2][:],
                                              in1=r_msk[p2][:], op=ALU.mult), chain=True)
        v_op(lambda e, p2=p2: e.reduce_sum(out=r_s1[p2][:], in_=r_cur[p2][:], axis=AX.X),
             chain=True)
        v_op(lambda e, p2=p2: e.tensor_scalar_max(out=r_s1[p2][:], in0=r_s1[p2][:],
                                                  scalar1=1e-12), chain=True)
        v_op(lambda e, p2=p2: e.reciprocal(out=r_s2[p2][:], in_=r_s1[p2][:]), chain=True)
        m_wq_arr[i] = v_op(lambda e, p2=p2: e.tensor_scalar(out=r_wb[p2][:],
                                                            in0=r_cur[p2][:],
                                                            scalar1=r_s2[p2][:], scalar2=None,
                                                            op0=ALU.mult),
                           waits=[(sP, m_wt_par[p2])], chain=True)

    def wt_part(i):
        p2 = i % 2
        m_wt = t_op(lambda e, p2=p2: e.transpose(wt_ps[p2][:], r_wb[p2][:], identb[:]),
                    waits=[(sV, m_wq_arr[i]), (sA, w_evac[p2])])
        m_wt_par[p2] = m_wt
        w_evac[p2] = a_op(lambda e, i=i, p2=p2: e.copy(out=wTb[:, i * 128:(i + 1) * 128],
                                                       in_=wt_ps[p2][:]),
                          waits=[(sP, m_wt)])

    prep_parts = {}

    def prep_act0(d):
        V = vt[d]
        lr, ldl = V["vecs"][:, 0:2], V["vecs"][:, 4:6]
        wv = [(dM, m_misc)]
        a_op(lambda e: e.activation(out=V["delta"][:], in_=ldl, func=AF.Exp), waits=wv)
        mA2 = a_op(lambda e: e.activation(out=V["elr"][:], in_=lr, func=AF.Exp), waits=wv)
        prep_parts[d] = {"mA2": mA2}

    def prep_v1(d):
        def vc(fn, waits=(), inc=True):
            return v_op(fn, waits=waits, inc=inc, chain=True)
        V = vt[d]
        im = V["vecs"][:, 2:4]
        mA2 = prep_parts[d]["mA2"]
        vc(lambda e: e.tensor_tensor(out=V["ngd"][:], in0=V["elr"][:], in1=V["delta"][:],
                                       op=ALU.mult), waits=[(sA, mA2)])
        mV1 = vc(lambda e: e.tensor_scalar(out=V["ngd"][:], in0=V["ngd"][:], scalar1=-1.0,
                                             scalar2=None, op0=ALU.mult))
        vc(lambda e: e.tensor_tensor(out=V["theta"][:], in0=im, in1=V["delta"][:],
                                       op=ALU.mult))
        MAGIC = 12582912.0
        vc(lambda e: e.tensor_scalar(out=V["kf"][:], in0=V["theta"][:],
                                     scalar1=1.0 / TWO_PI, scalar2=MAGIC,
                                     op0=ALU.mult, op1=ALU.add))
        vc(lambda e: e.tensor_scalar(out=V["kf"][:], in0=V["kf"][:], scalar1=-MAGIC,
                                     scalar2=None, op0=ALU.add))
        vc(lambda e: e.scalar_tensor_tensor(out=V["thr"][:], in0=V["kf"][:], scalar=-TWO_PI,
                                            in1=V["theta"][:], op0=ALU.mult, op1=ALU.add))
        m_half = vc(lambda e: e.tensor_scalar(out=V["half"][:], in0=V["thr"][:], scalar1=0.5,
                                                scalar2=None, op0=ALU.mult))
        prep_parts[d]["mV1"] = mV1
        prep_parts[d]["m_half"] = m_half

    def prep_act1(d):
        V = vt[d]
        m_r = a_op(lambda e: e.activation(out=V["rr"][:], in_=V["ngd"][:], func=AF.Exp),
                   waits=[(sV, prep_parts[d]["mV1"])])
        a_op(lambda e: e.activation(out=V["sh"][:], in_=V["half"][:], func=AF.Sin),
             waits=[(sV, prep_parts[d]["m_half"])])
        m_ch = a_op(lambda e: e.activation(out=V["ch"][:], in_=V["half"][:], func=AF.Sin,
                                           bias=c_hpi[:]), waits=[(sV, m_hpi)])
        prep_parts[d]["m_r"] = m_r
        prep_parts[d]["m_ch"] = m_ch

    def prep_v2(d):
        def vc(fn, waits=(), inc=True):
            return v_op(fn, waits=waits, inc=inc, chain=True)
        V = vt[d]
        im = V["vecs"][:, 2:4]
        m_ch = prep_parts[d]["m_ch"]
        m_r = prep_parts[d]["m_r"]
        vc(lambda e: e.tensor_tensor(out=V["sinth"][:], in0=V["sh"][:], in1=V["ch"][:],
                                       op=ALU.mult), waits=[(sA, m_ch)])
        vc(lambda e: e.tensor_scalar(out=V["sinth"][:], in0=V["sinth"][:], scalar1=2.0,
                                       scalar2=None, op0=ALU.mult))
        vc(lambda e: e.tensor_tensor(out=V["costh"][:], in0=V["sh"][:], in1=V["sh"][:],
                                       op=ALU.mult))
        m_cs = vc(lambda e: e.tensor_scalar(out=V["costh"][:], in0=V["costh"][:],
                                              scalar1=-2.0, scalar2=1.0,
                                              op0=ALU.mult, op1=ALU.add))
        vc(lambda e: e.tensor_tensor(out=V["lbr"][:], in0=V["rr"][:], in1=V["costh"][:],
                                       op=ALU.mult), waits=[(sA, m_r)])
        vc(lambda e: e.tensor_tensor(out=V["lbi"][:], in0=V["rr"][:], in1=V["sinth"][:],
                                       op=ALU.mult))
        vc(lambda e: e.tensor_tensor(out=V["l2m"][:], in0=V["elr"][:], in1=V["elr"][:],
                                       op=ALU.mult))
        vc(lambda e: e.tensor_tensor(out=V["tmp"][:], in0=im, in1=im, op=ALU.mult))
        vc(lambda e: e.tensor_tensor(out=V["l2m"][:], in0=V["l2m"][:], in1=V["tmp"][:],
                                       op=ALU.add))
        vc(lambda e: e.tensor_scalar(out=V["l2m"][:], in0=V["l2m"][:], scalar1=1e-12,
                                       scalar2=None, op0=ALU.is_lt))
        vc(lambda e: e.tensor_scalar(out=V["onem"][:], in0=V["l2m"][:], scalar1=-1.0,
                                       scalar2=1.0, op0=ALU.mult, op1=ALU.add))
        vc(lambda e: e.tensor_scalar(out=V["lsr"][:], in0=V["elr"][:], scalar1=-1.0,
                                       scalar2=None, op0=ALU.mult))
        vc(lambda e: e.tensor_tensor(out=V["lsr"][:], in0=V["lsr"][:], in1=V["onem"][:],
                                       op=ALU.mult))
        vc(lambda e: e.scalar_tensor_tensor(out=V["lsr"][:], in0=V["l2m"][:], scalar=1e-6,
                                              in1=V["lsr"][:], op0=ALU.mult, op1=ALU.add))
        vc(lambda e: e.tensor_tensor(out=V["lsi"][:], in0=im, in1=V["onem"][:],
                                       op=ALU.mult))
        vc(lambda e: e.tensor_tensor(out=V["dinv"][:], in0=V["lsr"][:], in1=V["lsr"][:],
                                       op=ALU.mult))
        vc(lambda e: e.tensor_tensor(out=V["tmp"][:], in0=V["lsi"][:], in1=V["lsi"][:],
                                       op=ALU.mult))
        vc(lambda e: e.tensor_tensor(out=V["dinv"][:], in0=V["dinv"][:], in1=V["tmp"][:],
                                       op=ALU.add))
        vc(lambda e: e.reciprocal(out=V["dinv"][:], in_=V["dinv"][:]))
        vc(lambda e: e.tensor_scalar(out=V["nr"][:], in0=V["lbr"][:], scalar1=-1.0,
                                       scalar2=None, op0=ALU.add))
        vc(lambda e: e.tensor_tensor(out=V["scr"][:], in0=V["nr"][:], in1=V["lsr"][:],
                                       op=ALU.mult))
        vc(lambda e: e.tensor_tensor(out=V["tmp"][:], in0=V["lbi"][:], in1=V["lsi"][:],
                                       op=ALU.mult))
        vc(lambda e: e.tensor_tensor(out=V["scr"][:], in0=V["scr"][:], in1=V["tmp"][:],
                                       op=ALU.add))
        vc(lambda e: e.tensor_tensor(out=V["scr"][:], in0=V["scr"][:], in1=V["dinv"][:],
                                       op=ALU.mult))
        vc(lambda e: e.tensor_tensor(out=V["sci"][:], in0=V["lbi"][:], in1=V["lsr"][:],
                                       op=ALU.mult))
        vc(lambda e: e.tensor_tensor(out=V["tmp"][:], in0=V["nr"][:], in1=V["lsi"][:],
                                       op=ALU.mult))
        vc(lambda e: e.tensor_tensor(out=V["sci"][:], in0=V["sci"][:], in1=V["tmp"][:],
                                       op=ALU.subtract))
        m_sc = vc(lambda e: e.tensor_tensor(out=V["sci"][:], in0=V["sci"][:],
                                              in1=V["dinv"][:], op=ALU.mult))
        prep_parts[d]["m_sc"] = m_sc
        prep_parts[d]["m_cs"] = m_cs

    def prep_v2b(d, half):
        def vc(fn, waits=(), inc=True):
            return v_op(fn, waits=waits, inc=inc, chain=True)
        V = vt[d]
        if half == 0:
            m_cs = prep_parts[d]["m_cs"]
            vc(lambda e: e.tensor_copy(out=V["AB"][:, 0:2], in_=V["costh"][:]),
                 waits=[(sV, m_cs)])
            vc(lambda e: e.tensor_copy(out=V["AB"][:, 2:4], in_=V["sinth"][:]))
        for k in (range(5) if half == 0 else range(5, 10)):
            a0, b0 = V["AB"][:, 4 * k:4 * k + 2], V["AB"][:, 4 * k + 2:4 * k + 4]
            a1 = V["AB"][:, 4 * (k + 1):4 * (k + 1) + 2]
            b1 = V["AB"][:, 4 * (k + 1) + 2:4 * (k + 1) + 4]
            vc(lambda e, a0=a0, a1=a1: e.tensor_tensor(out=a1, in0=a0, in1=a0, op=ALU.mult))
            vc(lambda e, b0=b0, b1=b1: e.tensor_tensor(out=b1, in0=b0, in1=b0, op=ALU.mult))
            vc(lambda e, a1=a1, b1=b1: e.tensor_tensor(out=a1, in0=a1, in1=b1,
                                                         op=ALU.subtract))
            vc(lambda e, a0=a0, b0=b0, b1=b1: e.tensor_tensor(out=b1, in0=a0, in1=b0,
                                                                op=ALU.mult))
            vc(lambda e, b1=b1: e.tensor_scalar(out=b1, in0=b1, scalar1=2.0, scalar2=None,
                                                  op0=ALU.mult))
        if half == 0:
            return
        m_ab = sV.n
        for jc in range(NDC):
            vc(lambda e, d=d, jc=jc: e.tensor_scalar(
                out=diagD[d][:, jc * 128:(jc + 1) * 128], in0=identf[:],
                scalar1=vt[d]["Dt"][:, jc:jc + 1], scalar2=None, op0=ALU.mult),
                waits=[(dM, m_misc)])
        prep_marks[d] = (prep_parts[d]["m_sc"], m_ab, prep_parts[d]["m_r"])

    prep_marks = {}

    m_ct_cast = {}
    ct_quarters = [(d, ri_i, hf) for d in "fb" for ri_i in range(2) for hf in range(2)]
    ct_load_marks = {}
    ct_cast_marks = {}
    stageb = [stage, stage2]
    dSTp = [dST, dST2]

    def ct_loads(ks):
        for k in ks:
            d, ri_i, hf = ct_quarters[k]
            war = [(sV, ct_cast_marks[k - 2])] if k - 2 in ct_cast_marks else []
            ct_load_marks[k] = g_dma(
                dSTp[k % 2], stageb[k % 2][:],
                dCT[d].rearrange("p (r h x) -> p r h x", r=2, h=2)[:, ri_i, hf, :],
                waits=war)

    def ct_casts(ks):
        for k in ks:
            d, ri_i, hf = ct_quarters[k]
            ri = "ri"[ri_i]
            sc = -1.0 if ri == "i" else 1.0
            ct_cast_marks[k] = v_op(lambda e, d=d, ri=ri, sc=sc, hf=hf, k=k: e.tensor_scalar(
                out=CTb[(d, ri)][:, hf * 1024:(hf + 1) * 1024], in0=stageb[k % 2][:],
                scalar1=sc, scalar2=None, op0=ALU.mult),
                waits=[(dSTp[k % 2], ct_load_marks[k])])
            m_ct_cast[(d, ri)] = ct_cast_marks[k]

    tr_evac = sA.n
    bb_state = {"cmb_v": 0, "cmb_g": 0, "tr": sA.n}

    def bbar_products_f():
        V = vt["f"]
        m_sc = prep_marks["f"][0]
        m_lbr = load(dBU[0], cmb["r"][:], dB["f"].rearrange("p (r x) -> p r x", r=2)[:, 0, :])
        m_lbi = load(dBU[1], cmb["i"][:], dB["f"].rearrange("p (r x) -> p r x", r=2)[:, 1, :])
        for ri in "ri":
            dst = Bbar1 if ri == "r" else Bbar2
            scr = bq["r"] if ri == "r" else uu["r"]
            for cc in range(2):
                sl = slice(cc * 1024, (cc + 1) * 1024)
                scr_c = V["scr"][:, cc:cc + 1]
                sci_c = V["sci"][:, cc:cc + 1]
                oth = "i" if ri == "r" else "r"
                sgn = ALU.subtract if ri == "r" else ALU.add
                v_op(lambda e, sl=sl, sci_c=sci_c, oth=oth, scr=scr: e.tensor_scalar(
                    out=scr[:, sl], in0=cmb[oth][:, sl], scalar1=sci_c, scalar2=None,
                    op0=ALU.mult),
                    waits=[(dBU[1], m_lbi), (dBU[0], m_lbr), (sV, m_sc)])
                v_op(lambda e, sl=sl, scr_c=scr_c, ri=ri, sgn=sgn, dst=dst, scr=scr:
                     e.scalar_tensor_tensor(
                    out=dst[:, sl], in0=cmb[ri][:, sl], scalar=scr_c, in1=scr[:, sl],
                    op0=ALU.mult, op1=sgn), chain=True)
        bb_state["m_bbar_f"] = sV.n
        bb_state["cmb_v"] = sV.n

    def bbar_transposes_f():
        m_bbar = bb_state["m_bbar_f"]
        tr = bb_state["tr"]
        for ri in "ri":
            src = Bbar1 if ri == "r" else Bbar2
            for cc in range(2):
                for half in range(2):
                    def tfn(e, cc=cc, half=half, src=src):
                        last = None
                        for k in range(4):
                            dc = half * 4 + k
                            last = e.transpose(
                                tp[:, k * 1024:k * 1024 + 128],
                                src[:, cc * 1024 + dc * 128:
                                    cc * 1024 + dc * 128 + 128],
                                identb[:])
                        return last
                    m_tp = t_op(tfn, waits=[(sV, m_bbar), (sA, tr),
                                            (sA, tp_evac[0]), (sA, tp_evac[1])])
                    def efn(e, ri=ri, cc=cc, half=half):
                        return e.copy(out=BTbv[ri][:, half * 4:(half + 1) * 4, cc, :],
                                      in_=tpv[:, 0:4, 0:128])
                    tr = a_op(efn, waits=[(sP, m_tp)])
                    tp_evac[0] = tr
                    tp_evac[1] = tr
        bb_state["tr"] = tr
        return tr

    def plan_bbar(d, use_tp=True):
        V = vt[d]
        m_sc, m_ab, m_r = prep_marks[d]
        m_lbr = load(dBU[0], cmb["r"][:], dB[d].rearrange("p (r x) -> p r x", r=2)[:, 0, :],
                     waits=[(sV, bb_state["cmb_v"]), (sG, bb_state["cmb_g"])])
        m_lbi = load(dBU[1], cmb["i"][:], dB[d].rearrange("p (r x) -> p r x", r=2)[:, 1, :],
                     waits=[(sV, bb_state["cmb_v"]), (sG, bb_state["cmb_g"])])
        m_tp_last = 0
        tr = bb_state["tr"]
        for ri in "ri":
            for cc in range(2):
                sl = slice(cc * 1024, (cc + 1) * 1024)
                scr_c = V["scr"][:, cc:cc + 1]
                sci_c = V["sci"][:, cc:cc + 1]
                oth = "i" if ri == "r" else "r"
                sgn = ALU.subtract if ri == "r" else ALU.add
                v_op(lambda e, sl=sl, sci_c=sci_c, oth=oth: e.tensor_scalar(
                    out=bq["r"][:, sl], in0=cmb[oth][:, sl], scalar1=sci_c, scalar2=None,
                    op0=ALU.mult),
                    waits=[(dBU[1], m_lbi), (dBU[0], m_lbr), (sV, m_sc),
                           (sP, m_tp_last)])
                v_op(lambda e, sl=sl, scr_c=scr_c, ri=ri, sgn=sgn: e.scalar_tensor_tensor(
                    out=Bbar1[:, sl], in0=cmb[ri][:, sl], scalar=scr_c, in1=bq["r"][:, sl],
                    op0=ALU.mult, op1=sgn), chain=True)
            m_bbar = sV.n
            if use_tp:
                for cc in range(2):
                    for half in range(2):
                        def tfn(e, cc=cc, half=half):
                            last = None
                            for k in range(4):
                                dc = half * 4 + k
                                last = e.transpose(
                                    tp[:, k * 1024:k * 1024 + 128],
                                    Bbar1[:, cc * 1024 + dc * 128:
                                          cc * 1024 + dc * 128 + 128],
                                    identb[:])
                            return last
                        m_tp = t_op(tfn, waits=[(sV, m_bbar), (sA, tr),
                                                (sA, tp_evac[0]), (sA, tp_evac[1])])
                        m_tp_last = m_tp
                        def efn(e, ri=ri, cc=cc, half=half):
                            return e.copy(out=BTbv[ri][:, half * 4:(half + 1) * 4, cc, :],
                                          in_=tpv[:, 0:4, 0:128])
                        tr = a_op(efn, waits=[(sP, m_tp)])
                        tp_evac[0] = tr
                        tp_evac[1] = tr
            else:
                bviews = [bu_ps[i].bitcast(BF16) for i in range(2)]
                for cc in range(2):
                    for half in range(4):
                        def tfn(e, cc=cc, half=half):
                            last = None
                            for k in range(2):
                                dc = half * 2 + k
                                last = e.transpose(
                                    bviews[k][:, 0:128],
                                    Bbar1[:, cc * 1024 + dc * 128:
                                          cc * 1024 + dc * 128 + 128],
                                    identb[:])
                            return last
                        m_tp = t_op(tfn, waits=[(sV, m_bbar), (sA, tr),
                                                (sV, prod_done[0]),
                                                (sV, prod_done[1])])
                        m_tp_last = m_tp
                        def efn(e, ri=ri, cc=cc, half=half):
                            e.copy(out=BTbv[ri][:, 2 * half, cc, :],
                                   in_=bviews[0][:, 0:128])
                            return e.copy(out=BTbv[ri][:, 2 * half + 1, cc, :],
                                          in_=bviews[1][:, 0:128])
                        tr = a_op(efn, waits=[(sP, m_tp)])
        bb_state["cmb_v"] = sV.n
        bb_state["tr"] = tr
        return tr


    prep_act0("f")
    prep_act0("b")
    ln_bn_part(0)
    ln_xnf_part(0)
    ln_bn_part(1)
    ln_xnf_part(1)
    ln_tp_part(0)
    m_btb = None
    m_A_v = None
    for i in range(NTOK):
        if i + 2 < NTOK:
            ln_bn_part(i + 2)
        router_mm_part(i)
        router_head_part(i)
        if i + 1 < NTOK:
            ln_tp_part(i + 1)
        if i >= 1:
            wt_part(i - 1)
        router_tail_part(i)
        if i + 2 < NTOK:
            ln_xnf_part(i + 2)
        if i == 0:
            prep_v1("f")
        elif i == 1:
            prep_act1("f")
            prep_v1("b")
        elif i == 2:
            prep_v2("f")
            prep_act1("b")
        elif i == 3:
            prep_v2b("f", 0)
        elif i == 4:
            prep_v2b("f", 1)
            prep_v2("b")
            ct_loads([0, 1])
        elif i == 5:
            prep_v2b("b", 0)
            ct_casts([0, 1])
            ct_loads([2, 3])
        elif i == 6:
            prep_v2b("b", 1)
            ct_casts([2, 3])
            ct_loads([4, 5])
        elif i == 7:
            ct_casts([4, 5])
            ct_loads([6, 7])
        elif i == 8:
            ct_casts([6, 7])
            bbar_products_f()
        elif i == 10:
            m_btb = bbar_transposes_f()
    wt_part(NTOK - 1)
    m_A_v = sV.n
    m_xnT_a = sA.n
    m_w_done = max(w_evac)




    es_aonly.close()
    m_aonly = [(sP, sP.n), (sV, sV.n), (sA, sA.n), (sG, sG.n)]



    es_psAB.close()
    es_psBH = contextlib.ExitStack()
    bu_ps = [es_psBH.enter_context(nc.psum_tensor(f"bu{i}", [128, 512], F32)).ap()
             for i in range(3)]
    g_ps = [es_psBH.enter_context(nc.psum_tensor(f"g_ps{i}", [128, 512], F32)).ap()
            for i in range(2)]
    u_ps = [es_psBH.enter_context(nc.psum_tensor(f"u_ps{i}", [128, 512], F32)).ap()
            for i in range(2)]
    w_ps = [es_psBH.enter_context(nc.psum_tensor("w_ps0", [128, 512], F32)).ap()]

    es_c1 = contextlib.ExitStack()
    gupstage = [sbr(es_c1, f"gupstage{i}", [128, 1024], F32) for i in range(2)]
    gupb = [sbr(es_c1, f"gupb{i}", [128, 2048], BF16) for i in range(2)]
    hstage = [sbr(es_c1, f"hstage{i}", [128, 512], BF16) for i in range(2)]
    t1s = [sbr(es_c1, f"t1s{i}", [128, 512], F32) for i in range(2)]

    gup_cast = [0] * NEXP
    gup_pe_done = [0] * NEXP
    h_silu = {}
    h_t2 = {}
    h_h = {}
    h_w = {}
    h_store = {}
    hstage_war = [0, 0]
    gup_halfcast = []

    def plan_gup_load(e_i):
        for hf in range(2):
            st_i = (2 * e_i + hf) % 2
            idx = 2 * e_i + hf
            war = [(sG, gup_halfcast[idx - 2])] if idx >= 2 else list(m_aonly)
            m_l = load(dGU[st_i], gupstage[st_i][:],
                       dgup[e_i].rearrange("p (h x) -> p h x", h=2)[:, hf, :], waits=war)
            wv = [(dGU[st_i], m_l)]
            if e_i >= 2:
                wv.append((sP, gup_pe_done[e_i - 2]))
            m_c = g_op(lambda e, st_i=st_i, e_i=e_i, hf=hf: e.tensor_copy(
                out=gupb[e_i % 2][:, hf * 1024:(hf + 1) * 1024], in_=gupstage[st_i][:]),
                waits=wv)
            gup_halfcast.append(m_c)
        gup_cast[e_i] = gup_halfcast[-1]

    plan_gup_load(0)
    plan_gup_load(1)
    h_state = {"n": 0}

    def emit_h_units(count):
        for _ in range(count):
            idx = h_state["n"]
            if idx >= NEXP * 4:
                return
            h_state["n"] += 1
            e_i, th = idx // 4, idx % 4
            is_moe = e_i < 32
            k2 = th % 2
            pg, pu, pw = g_ps[k2], u_ps[k2], w_ps[0]
            war_g = [(sA, h_silu.get(idx - 2, 0))]
            war_u = [(sV, h_t2.get(idx - 2, 0))]
            war_w = [(sV, h_h.get(idx - 1, 0))]
            if idx == 0:
                war_g += [(sA, m_btb), (sA, m_w_done), (sV, m_A_v)]
            def gmm(e, e_i=e_i, th=th, pg=pg):
                last = None
                for kc in range(NDC):
                    last = e.matmul(pg[:], gupb[e_i % 2][:, kc * 128:(kc + 1) * 128],
                                    xnT[:, kc * T + th * 512: kc * T + th * 512 + 512],
                                    start=(kc == 0), stop=(kc == NDC - 1))
                return last
            t_op(gmm, waits=[(sG, gup_cast[e_i]), (sA, m_xnT_a)] + war_g, inc=False)
            def umm(e, e_i=e_i, th=th, pu=pu):
                last = None
                for kc in range(NDC):
                    last = e.matmul(pu[:],
                                    gupb[e_i % 2][:, 1024 + kc * 128: 1024 + (kc + 1) * 128],
                                    xnT[:, kc * T + th * 512: kc * T + th * 512 + 512],
                                    start=(kc == 0), stop=(kc == NDC - 1))
                return last
            m_u = t_op(umm, waits=war_u)
            if is_moe:
                m_wb = t_op(lambda e, e_i=e_i, th=th, pw=pw: e.matmul(
                    pw[:], oh32b[:, e_i:e_i + 1].broadcast_to([32, 128]),
                    wTb[:, th * 512:(th + 1) * 512], start=True, stop=True),
                    waits=[(sA, m_w_done), (sV, m_oh)] + war_w)
            else:
                m_wb = 0
            m_si = a_op(lambda e, pg=pg, k=idx % 2: e.activation(
                out=t1s[k][:], in_=pg[:], func=AF.Silu),
                waits=[(sP, m_u), (sV, h_t2.get(idx - 2, 0))])
            h_silu[idx] = m_si
            m_t2 = v_op(lambda e, pu=pu, k=idx % 2: e.tensor_tensor(
                out=t1s[k][:], in0=t1s[k][:], in1=pu[:], op=ALU.mult),
                waits=[(sA, m_si)])
            h_t2[idx] = m_t2
            if is_moe:
                m_h = v_op(lambda e, pw=pw, k=idx % 2: e.tensor_tensor(
                    out=hstage[k][:], in0=t1s[k][:], in1=pw[:], op=ALU.mult),
                    waits=[(sP, m_wb), (dHS[idx % 2], hstage_war[idx % 2])])
            else:
                m_h = v_op(lambda e, k=idx % 2: e.tensor_copy(
                    out=hstage[k][:], in_=t1s[k][:]),
                    waits=[(dHS[idx % 2], hstage_war[idx % 2])])
            h_h[idx] = m_h
            h_w[idx] = m_h
            m_st = st_dma(dHS[idx % 2], hscr[e_i, th], hstage[idx % 2][:],
                          waits=[(sV, m_h)])
            h_store[idx] = m_st
            hstage_war[idx % 2] = m_st
            if th == 3:
                gup_pe_done[e_i] = sP.n
                if e_i + 2 < NEXP:
                    plan_gup_load(e_i + 2)

    m_xs_all = 0
    last_scan_read_v = 0
    last_pool_rw = 0
    prod_done = [0, 0, 0]
    first_bu = True
    cur_btb = m_btb
    for d in "fb":
        V = vt[d]
        m_sc, m_ab, m_r = prep_marks[d]
        rev = (d == "b")
        if d == "b":
            bb_state["cmb_v"] = last_scan_read_v
            bb_state["cmb_g"] = last_pool_rw
            cur_btb = plan_bbar("b", use_tp=False)
        for cc in range(2):
            v_op(lambda e: e.memset(cosT[:, 0:1], 1.0),
                 waits=[(sV, last_scan_read_v), (sG, last_pool_rw)])
            v_op(lambda e: e.memset(sinT[:, 0:1], 0.0))
            for k in range(11):
                m = 1 << k
                mm = min(m, T - m)
                ch_ = (m <= 512)
                am = V["AB"][:, 4 * k + cc:4 * k + cc + 1]
                bm = V["AB"][:, 4 * k + 2 + cc:4 * k + 2 + cc + 1]
                v_op(lambda e, mm=mm, bm=bm: e.tensor_scalar(
                    out=bq["i"][:, 0:mm], in0=sinT[:, 0:mm], scalar1=bm, scalar2=None,
                    op0=ALU.mult), chain=ch_,
                    waits=[(sV, m_ab)] if k == 0 else ())
                v_op(lambda e, m=m, mm=mm, am=am: e.scalar_tensor_tensor(
                    out=cosT[:, m:m + mm], in0=cosT[:, 0:mm], scalar=am, in1=bq["i"][:, 0:mm],
                    op0=ALU.mult, op1=ALU.subtract), chain=ch_)
                v_op(lambda e, mm=mm, bm=bm: e.tensor_scalar(
                    out=bq["i"][:, 0:mm], in0=cosT[:, 0:mm], scalar1=bm, scalar2=None,
                    op0=ALU.mult), chain=ch_)
                v_op(lambda e, m=m, mm=mm, am=am: e.scalar_tensor_tensor(
                    out=sinT[:, m:m + mm], in0=sinT[:, 0:mm], scalar=am, in1=bq["i"][:, 0:mm],
                    op0=ALU.mult, op1=ALU.add), chain=ch_)
            m_ph = sV.n

            for tt in range(4):
                bk_r = (2 * tt) % 3
                bk_i = (2 * tt + 1) % 3
                wv_r = [(sV, prod_done[bk_r])]
                wv_i = [(sV, prod_done[bk_i])]
                if first_bu:
                    wv_r += [(sA, m_w_done), (sV, m_A_v), (sA, m_xnT_a)]
                    first_bu = False
                wv_r.append((sA, cur_btb))
                wv_i.append((sA, cur_btb))
                def bfnr(e, d=d, cc=cc, tt=tt, pp=bu_ps[bk_r]):
                    last = None
                    for dc in range(NDC):
                        last = e.matmul(pp[:], BTbv["r"][:, dc, cc, :],
                                        xnT[:, dc * T + tt * 512: dc * T + tt * 512 + 512],
                                        start=(dc == 0), stop=(dc == NDC - 1))
                    return last
                m_mr = t_op(bfnr, waits=wv_r)
                def bfni(e, d=d, cc=cc, tt=tt, pp=bu_ps[bk_i]):
                    last = None
                    for dc in range(NDC):
                        last = e.matmul(pp[:], BTbv["i"][:, dc, cc, :],
                                        xnT[:, dc * T + tt * 512: dc * T + tt * 512 + 512],
                                        start=(dc == 0), stop=(dc == NDC - 1))
                    return last
                m_mi = t_op(bfni, waits=wv_i)
                ts_ = slice(tt * 512, (tt + 1) * 512)
                wv_p = [(sP, m_mi), (sV, m_ph), (sG, last_pool_rw)]
                v_op(lambda e, ts_=ts_, pp=bu_ps[bk_r]: e.tensor_tensor(
                    out=bq["r"][:, ts_], in0=cosT[:, ts_], in1=pp[:], op=ALU.mult),
                    waits=wv_p)
                v_op(lambda e, ts_=ts_, pp=bu_ps[bk_i]: e.tensor_tensor(
                    out=uu["r"][:, ts_], in0=sinT[:, ts_], in1=pp[:], op=ALU.mult))
                v_op(lambda e, ts_=ts_, pp=bu_ps[bk_i]: e.tensor_tensor(
                    out=bq["i"][:, ts_], in0=cosT[:, ts_], in1=pp[:], op=ALU.mult))
                m_p4 = v_op(lambda e, ts_=ts_, pp=bu_ps[bk_r]: e.tensor_tensor(
                    out=uu["i"][:, ts_], in0=sinT[:, ts_], in1=pp[:], op=ALU.mult))
                prod_done[bk_r] = m_p4
                prod_done[bk_i] = m_p4
                emit_h_units(1)
            A_ = (lambda ap: ap[:, ::-1]) if rev else (lambda ap: ap)
            sgn_r = ALU.subtract if rev else ALU.add
            sgn_i = ALU.add if rev else ALU.subtract
            m_c1 = v_op(lambda e, A_=A_, sgn_r=sgn_r: e.tensor_tensor(
                out=cmb["r"][:], in0=A_(bq["r"][:]), in1=A_(uu["r"][:]), op=sgn_r),
                waits=[(sV, m_p4)])
            m_c2 = g_op(lambda e, A_=A_, sgn_i=sgn_i: e.tensor_tensor(
                out=cmb["i"][:], in0=A_(bq["i"][:]), in1=A_(uu["i"][:]), op=sgn_i),
                waits=[(sV, m_p4)])
            rbc = V["rr"][:, cc:cc + 1].broadcast_to([128, T])
            m_s1 = v_op(lambda e, rbc=rbc: e.tensor_tensor_scan(
                out=bq["r"][:], data0=rbc, data1=cmb["r"][:], initial=0.0,
                op0=ALU.mult, op1=ALU.add), waits=[(sV, m_c1), (sA, m_r)])
            m_s2 = v_op(lambda e, rbc=rbc: e.tensor_tensor_scan(
                out=bq["i"][:], data0=rbc, data1=cmb["i"][:], initial=0.0,
                op0=ALU.mult, op1=ALU.add), waits=[(sG, m_c2)])
            emit_h_units(2)
            B_ = (lambda ap: ap[:, ::-1]) if rev else (lambda ap: ap)
            usgn_r = ALU.add if rev else ALU.subtract
            usgn_i = ALU.subtract if rev else ALU.add
            wv2 = [(sV, m_s1), (sV, m_s2)]
            mA = v_op(lambda e, B_=B_: e.tensor_tensor(out=uu["r"][:], in0=cosT[:],
                                                       in1=B_(bq["r"][:]), op=ALU.mult),
                      waits=wv2)
            mB = g_op(lambda e, B_=B_: e.tensor_tensor(out=uu["i"][:], in0=sinT[:],
                                                       in1=B_(bq["i"][:]), op=ALU.mult),
                      waits=wv2)
            mC = v_op(lambda e, d=d, cc=cc, usgn_r=usgn_r: e.tensor_tensor(
                out=xsT[(d, "r")][:, cc * T:(cc + 1) * T], in0=uu["r"][:], in1=uu["i"][:],
                op=usgn_r), waits=[(sG, mB)])
            mD = g_op(lambda e, B_=B_: e.tensor_tensor(out=uu["r"][:], in0=cosT[:],
                                                       in1=B_(bq["i"][:]), op=ALU.mult),
                      waits=wv2 + [(sV, mC)])
            mE = v_op(lambda e, B_=B_: e.tensor_tensor(out=uu["i"][:], in0=sinT[:],
                                                       in1=B_(bq["r"][:]), op=ALU.mult),
                      waits=[(sV, mC)])
            mF = v_op(lambda e, d=d, cc=cc, usgn_i=usgn_i: e.tensor_tensor(
                out=xsT[(d, "i")][:, cc * T:(cc + 1) * T], in0=uu["r"][:], in1=uu["i"][:],
                op=usgn_i), waits=[(sG, mD), (sV, mE)])
            last_scan_read_v = mF
            last_pool_rw = mD
            m_xs_all = mF
            emit_h_units(3)

    emit_h_units(NEXP * 4)
    m_hprod_a = sA.n
    m_hprod_v = sV.n

    es_psBH.close()
    es_ab.close()
    es_c = contextlib.ExitStack()
    downstage = [sbt(es_c, f"downstage{i}", [128, 2048], F32) for i in range(3)]
    downb = [sbt(es_c, f"downb{i}", [128, 2048], BF16) for i in range(3)]
    hld = [sbt(es_c, f"hld{i}", [128, 2048], BF16) for i in range(2)]
    gt = [sbt(es_c, f"gt{i}", [128, 512], F32) for i in range(8)]
    stl = [sbt(es_c, f"stl{i}", [128, 512], F32) for i in range(8)]
    sadd = [sbt(es_c, f"sadd{i}", [128, 512], F32) for i in range(2)]
    es_psD = contextlib.ExitStack()
    d_ps = [es_psD.enter_context(nc.psum_tensor(f"d_ps{i}", [128, 512], F32)).ap()
            for i in range(8)]

    down_cast = [0, 0, 0]
    down_pe = [0, 0, 0]
    hld_pe = [0, 0]
    prev_ys_evac = 0
    prev_comb = 0
    xrl_war_v = [ln_v_marks[-2], ln_v_marks[-1]]
    xrl_war_d = [0, 0]
    cur_xrl_ld = 0
    cur_xrl_sem = dX[0]
    sadd_war = [0, 0]
    dd_r = ddown.rearrange("e p j -> p e j")
    hs_r = hscr.rearrange("e t p c -> t p e c")
    dcast_all = {}
    hld_all = {}
    prev_geb_last = 0

    def plan_down_unit(s, u):
        k = u % 3
        m_dl = load(dDW[k], downstage[k].rearrange("p (e j) -> p e j", e=2),
                    dd_r[:, 2 * u:2 * u + 2, :],
                    waits=[(sV, down_cast[k])])
        m_dc = v_op(lambda e, k=k: e.tensor_copy(out=downb[k][:], in_=downstage[k][:]),
                    waits=[(dDW[k], m_dl), (sP, down_pe[k])])
        down_cast[k] = m_dc
        dcast_all[(s, u)] = m_dc

    def plan_hld_unit(s, v):
        kv = v % 2
        e0 = 4 * v
        wv_sp = [(dHS[((e0 + j) * 4 + s) % 2], h_store[(e0 + j) * 4 + s])
                 for j in range(4)]
        hld_all[(s, v)] = load(dHL[kv], hld[kv].rearrange("p (e c) -> p e c", e=4),
                               hs_r[s][:, e0:e0 + 4, :],
                               waits=wv_sp + [(sP, hld_pe[kv])])

    for s in range(4):
        m_last_mm = 0
        if s == 0:
            plan_down_unit(0, 0)
            plan_down_unit(0, 1)
            plan_down_unit(0, 2)
            plan_hld_unit(0, 0)
            plan_hld_unit(0, 1)
        for e_i in range(NEXP):
            u = e_i // 2
            k = u % 3
            v = e_i // 4
            kv = v % 2
            wv = [(sV, dcast_all[(s, u)]), (dHL[kv], hld_all[(s, v)])]
            if e_i == 0:
                wv += [(sA, prev_geb_last)]
                if s == 0:
                    wv += [(sA, m_hprod_a), (sV, m_hprod_v)]
            def dmm(e, e_i=e_i, k=k, kv=kv):
                last = None
                eo = (e_i % 2) * 1024
                ho = (e_i % 4) * 512
                for tc in range(4):
                    for jg in range(2):
                        last = e.matmul(d_ps[tc * 2 + jg][:],
                                        hld[kv][:, ho + tc * 128: ho + (tc + 1) * 128],
                                        downb[k][:, eo + jg * 512: eo + (jg + 1) * 512],
                                        start=(e_i == 0), stop=(e_i == NEXP - 1))
                return last
            m_mm = t_op(dmm, waits=wv)
            if e_i % 2 == 1:
                down_pe[k] = m_mm
                if u + 3 < 18:
                    plan_down_unit(s, u + 3)
            if e_i % 4 == 3:
                hld_pe[kv] = m_mm
                if v + 2 < 9:
                    plan_hld_unit(s, v + 2)
            m_last_mm = m_mm
        if s + 1 < 4:
            plan_down_unit(s + 1, 0)
            plan_down_unit(s + 1, 1)
            plan_down_unit(s + 1, 2)
            plan_hld_unit(s + 1, 0)
            plan_hld_unit(s + 1, 1)
        silu_marks = []
        for gi in range(8):
            m_si = a_op(lambda e, gi=gi: e.activation(out=gt[gi][:], in_=d_ps[gi][:],
                                                      func=AF.Silu),
                        waits=[(sP, m_last_mm), (sV, prev_comb)])
            silu_marks.append(m_si)
        ge_f = [0] * 8
        ysf_m = [0] * 8
        def ymm_mk(dname, tok, jg, pf):
            def ymm(e, d=dname, tok=tok, jg=jg, pf=pf):
                last = None
                first = True
                for cc in range(2):
                    for ri in "ri":
                        last = e.matmul(
                            pf[:],
                            xsT[(d, ri)][:, cc * T + tok: cc * T + tok + 128],
                            CTb[(d, ri)][:, cc * 1024 + jg * 512:
                                         cc * 1024 + jg * 512 + 512],
                            start=first, stop=False)
                        first = False
                for k4 in range(4):
                    dc = jg * 4 + k4
                    last = e.matmul(pf[:, k4 * 128:(k4 + 1) * 128],
                                    xnT[:, dc * T + tok: dc * T + tok + 128],
                                    diagD[d][:, dc * 128:(dc + 1) * 128],
                                    start=False, stop=(k4 == 3))
                return last
            return ymm
        for gi in range(8):
            tc, jg = gi // 2, gi % 2
            tok = s * 512 + tc * 128
            pf = d_ps[gi]
            wv = [(sV, m_xs_all), (sV, m_ct_cast[("f", "i")]),
                  (sV, m_ct_cast[("f", "r")]), (sA, silu_marks[gi])]
            ysf_m[gi] = t_op(ymm_mk("f", tok, jg, pf), waits=wv)
            ge_f[gi] = a_op(lambda e, gi=gi, pf=pf: e.activation(
                out=stl[gi][:], in_=pf[:], func=AF.Gelu),
                waits=[(sP, ysf_m[gi]), (sV, prev_comb)])
        ysb_info = []
        for gi in range(8):
            tc, jg = gi // 2, gi % 2
            tok = s * 512 + tc * 128
            tcg = s * 4 + tc
            kx = tcg % 2
            pf = d_ps[gi]
            if jg == 0 and tc < 2:
                cur_xrl_ld = g_dma(dX[kx], xrl[kx][:], dx[tcg * 128:(tcg + 1) * 128, :],
                                   waits=[(sV, xrl_war_v[kx]), (dOS[kx], xrl_war_d[kx])])
                cur_xrl_sem = dX[kx]
            elif jg == 0:
                cur_xrl_ld = None
                cur_xrl_sem = dX[kx]
            wv = [(sV, m_ct_cast[("b", "i")]), (sV, m_ct_cast[("b", "r")]),
                  (sA, ge_f[gi])]
            m_ysb = t_op(ymm_mk("b", tok, jg, pf), waits=wv)
            m_geb = a_op(lambda e, gi=gi, pf=pf: e.activation(
                out=sadd[gi % 2][:], in_=pf[:], func=AF.Gelu),
                waits=[(sP, m_ysb), (sV, sadd_war[gi % 2])])
            prev_geb_last = m_geb
            m_sadd = v_op(lambda e, gi=gi: e.tensor_tensor(
                out=stl[gi][:], in0=stl[gi][:], in1=sadd[gi % 2][:], op=ALU.add),
                waits=[(sA, m_geb)])
            sadd_war[gi % 2] = m_sadd
            ysb_info.append((gi, jg, kx, tcg, cur_xrl_ld, cur_xrl_sem))
        late_lds = {}
        for gi, jg, kx, tcg, xl, xl_sem in ysb_info:
            if xl is None and kx not in late_lds:
                late_lds[kx] = g_dma(dX[kx], xrl[kx][:],
                                     dx[tcg * 128:(tcg + 1) * 128, :],
                                     waits=[(sV, xrl_war_v[kx]),
                                            (dOS[kx], xrl_war_d[kx])])
            if xl is None:
                xl = late_lds[kx]
            m_og = v_op(lambda e, gi=gi: e.tensor_tensor(
                out=gt[gi][:], in0=stl[gi][:], in1=gt[gi][:], op=ALU.mult))
            m_out = v_op(lambda e, gi=gi, kx=kx, jg=jg: e.tensor_tensor(
                out=xrl[kx][:, jg * 512:(jg + 1) * 512], in0=gt[gi][:],
                in1=xrl[kx][:, jg * 512:(jg + 1) * 512], op=ALU.add),
                waits=[(sV, m_og), (xl_sem, xl)])
            prev_comb = m_out
            if jg == 1:
                m_so = g_dma(dOS[kx], dout[tcg * 128:(tcg + 1) * 128, :], xrl[kx][:],
                             waits=[(sV, m_out)])
                xrl_war_v[kx] = m_out
                xrl_war_d[kx] = m_so

    fin_waits = [(dOS[0], dOS[0].n), (dOS[1], dOS[1].n),
                 (dHS[0], dHS[0].n), (dHS[1], dHS[1].n)]

    def fin(e, fin_waits=fin_waits):
        for s_, v_ in fin_waits:
            e.wait_ge(s_.h, v_)
    q("gpsimd", fin)

    es_psD.close()
    es_c.close()

    with nc.Block() as block:
        @block.sync
        def _(e):
            for fn in PL["sync"]:
                fn(e)

        @block.scalar
        def _(e):
            for fn in PL["scalar"]:
                fn(e)

        @block.vector
        def _(e):
            for fn in PL["vector"]:
                fn(e)

        @block.tensor
        def _(e):
            for fn in PL["tensor"]:
                fn(e)

        @block.gpsimd
        def _(e):
            for fn in PL["gpsimd"]:
                fn(e)

    return nc


def _pack_weights(i):
    f32 = np.float32
    gup = np.empty((NEXP, 128, 2048), dtype=f32)
    down = np.empty((NEXP, 128, 1024), dtype=f32)
    gate_w = np.asarray(i["L0_ffn_gate_w"], dtype=f32)
    up_w = np.asarray(i["L0_ffn_up_w"], dtype=f32)
    down_w = np.asarray(i["L0_ffn_down_w"], dtype=f32)
    for e in range(32):
        for gu, w in ((0, gate_w[e]), (1, up_w[e])):
            wT = w.T.reshape(8, 128, 128)
            gup[e, :, gu * 1024:(gu + 1) * 1024] = wT.transpose(1, 0, 2).reshape(128, 1024)
        down[e] = down_w[e].T
    sgT = np.asarray(i["L0_ffn_sgate_w"], dtype=f32).T
    suT = np.asarray(i["L0_ffn_sup_w"], dtype=f32).T
    sdT = np.asarray(i["L0_ffn_sdown_w"], dtype=f32).T
    for hc in range(4):
        for gu, wT_full in ((0, sgT), (1, suT)):
            wT = wT_full[:, hc * 128:(hc + 1) * 128].reshape(8, 128, 128)
            gup[32 + hc, :, gu * 1024:(gu + 1) * 1024] = \
                wT.transpose(1, 0, 2).reshape(128, 1024)
        down[32 + hc] = sdT[hc * 128:(hc + 1) * 128, :]
    routerT = np.ascontiguousarray(
        np.asarray(i["L0_ffn_router_w"], dtype=f32).T.reshape(8, 128, 32)
        .transpose(1, 0, 2).reshape(128, 256))
    rbias = np.ascontiguousarray(np.asarray(i["L0_ffn_router_bias"], dtype=f32).reshape(1, 32))
    out = {"gupT": np.ascontiguousarray(gup), "downT": np.ascontiguousarray(down),
           "routerT": routerT, "rbias": rbias}
    for d, pfx in (("f", "fwd"), ("b", "bwd")):
        Br = np.asarray(i[f"L0_attn_{pfx}_B_real"], dtype=f32).reshape(2, 128, 1024)
        Bi = np.asarray(i[f"L0_attn_{pfx}_B_imag"], dtype=f32).reshape(2, 128, 1024)
        out[f"B_{d}"] = np.ascontiguousarray(
            np.stack([Br, Bi]).transpose(2, 0, 1, 3).reshape(128, 4096))
        Cr = np.asarray(i[f"L0_attn_{pfx}_C_real"], dtype=f32).T.reshape(2, 128, 1024)
        Ci = np.asarray(i[f"L0_attn_{pfx}_C_imag"], dtype=f32).T.reshape(2, 128, 1024)
        out[f"CT_{d}"] = np.ascontiguousarray(
            np.stack([Cr, Ci]).transpose(2, 0, 1, 3).reshape(128, 4096))
        lr = np.asarray(i[f"L0_attn_{pfx}_log_real"], dtype=f32).reshape(2, 128)
        im = np.asarray(i[f"L0_attn_{pfx}_imag"], dtype=f32).reshape(2, 128)
        ld = np.asarray(i[f"L0_attn_{pfx}_log_Delta"], dtype=f32).reshape(2, 128)
        out[f"vec_{d}"] = np.ascontiguousarray(
            np.stack([lr, im, ld]).transpose(2, 0, 1).reshape(128, 6))
        out[f"D_{d}"] = np.ascontiguousarray(
            np.asarray(i[f"L0_attn_{pfx}_D"], dtype=f32).reshape(8, 128).T)
    return out


def kernel(**inputs):
    shared = _pack_weights(inputs)
    x = np.asarray(inputs["L0_attn_x"], dtype=np.float32)
    nc = build_program()
    in_maps = []
    for b in range(8):
        m = dict(shared)
        m["x"] = np.ascontiguousarray(x[b])
        in_maps.append(m)
    res = run_bass_kernel_spmd(nc, in_maps, core_ids=list(range(8)))
    return np.stack([r["out"] for r in res.results]).astype(np.float32)


# version 31
# speedup vs baseline: 1.0062x; 1.0062x over previous
import math
import contextlib
import numpy as np
import concourse.bass as bass
import concourse.mybir as mybir
from concourse.bass_utils import run_bass_kernel_spmd

F32 = mybir.dt.float32
BF16 = mybir.dt.bfloat16
I32 = mybir.dt.int32
AF = mybir.ActivationFunctionType
ALU = mybir.AluOpType
AX = mybir.AxisListType

T = 2048
DM = 1024
NEXP = 36
NTOK = T // 128
NDC = DM // 128
TWO_PI = 2.0 * math.pi


class Sem:
    def __init__(self, nc, es, name):
        self.h = es.enter_context(nc.semaphore(name))
        self.n = 0

    def bump(self, k=1):
        self.n += k
        return self.n


def _w(e, waits):
    for s, v in waits:
        if v > 0:
            e.wait_ge(s.h, v)


def build_program():
    nc = bass.Bass("TRN2", target_bir_lowering=False)
    es = contextlib.ExitStack()

    dx = nc.dram_tensor("x", [T, DM], F32, kind="ExternalInput").ap()
    dout = nc.dram_tensor("out", [T, DM], F32, kind="ExternalOutput").ap()
    dgup = nc.dram_tensor("gupT", [NEXP, 128, 2048], F32, kind="ExternalInput").ap()
    ddown = nc.dram_tensor("downT", [NEXP, 128, 1024], F32, kind="ExternalInput").ap()
    drout = nc.dram_tensor("routerT", [128, 256], F32, kind="ExternalInput").ap()
    dbias = nc.dram_tensor("rbias", [1, 32], F32, kind="ExternalInput").ap()
    dB = {d: nc.dram_tensor(f"B_{d}", [128, 4096], F32, kind="ExternalInput").ap() for d in "fb"}
    dCT = {d: nc.dram_tensor(f"CT_{d}", [128, 4096], F32, kind="ExternalInput").ap() for d in "fb"}
    dvec = {d: nc.dram_tensor(f"vec_{d}", [128, 6], F32, kind="ExternalInput").ap() for d in "fb"}
    dD = {d: nc.dram_tensor(f"D_{d}", [128, 8], F32, kind="ExternalInput").ap() for d in "fb"}
    hscr = nc.dram_tensor("hscr", [NEXP, 4, 128, 512], BF16, kind="Internal").ap()

    identf_d = nc.inline_tensor(np.eye(128, dtype=np.float32), name="identf_dram").ap()
    oh32_d = nc.inline_tensor(np.eye(32, dtype=np.float32), name="oh32f_dram").ap()
    onesf_d = nc.inline_tensor(np.ones((1, 128), dtype=np.float32), name="onesf_dram").ap()

    sb = lambda n, sh, dt: nc.alloc_sbuf_tensor(n, sh, dt).ap()
    xnT = sb("xnT", [128, NDC * T], BF16)
    xnTv = xnT.rearrange("p (dc t) -> p dc t", dc=NDC)
    xsT = {(d, ri): sb(f"xs_{d}{ri}", [128, 2 * T], BF16) for d in "fb" for ri in "ri"}
    CTb = {(d, ri): sb(f"CTb_{d}{ri}", [128, 2048], BF16) for d in "fb" for ri in "ri"}
    diagD = {d: sb(f"diagD_{d}", [128, 1024], BF16) for d in "fb"}
    routerTb = sb("routerTb", [128, 256], BF16)
    bias_bc = sb("bias_bc", [128, 32], F32)
    wTb = sb("wTb", [32, T], BF16)
    identf = sb("identf", [128, 128], F32)
    identb = sb("identb", [128, 128], BF16)
    oh32b = sb("oh32b", [32, 32], BF16)
    onesf = sb("onesf", [1, 128], F32)
    biasrow = sb("biasrow", [1, 32], F32)
    c_hpi = sb("c_hpi", [128, 1], F32)
    c_eps = sb("c_eps", [128, 1], F32)
    c_nh = sb("c_nh", [128, 1], F32)
    vt = {}
    for d in "fb":
        vt[d] = {k: sb(f"v_{d}_{k}", [128, 2], F32)
                 for k in ["delta", "elr", "ngd", "rr", "theta", "kf", "thr", "half",
                           "sh", "ch", "sinth", "costh", "lbr", "lbi", "l2m", "onem",
                           "lsr", "lsi", "dinv", "nr", "scr", "sci", "tmp"]}
        vt[d]["vecs"] = sb(f"vecs_{d}", [128, 6], F32)
        vt[d]["Dt"] = sb(f"Dt_{d}", [128, 8], F32)
        vt[d]["kint"] = sb(f"kint_{d}", [128, 2], I32)
        vt[d]["AB"] = sb(f"AB_{d}", [128, 44], F32)

    routerTb_lo = sb("routerTb_lo", [128, 256], BF16)
    xrl = [sb(f"xrl{i}", [128, 1024], F32) for i in range(2)]

    es_ab = contextlib.ExitStack()
    sbt = lambda st, n, sh, dt: st.enter_context(nc.sbuf_tensor(n, sh, dt)).ap()
    es_aonly = contextlib.ExitStack()
    sbr = lambda st, n, sh, dt: st.enter_context(nc.sbuf_tensor(n, sh, dt, side="right")).ap()
    stage = sbr(es_aonly, "stage", [128, 1024], F32)
    stage2 = sbr(es_aonly, "stage2", [128, 1024], F32)
    Bbar2 = sbr(es_aonly, "Bbar2", [128, 2048], BF16)
    xn_tok = [sbr(es_aonly, f"xn_tok{i}", [128, 1024], BF16) for i in range(2)]
    xnf_tok = sbr(es_aonly, "xnf_tok", [128, 1024], F32)
    xnlo_tok = [sbr(es_aonly, f"xnlo_tok{i}", [128, 1024], BF16) for i in range(2)]
    xnTlo = [sbr(es_aonly, f"xnTlo{i}", [128, 1024], BF16) for i in range(2)]
    routstage = sbr(es_aonly, "routstage", [128, 256], F32)
    r_lg = [sbr(es_aonly, f"r_lg{i}", [128, 32], F32) for i in range(2)]
    r_e = [sbr(es_aonly, f"r_e{i}", [128, 32], F32) for i in range(2)]
    r_cur = [sbr(es_aonly, f"r_cur{i}", [128, 32], F32) for i in range(2)]
    r_msk = [sbr(es_aonly, f"r_msk{i}", [128, 32], F32) for i in range(2)]
    r_wb = [sbr(es_aonly, f"r_wb{i}", [128, 32], BF16) for i in range(2)]
    r_m = [sbr(es_aonly, f"r_m{i}", [128, 1], F32) for i in range(2)]
    r_s1 = [sbr(es_aonly, f"r_s1{i}", [128, 1], F32) for i in range(2)]
    r_s2 = [sbr(es_aonly, f"r_s2{i}", [128, 1], F32) for i in range(2)]
    ln_bn = [sbr(es_aonly, f"ln_bn{i}", [128, 12], F32) for i in range(2)]
    ln_mv = [sbr(es_aonly, f"ln_mv{i}", [128, 2], F32) for i in range(2)]
    ln_rstd = [sbr(es_aonly, f"ln_rstd{i}", [128, 1], F32) for i in range(2)]
    Bbar1 = sbt(es_ab, "Bbar1", [128, 2048], BF16)
    BTb = {ri: sbt(es_ab, f"BTb_{ri}", [128, 2048], BF16) for ri in "ri"}
    BTbv = {ri: BTb[ri].rearrange("p (dc cc c) -> p dc cc c", dc=8, cc=2) for ri in "ri"}
    cosT = sbt(es_ab, "cosT", [128, T], F32)
    sinT = sbt(es_ab, "sinT", [128, T], F32)
    cmb = {ri: sbt(es_ab, f"cmb_{ri}", [128, T], F32) for ri in "ri"}
    bq = {ri: sbt(es_ab, f"bq_{ri}", [128, T], F32) for ri in "ri"}
    uu = {ri: sbt(es_ab, f"u_{ri}", [128, T], F32) for ri in "ri"}

    dM = Sem(nc, es, "dM")
    dX = [Sem(nc, es, f"dX{i}") for i in range(2)]
    dST = Sem(nc, es, "dST")
    dST2 = Sem(nc, es, "dST2")
    dBU = [Sem(nc, es, f"dBU{i}") for i in range(2)]
    dGU = [Sem(nc, es, f"dGU{i}") for i in range(2)]
    dDW = [Sem(nc, es, f"dDW{i}") for i in range(3)]
    dHL = [Sem(nc, es, f"dHL{i}") for i in range(2)]
    dHS = [Sem(nc, es, f"dHS{i}") for i in range(2)]
    dOS = [Sem(nc, es, f"dOS{i}") for i in range(2)]
    sP = Sem(nc, es, "sP")
    sV = Sem(nc, es, "sV")
    sA = Sem(nc, es, "sA")
    sG = Sem(nc, es, "sG")

    PL = {k: [] for k in ["sync", "scalar", "vector", "tensor", "gpsimd"]}

    def q(key, fn):
        PL[key].append(fn)

    def load(sem, out_ap, in_ap, waits=()):
        waits = tuple(waits)

        def f(e, sem=sem, out_ap=out_ap, in_ap=in_ap, waits=waits):
            _w(e, waits)
            e.dma_start(out=out_ap, in_=in_ap).then_inc(sem.h, 16)
        q("sync", f)
        return sem.bump(16)

    def g_dma(sem, out_ap, in_ap, waits=()):
        waits = tuple(waits)

        def f(e, sem=sem, out_ap=out_ap, in_ap=in_ap, waits=waits):
            _w(e, waits)
            e.dma_start(out=out_ap, in_=in_ap).then_inc(sem.h, 16)
        q("gpsimd", f)
        return sem.bump(16)

    def st_dma(sem, out_ap, in_ap, waits=()):
        waits = tuple(waits)

        def f(e, sem=sem, out_ap=out_ap, in_ap=in_ap, waits=waits):
            _w(e, waits)
            e.dma_start(out=out_ap, in_=in_ap).then_inc(sem.h, 16)
        q("scalar", f)
        return sem.bump(16)

    def _mk(key, sem):
        def op(fn, waits=(), inc=True, chain=False):
            waits = tuple(waits) + (((sem, sem.n),) if chain else ())

            def f(e, fn=fn, waits=waits):
                _w(e, waits)
                inst = fn(e)
                if inc:
                    inst.then_inc(sem.h, 1)
            q(key, f)
            return sem.bump() if inc else sem.n
        return op

    v_op = _mk("vector", sV)
    a_op = _mk("scalar", sA)
    g_op = _mk("gpsimd", sG)
    t_op = _mk("tensor", sP)

    es_psAB = contextlib.ExitStack()
    tp = es_psAB.enter_context(nc.psum_tensor("tp", [128, 4096], BF16)).ap()
    tpv = tp.rearrange("p (b c) -> p b c", b=4)
    tpv8 = tp.rearrange("p (b k c) -> p b k c", b=4, k=8)
    r_ps = [es_psAB.enter_context(nc.psum_tensor(f"r_ps{i}", [128, 32], F32)).ap()
            for i in range(2)]
    wt_ps = [es_psAB.enter_context(nc.psum_tensor(f"wt_ps{i}", [32, 128], BF16)).ap()
             for i in range(2)]

    load(dM, identf[:], identf_d)
    load(dM, onesf[:], onesf_d)
    load(dM, biasrow[:], dbias)
    load(dM, routstage[:], drout)
    for d in "fb":
        load(dM, vt[d]["vecs"][:], dvec[d])
        load(dM, vt[d]["Dt"][:], dD[d])

    m_hpi = v_op(lambda e: e.memset(c_hpi[:], math.pi / 2.0))
    m_eps = v_op(lambda e: e.memset(c_eps[:], 1e-5))
    m_nh = v_op(lambda e: e.memset(c_nh[:], -0.5))
    load(dM, r_lg[0][0:32, 0:32], oh32_d)
    m_misc = dM.n
    m_identb = v_op(lambda e: e.tensor_copy(out=identb[:], in_=identf[:]),
                    waits=[(dM, m_misc)])
    m_oh = v_op(lambda e: e.tensor_copy(out=oh32b[:], in_=r_lg[0][0:32, 0:32]),
                waits=[(dM, m_misc)])
    m_routb = g_op(lambda e: e.tensor_copy(out=routerTb[:], in_=routstage[:]),
                   waits=[(dM, m_misc)])
    m_routlo = g_op(lambda e: e.tensor_tensor(out=routerTb_lo[:], in0=routstage[:],
                                              in1=routerTb[:], op=ALU.subtract),
                    waits=[(sG, m_routb)])
    m_biasmm = t_op(lambda e: e.matmul(r_ps[0][:], onesf[:], biasrow[:],
                                       start=True, stop=True),
                    waits=[(dM, m_misc)])
    m_biasbc = v_op(lambda e: e.tensor_copy(out=bias_bc[:], in_=r_ps[0][:]),
                    waits=[(sP, m_biasmm)])

    INV_D = 1.0 / DM
    ln_v_marks = [0] * NTOK
    ln_tp_marks = [0] * NTOK
    ln_lo_evac = [0] * NTOK
    rmm_marks = [0] * NTOK
    m_wq_arr = [0] * NTOK
    ln_rstd_m = [0, 0]
    tp_evac = [0, 0]
    w_evac = [0, 0]
    m_e_par = [0, 0]
    m_wt_par = [0, 0]
    m_lg_par = [0, 0]
    norm_par = [0, 0]
    def ln_bn_part(i):
        p2 = i % 2
        buf = xrl[p2]
        war = ([(sV, ln_v_marks[i - 2])] if i >= 2 else [])
        m_ld = load(dX[p2], buf[:], dx[i * 128:(i + 1) * 128, :], waits=war)
        v_op(lambda e, b=buf, p2=p2: e.bn_stats(out=ln_bn[p2][:, 0:6], in_=b[:, 0:512]),
             waits=[(dX[p2], m_ld), (sV, norm_par[p2])])
        v_op(lambda e, b=buf, p2=p2: e.bn_stats(out=ln_bn[p2][:, 6:12], in_=b[:, 512:1024]),
             chain=True)
        v_op(lambda e, p2=p2: e.bn_aggr(out=ln_mv[p2][:], in_=ln_bn[p2][:]), chain=True)
        m_var = sV.n
        a_op(lambda e, p2=p2: e.activation(out=ln_rstd[p2][:],
                                           in_=ln_mv[p2][:, 1:2], func=AF.Ln,
                                           bias=c_eps[:]),
             waits=[(sV, m_var), (sV, m_nh)])
        ln_rstd_m[p2] = a_op(lambda e, p2=p2: e.activation(out=ln_rstd[p2][:],
                                                           in_=ln_rstd[p2][:],
                                                           func=AF.Exp, scale=c_nh[:]),
                             chain=True)

    def ln_xnf_part(i):
        p2 = i % 2
        buf = xrl[p2]
        v_op(lambda e, b=buf, p2=p2: e.tensor_scalar(
            out=xnf_tok[:], in0=b[:], scalar1=ln_mv[p2][:, 0:1], scalar2=ln_rstd[p2][:],
            op0=ALU.subtract, op1=ALU.mult),
            waits=[(sA, ln_rstd_m[p2])] + ([(sP, ln_tp_marks[i - 2])] if i >= 2 else []),
            chain=True)
        v_op(lambda e, p2=p2: e.tensor_copy(out=xn_tok[p2][:], in_=xnf_tok[:]), chain=True)
        m_norm = v_op(lambda e, p2=p2: e.tensor_tensor(out=xnlo_tok[p2][:], in0=xnf_tok[:],
                                                       in1=xn_tok[p2][:], op=ALU.subtract),
                      chain=True)
        ln_v_marks[i] = m_norm
        norm_par[p2] = m_norm

    def ln_tp_part(i):
        p2 = i % 2
        m_norm = ln_v_marks[i]
        lo_v = xnTlo[p2].rearrange("p (dc t) -> p dc t", dc=8)
        war_lo = [(sP, rmm_marks[i - 2])] if i >= 2 else []
        for half in range(2):
            src_t = xn_tok[p2] if half == 0 else xnlo_tok[p2]
            def tfn(e, src_t=src_t, half=half):
                last = None
                for k in range(8):
                    last = e.transpose(
                        tp[:, half * 1024 + k * 128: half * 1024 + (k + 1) * 128],
                        src_t[:, k * 128:(k + 1) * 128], identb[:])
                return last
            m_tp = t_op(tfn, waits=[(sV, m_norm), (sV, m_identb), (sA, tp_evac[half])]
                             + (war_lo if half == 1 else []))
            ln_tp_marks[i] = m_tp
            def efn(e, i=i, half=half, lo_v=lo_v):
                if half == 1:
                    return e.copy(out=lo_v[:, 0:8, :], in_=tpv8[:, 1, 0:8, 0:128])
                return e.copy(out=xnTv[:, 0:8, i * 128:(i + 1) * 128],
                              in_=tpv8[:, 0, 0:8, 0:128])
            tp_evac[half] = a_op(efn, waits=[(sP, m_tp)])
        ln_lo_evac[i] = tp_evac[1]


    def router_mm_part(i):
        p2 = i % 2
        def rmm(e, i=i, p2=p2):
            last = None
            lo_t = xnTlo[p2]
            for dc in range(NDC):
                last = e.matmul(r_ps[p2][:],
                                xnT[:, dc * T + i * 128: dc * T + i * 128 + 128],
                                routerTb[:, dc * 32:(dc + 1) * 32],
                                start=(dc == 0), stop=False)
                last = e.matmul(r_ps[p2][:], lo_t[:, dc * 128:(dc + 1) * 128],
                                routerTb[:, dc * 32:(dc + 1) * 32],
                                start=False, stop=False)
                last = e.matmul(r_ps[p2][:],
                                xnT[:, dc * T + i * 128: dc * T + i * 128 + 128],
                                routerTb_lo[:, dc * 32:(dc + 1) * 32],
                                start=False, stop=(dc == NDC - 1))
            return last
        rmm_marks[i] = t_op(rmm, waits=[(sA, ln_lo_evac[i]), (sG, m_routlo),
                                        (sV, m_lg_par[p2])])

    def router_head_part(i):
        p2 = i % 2
        m_lg = v_op(lambda e, p2=p2: e.tensor_tensor(out=r_lg[p2][:], in0=r_ps[p2][:],
                                                     in1=bias_bc[:], op=ALU.add),
                    waits=[(sP, rmm_marks[i]), (sV, m_biasbc), (sA, m_e_par[p2])])
        m_lg_par[p2] = m_lg
        m_mx = v_op(lambda e, p2=p2: e.reduce_max(out=r_m[p2][:], in_=r_lg[p2][:],
                                                  axis=AX.X, negate=True), chain=True)
        m_e = a_op(lambda e, p2=p2: e.activation(out=r_e[p2][:], in_=r_lg[p2][:],
                                                 func=AF.Exp, bias=r_m[p2][:]),
                   waits=[(sV, m_mx)])
        m_e_par[p2] = m_e

    def router_tail_part(i):
        p2 = i % 2
        v_op(lambda e, p2=p2: e.tensor_copy(out=r_cur[p2][:], in_=r_e[p2][:]),
             waits=[(sA, m_e_par[p2])])
        for k in range(4):
            v_op(lambda e, p2=p2: e.reduce_max(out=r_m[p2][:], in_=r_cur[p2][:], axis=AX.X),
                 chain=True)
            if k < 3:
                v_op(lambda e, p2=p2: e.tensor_scalar(out=r_msk[p2][:], in0=r_cur[p2][:],
                                                      scalar1=r_m[p2][:], scalar2=None,
                                                      op0=ALU.is_ge), chain=True)
                v_op(lambda e, p2=p2: e.scalar_tensor_tensor(out=r_cur[p2][:],
                                                             in0=r_msk[p2][:],
                                                             scalar=-1e30, in1=r_cur[p2][:],
                                                             op0=ALU.mult, op1=ALU.add),
                     chain=True)
        v_op(lambda e, p2=p2: e.tensor_scalar(out=r_msk[p2][:], in0=r_e[p2][:],
                                              scalar1=r_m[p2][:], scalar2=None,
                                              op0=ALU.is_ge), chain=True)
        v_op(lambda e, p2=p2: e.tensor_tensor(out=r_cur[p2][:], in0=r_e[p2][:],
                                              in1=r_msk[p2][:], op=ALU.mult), chain=True)
        v_op(lambda e, p2=p2: e.reduce_sum(out=r_s1[p2][:], in_=r_cur[p2][:], axis=AX.X),
             chain=True)
        v_op(lambda e, p2=p2: e.tensor_scalar_max(out=r_s1[p2][:], in0=r_s1[p2][:],
                                                  scalar1=1e-12), chain=True)
        v_op(lambda e, p2=p2: e.reciprocal(out=r_s2[p2][:], in_=r_s1[p2][:]), chain=True)
        m_wq_arr[i] = v_op(lambda e, p2=p2: e.tensor_scalar(out=r_wb[p2][:],
                                                            in0=r_cur[p2][:],
                                                            scalar1=r_s2[p2][:], scalar2=None,
                                                            op0=ALU.mult),
                           waits=[(sP, m_wt_par[p2])], chain=True)

    def wt_part(i):
        p2 = i % 2
        m_wt = t_op(lambda e, p2=p2: e.transpose(wt_ps[p2][:], r_wb[p2][:], identb[:]),
                    waits=[(sV, m_wq_arr[i]), (sA, w_evac[p2])])
        m_wt_par[p2] = m_wt
        w_evac[p2] = a_op(lambda e, i=i, p2=p2: e.copy(out=wTb[:, i * 128:(i + 1) * 128],
                                                       in_=wt_ps[p2][:]),
                          waits=[(sP, m_wt)])

    prep_parts = {}

    def prep_act0(d):
        V = vt[d]
        lr, ldl = V["vecs"][:, 0:2], V["vecs"][:, 4:6]
        wv = [(dM, m_misc)]
        a_op(lambda e: e.activation(out=V["delta"][:], in_=ldl, func=AF.Exp), waits=wv)
        mA2 = a_op(lambda e: e.activation(out=V["elr"][:], in_=lr, func=AF.Exp), waits=wv)
        prep_parts[d] = {"mA2": mA2}

    def prep_v1(d):
        def vc(fn, waits=(), inc=True):
            return v_op(fn, waits=waits, inc=inc, chain=True)
        V = vt[d]
        im = V["vecs"][:, 2:4]
        mA2 = prep_parts[d]["mA2"]
        vc(lambda e: e.tensor_tensor(out=V["ngd"][:], in0=V["elr"][:], in1=V["delta"][:],
                                       op=ALU.mult), waits=[(sA, mA2)])
        mV1 = vc(lambda e: e.tensor_scalar(out=V["ngd"][:], in0=V["ngd"][:], scalar1=-1.0,
                                             scalar2=None, op0=ALU.mult))
        vc(lambda e: e.tensor_tensor(out=V["theta"][:], in0=im, in1=V["delta"][:],
                                       op=ALU.mult))
        MAGIC = 12582912.0
        vc(lambda e: e.tensor_scalar(out=V["kf"][:], in0=V["theta"][:],
                                     scalar1=1.0 / TWO_PI, scalar2=MAGIC,
                                     op0=ALU.mult, op1=ALU.add))
        vc(lambda e: e.tensor_scalar(out=V["kf"][:], in0=V["kf"][:], scalar1=-MAGIC,
                                     scalar2=None, op0=ALU.add))
        vc(lambda e: e.scalar_tensor_tensor(out=V["thr"][:], in0=V["kf"][:], scalar=-TWO_PI,
                                            in1=V["theta"][:], op0=ALU.mult, op1=ALU.add))
        m_half = vc(lambda e: e.tensor_scalar(out=V["half"][:], in0=V["thr"][:], scalar1=0.5,
                                                scalar2=None, op0=ALU.mult))
        prep_parts[d]["mV1"] = mV1
        prep_parts[d]["m_half"] = m_half

    def prep_act1(d):
        V = vt[d]
        m_r = a_op(lambda e: e.activation(out=V["rr"][:], in_=V["ngd"][:], func=AF.Exp),
                   waits=[(sV, prep_parts[d]["mV1"])])
        a_op(lambda e: e.activation(out=V["sh"][:], in_=V["half"][:], func=AF.Sin),
             waits=[(sV, prep_parts[d]["m_half"])])
        m_ch = a_op(lambda e: e.activation(out=V["ch"][:], in_=V["half"][:], func=AF.Sin,
                                           bias=c_hpi[:]), waits=[(sV, m_hpi)])
        prep_parts[d]["m_r"] = m_r
        prep_parts[d]["m_ch"] = m_ch

    def prep_v2(d):
        def vc(fn, waits=(), inc=True):
            return v_op(fn, waits=waits, inc=inc, chain=True)
        V = vt[d]
        im = V["vecs"][:, 2:4]
        m_ch = prep_parts[d]["m_ch"]
        m_r = prep_parts[d]["m_r"]
        vc(lambda e: e.tensor_tensor(out=V["sinth"][:], in0=V["sh"][:], in1=V["ch"][:],
                                       op=ALU.mult), waits=[(sA, m_ch)])
        vc(lambda e: e.tensor_scalar(out=V["sinth"][:], in0=V["sinth"][:], scalar1=2.0,
                                       scalar2=None, op0=ALU.mult))
        vc(lambda e: e.tensor_tensor(out=V["costh"][:], in0=V["sh"][:], in1=V["sh"][:],
                                       op=ALU.mult))
        m_cs = vc(lambda e: e.tensor_scalar(out=V["costh"][:], in0=V["costh"][:],
                                              scalar1=-2.0, scalar2=1.0,
                                              op0=ALU.mult, op1=ALU.add))
        vc(lambda e: e.tensor_tensor(out=V["lbr"][:], in0=V["rr"][:], in1=V["costh"][:],
                                       op=ALU.mult), waits=[(sA, m_r)])
        vc(lambda e: e.tensor_tensor(out=V["lbi"][:], in0=V["rr"][:], in1=V["sinth"][:],
                                       op=ALU.mult))
        vc(lambda e: e.tensor_tensor(out=V["l2m"][:], in0=V["elr"][:], in1=V["elr"][:],
                                       op=ALU.mult))
        vc(lambda e: e.tensor_tensor(out=V["tmp"][:], in0=im, in1=im, op=ALU.mult))
        vc(lambda e: e.tensor_tensor(out=V["l2m"][:], in0=V["l2m"][:], in1=V["tmp"][:],
                                       op=ALU.add))
        vc(lambda e: e.tensor_scalar(out=V["l2m"][:], in0=V["l2m"][:], scalar1=1e-12,
                                       scalar2=None, op0=ALU.is_lt))
        vc(lambda e: e.tensor_scalar(out=V["onem"][:], in0=V["l2m"][:], scalar1=-1.0,
                                       scalar2=1.0, op0=ALU.mult, op1=ALU.add))
        vc(lambda e: e.tensor_scalar(out=V["lsr"][:], in0=V["elr"][:], scalar1=-1.0,
                                       scalar2=None, op0=ALU.mult))
        vc(lambda e: e.tensor_tensor(out=V["lsr"][:], in0=V["lsr"][:], in1=V["onem"][:],
                                       op=ALU.mult))
        vc(lambda e: e.scalar_tensor_tensor(out=V["lsr"][:], in0=V["l2m"][:], scalar=1e-6,
                                              in1=V["lsr"][:], op0=ALU.mult, op1=ALU.add))
        vc(lambda e: e.tensor_tensor(out=V["lsi"][:], in0=im, in1=V["onem"][:],
                                       op=ALU.mult))
        vc(lambda e: e.tensor_tensor(out=V["dinv"][:], in0=V["lsr"][:], in1=V["lsr"][:],
                                       op=ALU.mult))
        vc(lambda e: e.tensor_tensor(out=V["tmp"][:], in0=V["lsi"][:], in1=V["lsi"][:],
                                       op=ALU.mult))
        vc(lambda e: e.tensor_tensor(out=V["dinv"][:], in0=V["dinv"][:], in1=V["tmp"][:],
                                       op=ALU.add))
        vc(lambda e: e.reciprocal(out=V["dinv"][:], in_=V["dinv"][:]))
        vc(lambda e: e.tensor_scalar(out=V["nr"][:], in0=V["lbr"][:], scalar1=-1.0,
                                       scalar2=None, op0=ALU.add))
        vc(lambda e: e.tensor_tensor(out=V["scr"][:], in0=V["nr"][:], in1=V["lsr"][:],
                                       op=ALU.mult))
        vc(lambda e: e.tensor_tensor(out=V["tmp"][:], in0=V["lbi"][:], in1=V["lsi"][:],
                                       op=ALU.mult))
        vc(lambda e: e.tensor_tensor(out=V["scr"][:], in0=V["scr"][:], in1=V["tmp"][:],
                                       op=ALU.add))
        vc(lambda e: e.tensor_tensor(out=V["scr"][:], in0=V["scr"][:], in1=V["dinv"][:],
                                       op=ALU.mult))
        vc(lambda e: e.tensor_tensor(out=V["sci"][:], in0=V["lbi"][:], in1=V["lsr"][:],
                                       op=ALU.mult))
        vc(lambda e: e.tensor_tensor(out=V["tmp"][:], in0=V["nr"][:], in1=V["lsi"][:],
                                       op=ALU.mult))
        vc(lambda e: e.tensor_tensor(out=V["sci"][:], in0=V["sci"][:], in1=V["tmp"][:],
                                       op=ALU.subtract))
        m_sc = vc(lambda e: e.tensor_tensor(out=V["sci"][:], in0=V["sci"][:],
                                              in1=V["dinv"][:], op=ALU.mult))
        prep_parts[d]["m_sc"] = m_sc
        prep_parts[d]["m_cs"] = m_cs

    def prep_v2b(d, half):
        def vc(fn, waits=(), inc=True):
            return v_op(fn, waits=waits, inc=inc, chain=True)
        V = vt[d]
        if half == 0:
            m_cs = prep_parts[d]["m_cs"]
            vc(lambda e: e.tensor_copy(out=V["AB"][:, 0:2], in_=V["costh"][:]),
                 waits=[(sV, m_cs)])
            vc(lambda e: e.tensor_copy(out=V["AB"][:, 2:4], in_=V["sinth"][:]))
        for k in (range(5) if half == 0 else range(5, 10)):
            a0, b0 = V["AB"][:, 4 * k:4 * k + 2], V["AB"][:, 4 * k + 2:4 * k + 4]
            a1 = V["AB"][:, 4 * (k + 1):4 * (k + 1) + 2]
            b1 = V["AB"][:, 4 * (k + 1) + 2:4 * (k + 1) + 4]
            vc(lambda e, a0=a0, a1=a1: e.tensor_tensor(out=a1, in0=a0, in1=a0, op=ALU.mult))
            vc(lambda e, b0=b0, b1=b1: e.tensor_tensor(out=b1, in0=b0, in1=b0, op=ALU.mult))
            vc(lambda e, a1=a1, b1=b1: e.tensor_tensor(out=a1, in0=a1, in1=b1,
                                                         op=ALU.subtract))
            vc(lambda e, a0=a0, b0=b0, b1=b1: e.tensor_tensor(out=b1, in0=a0, in1=b0,
                                                                op=ALU.mult))
            vc(lambda e, b1=b1: e.tensor_scalar(out=b1, in0=b1, scalar1=2.0, scalar2=None,
                                                  op0=ALU.mult))
        if half == 0:
            return
        m_ab = sV.n
        for jc in range(NDC):
            vc(lambda e, d=d, jc=jc: e.tensor_scalar(
                out=diagD[d][:, jc * 128:(jc + 1) * 128], in0=identf[:],
                scalar1=vt[d]["Dt"][:, jc:jc + 1], scalar2=None, op0=ALU.mult),
                waits=[(dM, m_misc)])
        prep_marks[d] = (prep_parts[d]["m_sc"], m_ab, prep_parts[d]["m_r"])

    prep_marks = {}

    m_ct_cast = {}
    ct_quarters = [(d, ri_i, hf) for d in "fb" for ri_i in range(2) for hf in range(2)]
    ct_load_marks = {}
    ct_cast_marks = {}
    stageb = [stage, stage2]
    dSTp = [dST, dST2]

    def ct_loads(ks):
        for k in ks:
            d, ri_i, hf = ct_quarters[k]
            war = [(sV, ct_cast_marks[k - 2])] if k - 2 in ct_cast_marks else []
            ct_load_marks[k] = g_dma(
                dSTp[k % 2], stageb[k % 2][:],
                dCT[d].rearrange("p (r h x) -> p r h x", r=2, h=2)[:, ri_i, hf, :],
                waits=war)

    def ct_casts(ks):
        for k in ks:
            d, ri_i, hf = ct_quarters[k]
            ri = "ri"[ri_i]
            sc = -1.0 if ri == "i" else 1.0
            ct_cast_marks[k] = v_op(lambda e, d=d, ri=ri, sc=sc, hf=hf, k=k: e.tensor_scalar(
                out=CTb[(d, ri)][:, hf * 1024:(hf + 1) * 1024], in0=stageb[k % 2][:],
                scalar1=sc, scalar2=None, op0=ALU.mult),
                waits=[(dSTp[k % 2], ct_load_marks[k])])
            m_ct_cast[(d, ri)] = ct_cast_marks[k]

    tr_evac = sA.n
    bb_state = {"cmb_v": 0, "cmb_g": 0, "tr": sA.n}

    def bbar_products_f():
        V = vt["f"]
        m_sc = prep_marks["f"][0]
        m_lbr = load(dBU[0], cmb["r"][:], dB["f"].rearrange("p (r x) -> p r x", r=2)[:, 0, :])
        m_lbi = load(dBU[1], cmb["i"][:], dB["f"].rearrange("p (r x) -> p r x", r=2)[:, 1, :])
        for ri in "ri":
            dst = Bbar1 if ri == "r" else Bbar2
            scr = bq["r"] if ri == "r" else uu["r"]
            for cc in range(2):
                sl = slice(cc * 1024, (cc + 1) * 1024)
                scr_c = V["scr"][:, cc:cc + 1]
                sci_c = V["sci"][:, cc:cc + 1]
                oth = "i" if ri == "r" else "r"
                sgn = ALU.subtract if ri == "r" else ALU.add
                v_op(lambda e, sl=sl, sci_c=sci_c, oth=oth, scr=scr: e.tensor_scalar(
                    out=scr[:, sl], in0=cmb[oth][:, sl], scalar1=sci_c, scalar2=None,
                    op0=ALU.mult),
                    waits=[(dBU[1], m_lbi), (dBU[0], m_lbr), (sV, m_sc)])
                v_op(lambda e, sl=sl, scr_c=scr_c, ri=ri, sgn=sgn, dst=dst, scr=scr:
                     e.scalar_tensor_tensor(
                    out=dst[:, sl], in0=cmb[ri][:, sl], scalar=scr_c, in1=scr[:, sl],
                    op0=ALU.mult, op1=sgn), chain=True)
        bb_state["m_bbar_f"] = sV.n
        bb_state["cmb_v"] = sV.n

    def bbar_transposes_f():
        m_bbar = bb_state["m_bbar_f"]
        tr = bb_state["tr"]
        for ri in "ri":
            src = Bbar1 if ri == "r" else Bbar2
            for cc in range(2):
                for half in range(2):
                    def tfn(e, cc=cc, half=half, src=src):
                        last = None
                        for k in range(4):
                            dc = half * 4 + k
                            last = e.transpose(
                                tp[:, k * 1024:k * 1024 + 128],
                                src[:, cc * 1024 + dc * 128:
                                    cc * 1024 + dc * 128 + 128],
                                identb[:])
                        return last
                    m_tp = t_op(tfn, waits=[(sV, m_bbar), (sA, tr),
                                            (sA, tp_evac[0]), (sA, tp_evac[1])])
                    def efn(e, ri=ri, cc=cc, half=half):
                        return e.copy(out=BTbv[ri][:, half * 4:(half + 1) * 4, cc, :],
                                      in_=tpv[:, 0:4, 0:128])
                    tr = a_op(efn, waits=[(sP, m_tp)])
                    tp_evac[0] = tr
                    tp_evac[1] = tr
        bb_state["tr"] = tr
        return tr

    def plan_bbar(d, use_tp=True):
        V = vt[d]
        m_sc, m_ab, m_r = prep_marks[d]
        m_lbr = load(dBU[0], cmb["r"][:], dB[d].rearrange("p (r x) -> p r x", r=2)[:, 0, :],
                     waits=[(sV, bb_state["cmb_v"]), (sG, bb_state["cmb_g"])])
        m_lbi = load(dBU[1], cmb["i"][:], dB[d].rearrange("p (r x) -> p r x", r=2)[:, 1, :],
                     waits=[(sV, bb_state["cmb_v"]), (sG, bb_state["cmb_g"])])
        m_tp_last = 0
        tr = bb_state["tr"]
        for ri in "ri":
            for cc in range(2):
                sl = slice(cc * 1024, (cc + 1) * 1024)
                scr_c = V["scr"][:, cc:cc + 1]
                sci_c = V["sci"][:, cc:cc + 1]
                oth = "i" if ri == "r" else "r"
                sgn = ALU.subtract if ri == "r" else ALU.add
                v_op(lambda e, sl=sl, sci_c=sci_c, oth=oth: e.tensor_scalar(
                    out=bq["r"][:, sl], in0=cmb[oth][:, sl], scalar1=sci_c, scalar2=None,
                    op0=ALU.mult),
                    waits=[(dBU[1], m_lbi), (dBU[0], m_lbr), (sV, m_sc),
                           (sP, m_tp_last)])
                v_op(lambda e, sl=sl, scr_c=scr_c, ri=ri, sgn=sgn: e.scalar_tensor_tensor(
                    out=Bbar1[:, sl], in0=cmb[ri][:, sl], scalar=scr_c, in1=bq["r"][:, sl],
                    op0=ALU.mult, op1=sgn), chain=True)
            m_bbar = sV.n
            if use_tp:
                for cc in range(2):
                    for half in range(2):
                        def tfn(e, cc=cc, half=half):
                            last = None
                            for k in range(4):
                                dc = half * 4 + k
                                last = e.transpose(
                                    tp[:, k * 1024:k * 1024 + 128],
                                    Bbar1[:, cc * 1024 + dc * 128:
                                          cc * 1024 + dc * 128 + 128],
                                    identb[:])
                            return last
                        m_tp = t_op(tfn, waits=[(sV, m_bbar), (sA, tr),
                                                (sA, tp_evac[0]), (sA, tp_evac[1])])
                        m_tp_last = m_tp
                        def efn(e, ri=ri, cc=cc, half=half):
                            return e.copy(out=BTbv[ri][:, half * 4:(half + 1) * 4, cc, :],
                                          in_=tpv[:, 0:4, 0:128])
                        tr = a_op(efn, waits=[(sP, m_tp)])
                        tp_evac[0] = tr
                        tp_evac[1] = tr
            else:
                bviews = [bu_ps[i].bitcast(BF16) for i in range(2)]
                for cc in range(2):
                    for half in range(4):
                        def tfn(e, cc=cc, half=half):
                            last = None
                            for k in range(2):
                                dc = half * 2 + k
                                last = e.transpose(
                                    bviews[k][:, 0:128],
                                    Bbar1[:, cc * 1024 + dc * 128:
                                          cc * 1024 + dc * 128 + 128],
                                    identb[:])
                            return last
                        m_tp = t_op(tfn, waits=[(sV, m_bbar), (sA, tr),
                                                (sV, prod_done[0]),
                                                (sV, prod_done[1])])
                        m_tp_last = m_tp
                        def efn(e, ri=ri, cc=cc, half=half):
                            e.copy(out=BTbv[ri][:, 2 * half, cc, :],
                                   in_=bviews[0][:, 0:128])
                            return e.copy(out=BTbv[ri][:, 2 * half + 1, cc, :],
                                          in_=bviews[1][:, 0:128])
                        tr = a_op(efn, waits=[(sP, m_tp)])
        bb_state["cmb_v"] = sV.n
        bb_state["tr"] = tr
        return tr


    prep_act0("f")
    prep_act0("b")
    ln_bn_part(0)
    ln_xnf_part(0)
    ln_bn_part(1)
    ln_xnf_part(1)
    ln_tp_part(0)
    m_btb = None
    m_A_v = None
    for i in range(NTOK):
        if i + 2 < NTOK:
            ln_bn_part(i + 2)
        router_mm_part(i)
        router_head_part(i)
        if i + 1 < NTOK:
            ln_tp_part(i + 1)
        if i >= 1:
            wt_part(i - 1)
        router_tail_part(i)
        if i + 2 < NTOK:
            ln_xnf_part(i + 2)
        if i == 0:
            prep_v1("f")
        elif i == 1:
            prep_act1("f")
            prep_v1("b")
        elif i == 2:
            prep_v2("f")
            prep_act1("b")
        elif i == 3:
            prep_v2b("f", 0)
        elif i == 4:
            prep_v2b("f", 1)
            prep_v2("b")
            ct_loads([0, 1])
        elif i == 5:
            prep_v2b("b", 0)
            ct_casts([0, 1])
            ct_loads([2, 3])
        elif i == 6:
            prep_v2b("b", 1)
            ct_casts([2, 3])
            ct_loads([4, 5])
        elif i == 7:
            ct_casts([4, 5])
            ct_loads([6, 7])
        elif i == 8:
            ct_casts([6, 7])
            bbar_products_f()
        elif i == 10:
            m_btb = bbar_transposes_f()
    wt_part(NTOK - 1)
    m_A_v = sV.n
    m_xnT_a = sA.n
    m_w_done = max(w_evac)




    es_aonly.close()
    m_aonly = [(sP, sP.n), (sV, sV.n), (sA, sA.n), (sG, sG.n)]



    es_psAB.close()
    es_psBH = contextlib.ExitStack()
    bu_ps = [es_psBH.enter_context(nc.psum_tensor(f"bu{i}", [128, 512], F32)).ap()
             for i in range(3)]
    g_ps = [es_psBH.enter_context(nc.psum_tensor(f"g_ps{i}", [128, 512], F32)).ap()
            for i in range(2)]
    u_ps = [es_psBH.enter_context(nc.psum_tensor(f"u_ps{i}", [128, 512], F32)).ap()
            for i in range(2)]
    w_ps = [es_psBH.enter_context(nc.psum_tensor("w_ps0", [128, 512], F32)).ap()]

    es_c1 = contextlib.ExitStack()
    gupstage = [sbr(es_c1, f"gupstage{i}", [128, 1024], F32) for i in range(2)]
    gupb = [sbr(es_c1, f"gupb{i}", [128, 2048], BF16) for i in range(2)]
    hstage = [sbr(es_c1, f"hstage{i}", [128, 512], BF16) for i in range(2)]
    t1s = [sbr(es_c1, f"t1s{i}", [128, 512], F32) for i in range(2)]

    gup_cast = [0] * NEXP
    gup_pe_done = [0] * NEXP
    h_silu = {}
    h_t2 = {}
    h_h = {}
    h_w = {}
    h_store = {}
    hstage_war = [0, 0]
    gup_halfcast = []

    def plan_gup_load(e_i):
        for hf in range(2):
            st_i = (2 * e_i + hf) % 2
            idx = 2 * e_i + hf
            war = [(sG, gup_halfcast[idx - 2])] if idx >= 2 else list(m_aonly)
            m_l = load(dGU[st_i], gupstage[st_i][:],
                       dgup[e_i].rearrange("p (h x) -> p h x", h=2)[:, hf, :], waits=war)
            wv = [(dGU[st_i], m_l)]
            if e_i >= 2:
                wv.append((sP, gup_pe_done[e_i - 2]))
            m_c = g_op(lambda e, st_i=st_i, e_i=e_i, hf=hf: e.tensor_copy(
                out=gupb[e_i % 2][:, hf * 1024:(hf + 1) * 1024], in_=gupstage[st_i][:]),
                waits=wv)
            gup_halfcast.append(m_c)
        gup_cast[e_i] = gup_halfcast[-1]

    plan_gup_load(0)
    plan_gup_load(1)
    h_state = {"n": 0}

    def emit_h_units(count):
        for _ in range(count):
            idx = h_state["n"]
            if idx >= NEXP * 4:
                return
            h_state["n"] += 1
            e_i, th = idx // 4, idx % 4
            is_moe = e_i < 32
            k2 = th % 2
            pg, pu, pw = g_ps[k2], u_ps[k2], w_ps[0]
            war_g = [(sA, h_silu.get(idx - 2, 0))]
            war_u = [(sV, h_t2.get(idx - 2, 0))]
            war_w = [(sV, h_h.get(idx - 1, 0))]
            if idx == 0:
                war_g += [(sA, m_btb), (sA, m_w_done), (sV, m_A_v)]
            def gmm(e, e_i=e_i, th=th, pg=pg):
                last = None
                for kc in range(NDC):
                    last = e.matmul(pg[:], gupb[e_i % 2][:, kc * 128:(kc + 1) * 128],
                                    xnT[:, kc * T + th * 512: kc * T + th * 512 + 512],
                                    start=(kc == 0), stop=(kc == NDC - 1))
                return last
            t_op(gmm, waits=[(sG, gup_cast[e_i]), (sA, m_xnT_a)] + war_g, inc=False)
            def umm(e, e_i=e_i, th=th, pu=pu):
                last = None
                for kc in range(NDC):
                    last = e.matmul(pu[:],
                                    gupb[e_i % 2][:, 1024 + kc * 128: 1024 + (kc + 1) * 128],
                                    xnT[:, kc * T + th * 512: kc * T + th * 512 + 512],
                                    start=(kc == 0), stop=(kc == NDC - 1))
                return last
            m_u = t_op(umm, waits=war_u)
            if is_moe:
                m_wb = t_op(lambda e, e_i=e_i, th=th, pw=pw: e.matmul(
                    pw[:], oh32b[:, e_i:e_i + 1].broadcast_to([32, 128]),
                    wTb[:, th * 512:(th + 1) * 512], start=True, stop=True),
                    waits=[(sA, m_w_done), (sV, m_oh)] + war_w)
            else:
                m_wb = 0
            m_si = a_op(lambda e, pg=pg, k=idx % 2: e.activation(
                out=t1s[k][:], in_=pg[:], func=AF.Silu),
                waits=[(sP, m_u), (sV, h_t2.get(idx - 2, 0))])
            h_silu[idx] = m_si
            m_t2 = v_op(lambda e, pu=pu, k=idx % 2: e.tensor_tensor(
                out=t1s[k][:], in0=t1s[k][:], in1=pu[:], op=ALU.mult),
                waits=[(sA, m_si)])
            h_t2[idx] = m_t2
            if is_moe:
                m_h = v_op(lambda e, pw=pw, k=idx % 2: e.tensor_tensor(
                    out=hstage[k][:], in0=t1s[k][:], in1=pw[:], op=ALU.mult),
                    waits=[(sP, m_wb), (dHS[idx % 2], hstage_war[idx % 2])])
            else:
                m_h = v_op(lambda e, k=idx % 2: e.tensor_copy(
                    out=hstage[k][:], in_=t1s[k][:]),
                    waits=[(dHS[idx % 2], hstage_war[idx % 2])])
            h_h[idx] = m_h
            h_w[idx] = m_h
            m_st = st_dma(dHS[idx % 2], hscr[e_i, th], hstage[idx % 2][:],
                          waits=[(sV, m_h)])
            h_store[idx] = m_st
            hstage_war[idx % 2] = m_st
            if th == 3:
                gup_pe_done[e_i] = sP.n
                if e_i + 2 < NEXP:
                    plan_gup_load(e_i + 2)

    m_xs_all = 0
    last_scan_read_v = 0
    last_pool_rw = 0
    prod_done = [0, 0, 0]
    first_bu = True
    cur_btb = m_btb
    for d in "fb":
        V = vt[d]
        m_sc, m_ab, m_r = prep_marks[d]
        rev = (d == "b")
        if d == "b":
            bb_state["cmb_v"] = last_scan_read_v
            bb_state["cmb_g"] = last_pool_rw
            cur_btb = plan_bbar("b", use_tp=False)
        for cc in range(2):
            v_op(lambda e: e.memset(cosT[:, 0:1], 1.0),
                 waits=[(sV, last_scan_read_v), (sG, last_pool_rw)])
            v_op(lambda e: e.memset(sinT[:, 0:1], 0.0))
            for k in range(11):
                m = 1 << k
                mm = min(m, T - m)
                ch_ = (m <= 512)
                am = V["AB"][:, 4 * k + cc:4 * k + cc + 1]
                bm = V["AB"][:, 4 * k + 2 + cc:4 * k + 2 + cc + 1]
                v_op(lambda e, mm=mm, bm=bm: e.tensor_scalar(
                    out=bq["i"][:, 0:mm], in0=sinT[:, 0:mm], scalar1=bm, scalar2=None,
                    op0=ALU.mult), chain=ch_,
                    waits=[(sV, m_ab)] if k == 0 else ())
                v_op(lambda e, m=m, mm=mm, am=am: e.scalar_tensor_tensor(
                    out=cosT[:, m:m + mm], in0=cosT[:, 0:mm], scalar=am, in1=bq["i"][:, 0:mm],
                    op0=ALU.mult, op1=ALU.subtract), chain=ch_)
                v_op(lambda e, mm=mm, bm=bm: e.tensor_scalar(
                    out=bq["i"][:, 0:mm], in0=cosT[:, 0:mm], scalar1=bm, scalar2=None,
                    op0=ALU.mult), chain=ch_)
                v_op(lambda e, m=m, mm=mm, am=am: e.scalar_tensor_tensor(
                    out=sinT[:, m:m + mm], in0=sinT[:, 0:mm], scalar=am, in1=bq["i"][:, 0:mm],
                    op0=ALU.mult, op1=ALU.add), chain=ch_)
            m_ph = sV.n

            for tt in range(4):
                bk_r = (2 * tt) % 3
                bk_i = (2 * tt + 1) % 3
                wv_r = [(sV, prod_done[bk_r])]
                wv_i = [(sV, prod_done[bk_i])]
                if first_bu:
                    wv_r += [(sA, m_w_done), (sV, m_A_v), (sA, m_xnT_a)]
                    first_bu = False
                wv_r.append((sA, cur_btb))
                wv_i.append((sA, cur_btb))
                def bfnr(e, d=d, cc=cc, tt=tt, pp=bu_ps[bk_r]):
                    last = None
                    for dc in range(NDC):
                        last = e.matmul(pp[:], BTbv["r"][:, dc, cc, :],
                                        xnT[:, dc * T + tt * 512: dc * T + tt * 512 + 512],
                                        start=(dc == 0), stop=(dc == NDC - 1))
                    return last
                m_mr = t_op(bfnr, waits=wv_r)
                def bfni(e, d=d, cc=cc, tt=tt, pp=bu_ps[bk_i]):
                    last = None
                    for dc in range(NDC):
                        last = e.matmul(pp[:], BTbv["i"][:, dc, cc, :],
                                        xnT[:, dc * T + tt * 512: dc * T + tt * 512 + 512],
                                        start=(dc == 0), stop=(dc == NDC - 1))
                    return last
                m_mi = t_op(bfni, waits=wv_i)
                ts_ = slice(tt * 512, (tt + 1) * 512)
                wv_p = [(sP, m_mi), (sV, m_ph), (sG, last_pool_rw)]
                v_op(lambda e, ts_=ts_, pp=bu_ps[bk_r]: e.tensor_tensor(
                    out=bq["r"][:, ts_], in0=cosT[:, ts_], in1=pp[:], op=ALU.mult),
                    waits=wv_p)
                v_op(lambda e, ts_=ts_, pp=bu_ps[bk_i]: e.tensor_tensor(
                    out=uu["r"][:, ts_], in0=sinT[:, ts_], in1=pp[:], op=ALU.mult))
                v_op(lambda e, ts_=ts_, pp=bu_ps[bk_i]: e.tensor_tensor(
                    out=bq["i"][:, ts_], in0=cosT[:, ts_], in1=pp[:], op=ALU.mult))
                m_p4 = v_op(lambda e, ts_=ts_, pp=bu_ps[bk_r]: e.tensor_tensor(
                    out=uu["i"][:, ts_], in0=sinT[:, ts_], in1=pp[:], op=ALU.mult))
                prod_done[bk_r] = m_p4
                prod_done[bk_i] = m_p4
                emit_h_units(1)
            A_ = (lambda ap: ap[:, ::-1]) if rev else (lambda ap: ap)
            sgn_r = ALU.subtract if rev else ALU.add
            sgn_i = ALU.add if rev else ALU.subtract
            m_c1 = v_op(lambda e, A_=A_, sgn_r=sgn_r: e.tensor_tensor(
                out=cmb["r"][:], in0=A_(bq["r"][:]), in1=A_(uu["r"][:]), op=sgn_r),
                waits=[(sV, m_p4)])
            m_c2 = g_op(lambda e, A_=A_, sgn_i=sgn_i: e.tensor_tensor(
                out=cmb["i"][:], in0=A_(bq["i"][:]), in1=A_(uu["i"][:]), op=sgn_i),
                waits=[(sV, m_p4)])
            rbc = V["rr"][:, cc:cc + 1].broadcast_to([128, T])
            m_s1 = v_op(lambda e, rbc=rbc: e.tensor_tensor_scan(
                out=bq["r"][:], data0=rbc, data1=cmb["r"][:], initial=0.0,
                op0=ALU.mult, op1=ALU.add), waits=[(sV, m_c1), (sA, m_r)])
            m_s2 = v_op(lambda e, rbc=rbc: e.tensor_tensor_scan(
                out=bq["i"][:], data0=rbc, data1=cmb["i"][:], initial=0.0,
                op0=ALU.mult, op1=ALU.add), waits=[(sG, m_c2)])
            emit_h_units(2)
            B_ = (lambda ap: ap[:, ::-1]) if rev else (lambda ap: ap)
            usgn_r = ALU.add if rev else ALU.subtract
            usgn_i = ALU.subtract if rev else ALU.add
            wv2 = [(sV, m_s1), (sV, m_s2)]
            mA = v_op(lambda e, B_=B_: e.tensor_tensor(out=uu["r"][:], in0=cosT[:],
                                                       in1=B_(bq["r"][:]), op=ALU.mult),
                      waits=wv2)
            mB = g_op(lambda e, B_=B_: e.tensor_tensor(out=uu["i"][:], in0=sinT[:],
                                                       in1=B_(bq["i"][:]), op=ALU.mult),
                      waits=wv2)
            mC = v_op(lambda e, d=d, cc=cc, usgn_r=usgn_r: e.tensor_tensor(
                out=xsT[(d, "r")][:, cc * T:(cc + 1) * T], in0=uu["r"][:], in1=uu["i"][:],
                op=usgn_r), waits=[(sG, mB)])
            mD = g_op(lambda e, B_=B_: e.tensor_tensor(out=uu["r"][:], in0=cosT[:],
                                                       in1=B_(bq["i"][:]), op=ALU.mult),
                      waits=wv2 + [(sV, mC)])
            mE = v_op(lambda e, B_=B_: e.tensor_tensor(out=uu["i"][:], in0=sinT[:],
                                                       in1=B_(bq["r"][:]), op=ALU.mult),
                      waits=[(sV, mC)])
            mF = v_op(lambda e, d=d, cc=cc, usgn_i=usgn_i: e.tensor_tensor(
                out=xsT[(d, "i")][:, cc * T:(cc + 1) * T], in0=uu["r"][:], in1=uu["i"][:],
                op=usgn_i), waits=[(sG, mD), (sV, mE)])
            last_scan_read_v = mF
            last_pool_rw = mD
            m_xs_all = mF
            emit_h_units(3)

    emit_h_units(NEXP * 4)
    m_hprod_a = sA.n
    m_hprod_v = sV.n

    es_psBH.close()
    es_ab.close()
    es_c = contextlib.ExitStack()
    downstage = [sbt(es_c, f"downstage{i}", [128, 2048], F32) for i in range(3)]
    downb = [sbt(es_c, f"downb{i}", [128, 2048], BF16) for i in range(3)]
    hld = [sbt(es_c, f"hld{i}", [128, 2048], BF16) for i in range(2)]
    gt = [sbt(es_c, f"gt{i}", [128, 512], F32) for i in range(8)]
    stl = [sbt(es_c, f"stl{i}", [128, 512], F32) for i in range(8)]
    sadd = [sbt(es_c, f"sadd{i}", [128, 512], F32) for i in range(2)]
    es_psD = contextlib.ExitStack()
    d_ps = [es_psD.enter_context(nc.psum_tensor(f"d_ps{i}", [128, 512], F32)).ap()
            for i in range(8)]

    down_cast = [0, 0, 0]
    down_pe = [0, 0, 0]
    hld_pe = [0, 0]
    prev_ys_evac = 0
    prev_comb = 0
    xrl_war_v = [ln_v_marks[-2], ln_v_marks[-1]]
    xrl_war_d = [0, 0]
    cur_xrl_ld = 0
    cur_xrl_sem = dX[0]
    sadd_war = [0, 0]
    dd_r = ddown.rearrange("e p j -> p e j")
    hs_r = hscr.rearrange("e t p c -> t p e c")
    dcast_all = {}
    hld_all = {}
    prev_geb_last = 0

    def plan_down_unit(s, u):
        k = u % 3
        m_dl = load(dDW[k], downstage[k].rearrange("p (e j) -> p e j", e=2),
                    dd_r[:, 2 * u:2 * u + 2, :],
                    waits=[(sA, down_cast[k])])
        m_dc = a_op(lambda e, k=k: e.copy(out=downb[k][:], in_=downstage[k][:]),
                    waits=[(dDW[k], m_dl), (sP, down_pe[k])])
        down_cast[k] = m_dc
        dcast_all[(s, u)] = m_dc

    def plan_hld_unit(s, v):
        kv = v % 2
        e0 = 4 * v
        wv_sp = [(dHS[((e0 + j) * 4 + s) % 2], h_store[(e0 + j) * 4 + s])
                 for j in range(4)]
        hld_all[(s, v)] = load(dHL[kv], hld[kv].rearrange("p (e c) -> p e c", e=4),
                               hs_r[s][:, e0:e0 + 4, :],
                               waits=wv_sp + [(sP, hld_pe[kv])])

    for s in range(4):
        m_last_mm = 0
        if s == 0:
            plan_down_unit(0, 0)
            plan_down_unit(0, 1)
            plan_down_unit(0, 2)
            plan_hld_unit(0, 0)
            plan_hld_unit(0, 1)
        for e_i in range(NEXP):
            u = e_i // 2
            k = u % 3
            v = e_i // 4
            kv = v % 2
            wv = [(sA, dcast_all[(s, u)]), (dHL[kv], hld_all[(s, v)])]
            if e_i == 0:
                wv += [(sA, prev_geb_last)]
                if s == 0:
                    wv += [(sA, m_hprod_a), (sV, m_hprod_v)]
            def dmm(e, e_i=e_i, k=k, kv=kv):
                last = None
                eo = (e_i % 2) * 1024
                ho = (e_i % 4) * 512
                for tc in range(4):
                    for jg in range(2):
                        last = e.matmul(d_ps[tc * 2 + jg][:],
                                        hld[kv][:, ho + tc * 128: ho + (tc + 1) * 128],
                                        downb[k][:, eo + jg * 512: eo + (jg + 1) * 512],
                                        start=(e_i == 0), stop=(e_i == NEXP - 1))
                return last
            m_mm = t_op(dmm, waits=wv)
            if e_i % 2 == 1:
                down_pe[k] = m_mm
                if u + 3 < 18:
                    plan_down_unit(s, u + 3)
            if e_i % 4 == 3:
                hld_pe[kv] = m_mm
                if v + 2 < 9:
                    plan_hld_unit(s, v + 2)
            m_last_mm = m_mm
        if s + 1 < 4:
            plan_down_unit(s + 1, 0)
            plan_down_unit(s + 1, 1)
            plan_down_unit(s + 1, 2)
            plan_hld_unit(s + 1, 0)
            plan_hld_unit(s + 1, 1)
        silu_marks = []
        for gi in range(8):
            m_si = a_op(lambda e, gi=gi: e.activation(out=gt[gi][:], in_=d_ps[gi][:],
                                                      func=AF.Silu),
                        waits=[(sP, m_last_mm), (sV, prev_comb)])
            silu_marks.append(m_si)
        ge_f = [0] * 8
        ysf_m = [0] * 8
        def ymm_mk(dname, tok, jg, pf):
            def ymm(e, d=dname, tok=tok, jg=jg, pf=pf):
                last = None
                first = True
                for cc in range(2):
                    for ri in "ri":
                        last = e.matmul(
                            pf[:],
                            xsT[(d, ri)][:, cc * T + tok: cc * T + tok + 128],
                            CTb[(d, ri)][:, cc * 1024 + jg * 512:
                                         cc * 1024 + jg * 512 + 512],
                            start=first, stop=False)
                        first = False
                for k4 in range(4):
                    dc = jg * 4 + k4
                    last = e.matmul(pf[:, k4 * 128:(k4 + 1) * 128],
                                    xnT[:, dc * T + tok: dc * T + tok + 128],
                                    diagD[d][:, dc * 128:(dc + 1) * 128],
                                    start=False, stop=(k4 == 3))
                return last
            return ymm
        for gi in range(8):
            tc, jg = gi // 2, gi % 2
            tok = s * 512 + tc * 128
            pf = d_ps[gi]
            wv = [(sV, m_xs_all), (sV, m_ct_cast[("f", "i")]),
                  (sV, m_ct_cast[("f", "r")]), (sA, silu_marks[gi])]
            ysf_m[gi] = t_op(ymm_mk("f", tok, jg, pf), waits=wv)
            ge_f[gi] = a_op(lambda e, gi=gi, pf=pf: e.activation(
                out=stl[gi][:], in_=pf[:], func=AF.Gelu),
                waits=[(sP, ysf_m[gi]), (sV, prev_comb)])
        ysb_info = []
        for gi in range(8):
            tc, jg = gi // 2, gi % 2
            tok = s * 512 + tc * 128
            tcg = s * 4 + tc
            kx = tcg % 2
            pf = d_ps[gi]
            if jg == 0 and tc < 2:
                cur_xrl_ld = g_dma(dX[kx], xrl[kx][:], dx[tcg * 128:(tcg + 1) * 128, :],
                                   waits=[(sV, xrl_war_v[kx]), (dOS[kx], xrl_war_d[kx])])
                cur_xrl_sem = dX[kx]
            elif jg == 0:
                cur_xrl_ld = None
                cur_xrl_sem = dX[kx]
            wv = [(sV, m_ct_cast[("b", "i")]), (sV, m_ct_cast[("b", "r")]),
                  (sA, ge_f[gi])]
            m_ysb = t_op(ymm_mk("b", tok, jg, pf), waits=wv)
            m_geb = a_op(lambda e, gi=gi, pf=pf: e.activation(
                out=sadd[gi % 2][:], in_=pf[:], func=AF.Gelu),
                waits=[(sP, m_ysb), (sV, sadd_war[gi % 2])])
            prev_geb_last = m_geb
            m_sadd = v_op(lambda e, gi=gi: e.tensor_tensor(
                out=stl[gi][:], in0=stl[gi][:], in1=sadd[gi % 2][:], op=ALU.add),
                waits=[(sA, m_geb)])
            sadd_war[gi % 2] = m_sadd
            ysb_info.append((gi, jg, kx, tcg, cur_xrl_ld, cur_xrl_sem))
        late_lds = {}
        for gi, jg, kx, tcg, xl, xl_sem in ysb_info:
            if xl is None and kx not in late_lds:
                late_lds[kx] = g_dma(dX[kx], xrl[kx][:],
                                     dx[tcg * 128:(tcg + 1) * 128, :],
                                     waits=[(sV, xrl_war_v[kx]),
                                            (dOS[kx], xrl_war_d[kx])])
            if xl is None:
                xl = late_lds[kx]
            m_og = v_op(lambda e, gi=gi: e.tensor_tensor(
                out=gt[gi][:], in0=stl[gi][:], in1=gt[gi][:], op=ALU.mult))
            m_out = v_op(lambda e, gi=gi, kx=kx, jg=jg: e.tensor_tensor(
                out=xrl[kx][:, jg * 512:(jg + 1) * 512], in0=gt[gi][:],
                in1=xrl[kx][:, jg * 512:(jg + 1) * 512], op=ALU.add),
                waits=[(sV, m_og), (xl_sem, xl)])
            prev_comb = m_out
            if jg == 1:
                m_so = g_dma(dOS[kx], dout[tcg * 128:(tcg + 1) * 128, :], xrl[kx][:],
                             waits=[(sV, m_out)])
                xrl_war_v[kx] = m_out
                xrl_war_d[kx] = m_so

    fin_waits = [(dOS[0], dOS[0].n), (dOS[1], dOS[1].n),
                 (dHS[0], dHS[0].n), (dHS[1], dHS[1].n)]

    def fin(e, fin_waits=fin_waits):
        for s_, v_ in fin_waits:
            e.wait_ge(s_.h, v_)
    q("gpsimd", fin)

    es_psD.close()
    es_c.close()

    with nc.Block() as block:
        @block.sync
        def _(e):
            for fn in PL["sync"]:
                fn(e)

        @block.scalar
        def _(e):
            for fn in PL["scalar"]:
                fn(e)

        @block.vector
        def _(e):
            for fn in PL["vector"]:
                fn(e)

        @block.tensor
        def _(e):
            for fn in PL["tensor"]:
                fn(e)

        @block.gpsimd
        def _(e):
            for fn in PL["gpsimd"]:
                fn(e)

    return nc


def _pack_weights(i):
    f32 = np.float32
    gup = np.empty((NEXP, 128, 2048), dtype=f32)
    down = np.empty((NEXP, 128, 1024), dtype=f32)
    gate_w = np.asarray(i["L0_ffn_gate_w"], dtype=f32)
    up_w = np.asarray(i["L0_ffn_up_w"], dtype=f32)
    down_w = np.asarray(i["L0_ffn_down_w"], dtype=f32)
    for e in range(32):
        for gu, w in ((0, gate_w[e]), (1, up_w[e])):
            wT = w.T.reshape(8, 128, 128)
            gup[e, :, gu * 1024:(gu + 1) * 1024] = wT.transpose(1, 0, 2).reshape(128, 1024)
        down[e] = down_w[e].T
    sgT = np.asarray(i["L0_ffn_sgate_w"], dtype=f32).T
    suT = np.asarray(i["L0_ffn_sup_w"], dtype=f32).T
    sdT = np.asarray(i["L0_ffn_sdown_w"], dtype=f32).T
    for hc in range(4):
        for gu, wT_full in ((0, sgT), (1, suT)):
            wT = wT_full[:, hc * 128:(hc + 1) * 128].reshape(8, 128, 128)
            gup[32 + hc, :, gu * 1024:(gu + 1) * 1024] = \
                wT.transpose(1, 0, 2).reshape(128, 1024)
        down[32 + hc] = sdT[hc * 128:(hc + 1) * 128, :]
    routerT = np.ascontiguousarray(
        np.asarray(i["L0_ffn_router_w"], dtype=f32).T.reshape(8, 128, 32)
        .transpose(1, 0, 2).reshape(128, 256))
    rbias = np.ascontiguousarray(np.asarray(i["L0_ffn_router_bias"], dtype=f32).reshape(1, 32))
    out = {"gupT": np.ascontiguousarray(gup), "downT": np.ascontiguousarray(down),
           "routerT": routerT, "rbias": rbias}
    for d, pfx in (("f", "fwd"), ("b", "bwd")):
        Br = np.asarray(i[f"L0_attn_{pfx}_B_real"], dtype=f32).reshape(2, 128, 1024)
        Bi = np.asarray(i[f"L0_attn_{pfx}_B_imag"], dtype=f32).reshape(2, 128, 1024)
        out[f"B_{d}"] = np.ascontiguousarray(
            np.stack([Br, Bi]).transpose(2, 0, 1, 3).reshape(128, 4096))
        Cr = np.asarray(i[f"L0_attn_{pfx}_C_real"], dtype=f32).T.reshape(2, 128, 1024)
        Ci = np.asarray(i[f"L0_attn_{pfx}_C_imag"], dtype=f32).T.reshape(2, 128, 1024)
        out[f"CT_{d}"] = np.ascontiguousarray(
            np.stack([Cr, Ci]).transpose(2, 0, 1, 3).reshape(128, 4096))
        lr = np.asarray(i[f"L0_attn_{pfx}_log_real"], dtype=f32).reshape(2, 128)
        im = np.asarray(i[f"L0_attn_{pfx}_imag"], dtype=f32).reshape(2, 128)
        ld = np.asarray(i[f"L0_attn_{pfx}_log_Delta"], dtype=f32).reshape(2, 128)
        out[f"vec_{d}"] = np.ascontiguousarray(
            np.stack([lr, im, ld]).transpose(2, 0, 1).reshape(128, 6))
        out[f"D_{d}"] = np.ascontiguousarray(
            np.asarray(i[f"L0_attn_{pfx}_D"], dtype=f32).reshape(8, 128).T)
    return out


def kernel(**inputs):
    shared = _pack_weights(inputs)
    x = np.asarray(inputs["L0_attn_x"], dtype=np.float32)
    nc = build_program()
    in_maps = []
    for b in range(8):
        m = dict(shared)
        m["x"] = np.ascontiguousarray(x[b])
        in_maps.append(m)
    res = run_bass_kernel_spmd(nc, in_maps, core_ids=list(range(8)))
    return np.stack([r["out"] for r in res.results]).astype(np.float32)


# version 32
# speedup vs baseline: 1.0332x; 1.0268x over previous
import math
import contextlib
import numpy as np
import concourse.bass as bass
import concourse.mybir as mybir
from concourse.bass_utils import run_bass_kernel_spmd

F32 = mybir.dt.float32
BF16 = mybir.dt.bfloat16
I32 = mybir.dt.int32
AF = mybir.ActivationFunctionType
ALU = mybir.AluOpType
AX = mybir.AxisListType

T = 2048
DM = 1024
NEXP = 36
NTOK = T // 128
NDC = DM // 128
TWO_PI = 2.0 * math.pi


class Sem:
    def __init__(self, nc, es, name):
        self.h = es.enter_context(nc.semaphore(name))
        self.n = 0

    def bump(self, k=1):
        self.n += k
        return self.n


def _w(e, waits):
    for s, v in waits:
        if v > 0:
            e.wait_ge(s.h, v)


def build_program():
    nc = bass.Bass("TRN2", target_bir_lowering=False)
    es = contextlib.ExitStack()

    dx = nc.dram_tensor("x", [T, DM], F32, kind="ExternalInput").ap()
    dout = nc.dram_tensor("out", [T, DM], F32, kind="ExternalOutput").ap()
    dgup = nc.dram_tensor("gupT", [NEXP, 128, 2048], F32, kind="ExternalInput").ap()
    ddown = nc.dram_tensor("downT", [NEXP, 128, 1024], F32, kind="ExternalInput").ap()
    drout = nc.dram_tensor("routerT", [128, 256], F32, kind="ExternalInput").ap()
    dbias = nc.dram_tensor("rbias", [1, 32], F32, kind="ExternalInput").ap()
    dB = {d: nc.dram_tensor(f"B_{d}", [128, 4096], F32, kind="ExternalInput").ap() for d in "fb"}
    dCT = {d: nc.dram_tensor(f"CT_{d}", [128, 4096], F32, kind="ExternalInput").ap() for d in "fb"}
    dvec = {d: nc.dram_tensor(f"vec_{d}", [128, 6], F32, kind="ExternalInput").ap() for d in "fb"}
    dD = {d: nc.dram_tensor(f"D_{d}", [128, 8], F32, kind="ExternalInput").ap() for d in "fb"}
    hscr = nc.dram_tensor("hscr", [NEXP, 4, 128, 512], BF16, kind="Internal").ap()

    identf_d = nc.inline_tensor(np.eye(128, dtype=np.float32), name="identf_dram").ap()
    oh32_d = nc.inline_tensor(np.eye(32, dtype=np.float32), name="oh32f_dram").ap()
    onesf_d = nc.inline_tensor(np.ones((1, 128), dtype=np.float32), name="onesf_dram").ap()

    sb = lambda n, sh, dt: nc.alloc_sbuf_tensor(n, sh, dt).ap()
    xnT = sb("xnT", [128, NDC * T], BF16)
    xnTv = xnT.rearrange("p (dc t) -> p dc t", dc=NDC)
    xsT = {(d, ri): sb(f"xs_{d}{ri}", [128, 2 * T], BF16) for d in "fb" for ri in "ri"}
    CTb = {(d, ri): sb(f"CTb_{d}{ri}", [128, 2048], BF16) for d in "fb" for ri in "ri"}
    diagD = {d: sb(f"diagD_{d}", [128, 1024], BF16) for d in "fb"}
    routerTb = sb("routerTb", [128, 256], BF16)
    bias_bc = sb("bias_bc", [128, 32], F32)
    wTb = sb("wTb", [32, T], BF16)
    identf = sb("identf", [128, 128], F32)
    identb = sb("identb", [128, 128], BF16)
    oh32b = sb("oh32b", [32, 32], BF16)
    onesf = sb("onesf", [1, 128], F32)
    biasrow = sb("biasrow", [1, 32], F32)
    c_hpi = sb("c_hpi", [128, 1], F32)
    c_eps = sb("c_eps", [128, 1], F32)
    c_nh = sb("c_nh", [128, 1], F32)
    vt = {}
    for d in "fb":
        vt[d] = {k: sb(f"v_{d}_{k}", [128, 2], F32)
                 for k in ["delta", "elr", "ngd", "rr", "theta", "kf", "thr", "half",
                           "sh", "ch", "sinth", "costh", "lbr", "lbi", "l2m", "onem",
                           "lsr", "lsi", "dinv", "nr", "scr", "sci", "tmp"]}
        vt[d]["vecs"] = sb(f"vecs_{d}", [128, 6], F32)
        vt[d]["Dt"] = sb(f"Dt_{d}", [128, 8], F32)
        vt[d]["kint"] = sb(f"kint_{d}", [128, 2], I32)
        vt[d]["AB"] = sb(f"AB_{d}", [128, 44], F32)

    routerTb_lo = sb("routerTb_lo", [128, 256], BF16)
    xrl = [sb(f"xrl{i}", [128, 1024], F32) for i in range(2)]

    es_ab = contextlib.ExitStack()
    sbt = lambda st, n, sh, dt: st.enter_context(nc.sbuf_tensor(n, sh, dt)).ap()
    es_aonly = contextlib.ExitStack()
    sbr = lambda st, n, sh, dt: st.enter_context(nc.sbuf_tensor(n, sh, dt, side="right")).ap()
    stage = sbr(es_aonly, "stage", [128, 1024], F32)
    stage2 = sbr(es_aonly, "stage2", [128, 1024], F32)
    Bbar2 = sbr(es_aonly, "Bbar2", [128, 2048], BF16)
    xn_tok = [sbr(es_aonly, f"xn_tok{i}", [128, 1024], BF16) for i in range(2)]
    xnf_tok = sbr(es_aonly, "xnf_tok", [128, 1024], F32)
    xnlo_tok = [sbr(es_aonly, f"xnlo_tok{i}", [128, 1024], BF16) for i in range(2)]
    xnTlo = [sbr(es_aonly, f"xnTlo{i}", [128, 1024], BF16) for i in range(2)]
    routstage = sbr(es_aonly, "routstage", [128, 256], F32)
    r_lg = [sbr(es_aonly, f"r_lg{i}", [128, 32], F32) for i in range(2)]
    r_e = [sbr(es_aonly, f"r_e{i}", [128, 32], F32) for i in range(2)]
    r_cur = [sbr(es_aonly, f"r_cur{i}", [128, 32], F32) for i in range(2)]
    r_msk = [sbr(es_aonly, f"r_msk{i}", [128, 32], F32) for i in range(2)]
    r_wb = [sbr(es_aonly, f"r_wb{i}", [128, 32], BF16) for i in range(2)]
    r_m = [sbr(es_aonly, f"r_m{i}", [128, 1], F32) for i in range(2)]
    r_s1 = [sbr(es_aonly, f"r_s1{i}", [128, 1], F32) for i in range(2)]
    r_s2 = [sbr(es_aonly, f"r_s2{i}", [128, 1], F32) for i in range(2)]
    ln_bn = [sbr(es_aonly, f"ln_bn{i}", [128, 12], F32) for i in range(2)]
    ln_mv = [sbr(es_aonly, f"ln_mv{i}", [128, 2], F32) for i in range(2)]
    ln_rstd = [sbr(es_aonly, f"ln_rstd{i}", [128, 1], F32) for i in range(2)]
    Bbar1 = sbt(es_ab, "Bbar1", [128, 2048], BF16)
    BTb = {ri: sbt(es_ab, f"BTb_{ri}", [128, 2048], BF16) for ri in "ri"}
    BTbv = {ri: BTb[ri].rearrange("p (dc cc c) -> p dc cc c", dc=8, cc=2) for ri in "ri"}
    cosT = sbt(es_ab, "cosT", [128, T], F32)
    sinT = sbt(es_ab, "sinT", [128, T], F32)
    cmb = {ri: sbt(es_ab, f"cmb_{ri}", [128, T], F32) for ri in "ri"}
    bq = {ri: sbt(es_ab, f"bq_{ri}", [128, T], F32) for ri in "ri"}
    uu = {ri: sbt(es_ab, f"u_{ri}", [128, T], F32) for ri in "ri"}

    dM = Sem(nc, es, "dM")
    dX = [Sem(nc, es, f"dX{i}") for i in range(4)]
    dST = Sem(nc, es, "dST")
    dST2 = Sem(nc, es, "dST2")
    dBU = [Sem(nc, es, f"dBU{i}") for i in range(2)]
    dGU = [Sem(nc, es, f"dGU{i}") for i in range(2)]
    dDW = [Sem(nc, es, f"dDW{i}") for i in range(3)]
    dHL = [Sem(nc, es, f"dHL{i}") for i in range(2)]
    dHS = [Sem(nc, es, f"dHS{i}") for i in range(2)]
    dOS = [Sem(nc, es, f"dOS{i}") for i in range(4)]
    sP = Sem(nc, es, "sP")
    sV = Sem(nc, es, "sV")
    sA = Sem(nc, es, "sA")
    sG = Sem(nc, es, "sG")

    PL = {k: [] for k in ["sync", "scalar", "vector", "tensor", "gpsimd"]}

    def q(key, fn):
        PL[key].append(fn)

    def load(sem, out_ap, in_ap, waits=()):
        waits = tuple(waits)

        def f(e, sem=sem, out_ap=out_ap, in_ap=in_ap, waits=waits):
            _w(e, waits)
            e.dma_start(out=out_ap, in_=in_ap).then_inc(sem.h, 16)
        q("sync", f)
        return sem.bump(16)

    def g_dma(sem, out_ap, in_ap, waits=()):
        waits = tuple(waits)

        def f(e, sem=sem, out_ap=out_ap, in_ap=in_ap, waits=waits):
            _w(e, waits)
            e.dma_start(out=out_ap, in_=in_ap).then_inc(sem.h, 16)
        q("gpsimd", f)
        return sem.bump(16)

    def st_dma(sem, out_ap, in_ap, waits=()):
        waits = tuple(waits)

        def f(e, sem=sem, out_ap=out_ap, in_ap=in_ap, waits=waits):
            _w(e, waits)
            e.dma_start(out=out_ap, in_=in_ap).then_inc(sem.h, 16)
        q("scalar", f)
        return sem.bump(16)

    def _mk(key, sem):
        def op(fn, waits=(), inc=True, chain=False):
            waits = tuple(waits) + (((sem, sem.n),) if chain else ())

            def f(e, fn=fn, waits=waits):
                _w(e, waits)
                inst = fn(e)
                if inc:
                    inst.then_inc(sem.h, 1)
            q(key, f)
            return sem.bump() if inc else sem.n
        return op

    v_op = _mk("vector", sV)
    a_op = _mk("scalar", sA)
    g_op = _mk("gpsimd", sG)
    t_op = _mk("tensor", sP)

    es_psAB = contextlib.ExitStack()
    tp = es_psAB.enter_context(nc.psum_tensor("tp", [128, 4096], BF16)).ap()
    tpv = tp.rearrange("p (b c) -> p b c", b=4)
    tpv8 = tp.rearrange("p (b k c) -> p b k c", b=4, k=8)
    r_ps = [es_psAB.enter_context(nc.psum_tensor(f"r_ps{i}", [128, 32], F32)).ap()
            for i in range(2)]
    wt_ps = [es_psAB.enter_context(nc.psum_tensor(f"wt_ps{i}", [32, 128], BF16)).ap()
             for i in range(2)]

    load(dM, identf[:], identf_d)
    load(dM, onesf[:], onesf_d)
    load(dM, biasrow[:], dbias)
    load(dM, routstage[:], drout)
    for d in "fb":
        load(dM, vt[d]["vecs"][:], dvec[d])
        load(dM, vt[d]["Dt"][:], dD[d])

    m_hpi = v_op(lambda e: e.memset(c_hpi[:], math.pi / 2.0))
    m_eps = v_op(lambda e: e.memset(c_eps[:], 1e-5))
    m_nh = v_op(lambda e: e.memset(c_nh[:], -0.5))
    load(dM, r_lg[0][0:32, 0:32], oh32_d)
    m_misc = dM.n
    m_identb = v_op(lambda e: e.tensor_copy(out=identb[:], in_=identf[:]),
                    waits=[(dM, m_misc)])
    m_oh = v_op(lambda e: e.tensor_copy(out=oh32b[:], in_=r_lg[0][0:32, 0:32]),
                waits=[(dM, m_misc)])
    m_routb = g_op(lambda e: e.tensor_copy(out=routerTb[:], in_=routstage[:]),
                   waits=[(dM, m_misc)])
    m_routlo = g_op(lambda e: e.tensor_tensor(out=routerTb_lo[:], in0=routstage[:],
                                              in1=routerTb[:], op=ALU.subtract),
                    waits=[(sG, m_routb)])
    m_biasmm = t_op(lambda e: e.matmul(r_ps[0][:], onesf[:], biasrow[:],
                                       start=True, stop=True),
                    waits=[(dM, m_misc)])
    m_biasbc = v_op(lambda e: e.tensor_copy(out=bias_bc[:], in_=r_ps[0][:]),
                    waits=[(sP, m_biasmm)])

    INV_D = 1.0 / DM
    ln_v_marks = [0] * NTOK
    ln_tp_marks = [0] * NTOK
    ln_lo_evac = [0] * NTOK
    rmm_marks = [0] * NTOK
    m_wq_arr = [0] * NTOK
    ln_rstd_m = [0, 0]
    tp_evac = [0, 0]
    w_evac = [0, 0]
    m_e_par = [0, 0]
    m_wt_par = [0, 0]
    m_lg_par = [0, 0]
    norm_par = [0, 0]
    def ln_bn_part(i):
        p2 = i % 2
        buf = xrl[p2]
        war = ([(sV, ln_v_marks[i - 2])] if i >= 2 else [])
        m_ld = load(dX[p2], buf[:], dx[i * 128:(i + 1) * 128, :], waits=war)
        v_op(lambda e, b=buf, p2=p2: e.bn_stats(out=ln_bn[p2][:, 0:6], in_=b[:, 0:512]),
             waits=[(dX[p2], m_ld), (sV, norm_par[p2])])
        v_op(lambda e, b=buf, p2=p2: e.bn_stats(out=ln_bn[p2][:, 6:12], in_=b[:, 512:1024]),
             chain=True)
        v_op(lambda e, p2=p2: e.bn_aggr(out=ln_mv[p2][:], in_=ln_bn[p2][:]), chain=True)
        m_var = sV.n
        a_op(lambda e, p2=p2: e.activation(out=ln_rstd[p2][:],
                                           in_=ln_mv[p2][:, 1:2], func=AF.Ln,
                                           bias=c_eps[:]),
             waits=[(sV, m_var), (sV, m_nh)])
        ln_rstd_m[p2] = a_op(lambda e, p2=p2: e.activation(out=ln_rstd[p2][:],
                                                           in_=ln_rstd[p2][:],
                                                           func=AF.Exp, scale=c_nh[:]),
                             chain=True)

    def ln_xnf_part(i):
        p2 = i % 2
        buf = xrl[p2]
        v_op(lambda e, b=buf, p2=p2: e.tensor_scalar(
            out=xnf_tok[:], in0=b[:], scalar1=ln_mv[p2][:, 0:1], scalar2=ln_rstd[p2][:],
            op0=ALU.subtract, op1=ALU.mult),
            waits=[(sA, ln_rstd_m[p2])] + ([(sP, ln_tp_marks[i - 2])] if i >= 2 else []),
            chain=True)
        v_op(lambda e, p2=p2: e.tensor_copy(out=xn_tok[p2][:], in_=xnf_tok[:]), chain=True)
        m_norm = v_op(lambda e, p2=p2: e.tensor_tensor(out=xnlo_tok[p2][:], in0=xnf_tok[:],
                                                       in1=xn_tok[p2][:], op=ALU.subtract),
                      chain=True)
        ln_v_marks[i] = m_norm
        norm_par[p2] = m_norm

    def ln_tp_part(i):
        p2 = i % 2
        m_norm = ln_v_marks[i]
        lo_v = xnTlo[p2].rearrange("p (dc t) -> p dc t", dc=8)
        war_lo = [(sP, rmm_marks[i - 2])] if i >= 2 else []
        for half in range(2):
            src_t = xn_tok[p2] if half == 0 else xnlo_tok[p2]
            def tfn(e, src_t=src_t, half=half):
                last = None
                for k in range(8):
                    last = e.transpose(
                        tp[:, half * 1024 + k * 128: half * 1024 + (k + 1) * 128],
                        src_t[:, k * 128:(k + 1) * 128], identb[:])
                return last
            m_tp = t_op(tfn, waits=[(sV, m_norm), (sV, m_identb), (sA, tp_evac[half])]
                             + (war_lo if half == 1 else []))
            ln_tp_marks[i] = m_tp
            def efn(e, i=i, half=half, lo_v=lo_v):
                if half == 1:
                    return e.copy(out=lo_v[:, 0:8, :], in_=tpv8[:, 1, 0:8, 0:128])
                return e.copy(out=xnTv[:, 0:8, i * 128:(i + 1) * 128],
                              in_=tpv8[:, 0, 0:8, 0:128])
            tp_evac[half] = a_op(efn, waits=[(sP, m_tp)])
        ln_lo_evac[i] = tp_evac[1]


    def router_mm_part(i):
        p2 = i % 2
        def rmm(e, i=i, p2=p2):
            last = None
            lo_t = xnTlo[p2]
            for dc in range(NDC):
                last = e.matmul(r_ps[p2][:],
                                xnT[:, dc * T + i * 128: dc * T + i * 128 + 128],
                                routerTb[:, dc * 32:(dc + 1) * 32],
                                start=(dc == 0), stop=False)
                last = e.matmul(r_ps[p2][:], lo_t[:, dc * 128:(dc + 1) * 128],
                                routerTb[:, dc * 32:(dc + 1) * 32],
                                start=False, stop=False)
                last = e.matmul(r_ps[p2][:],
                                xnT[:, dc * T + i * 128: dc * T + i * 128 + 128],
                                routerTb_lo[:, dc * 32:(dc + 1) * 32],
                                start=False, stop=(dc == NDC - 1))
            return last
        rmm_marks[i] = t_op(rmm, waits=[(sA, ln_lo_evac[i]), (sG, m_routlo),
                                        (sV, m_lg_par[p2])])

    def router_head_part(i):
        p2 = i % 2
        m_lg = v_op(lambda e, p2=p2: e.tensor_tensor(out=r_lg[p2][:], in0=r_ps[p2][:],
                                                     in1=bias_bc[:], op=ALU.add),
                    waits=[(sP, rmm_marks[i]), (sV, m_biasbc), (sA, m_e_par[p2])])
        m_lg_par[p2] = m_lg
        m_mx = v_op(lambda e, p2=p2: e.reduce_max(out=r_m[p2][:], in_=r_lg[p2][:],
                                                  axis=AX.X, negate=True), chain=True)
        m_e = a_op(lambda e, p2=p2: e.activation(out=r_e[p2][:], in_=r_lg[p2][:],
                                                 func=AF.Exp, bias=r_m[p2][:]),
                   waits=[(sV, m_mx)])
        m_e_par[p2] = m_e

    def router_tail_part(i):
        p2 = i % 2
        v_op(lambda e, p2=p2: e.tensor_copy(out=r_cur[p2][:], in_=r_e[p2][:]),
             waits=[(sA, m_e_par[p2])])
        for k in range(4):
            v_op(lambda e, p2=p2: e.reduce_max(out=r_m[p2][:], in_=r_cur[p2][:], axis=AX.X),
                 chain=True)
            if k < 3:
                v_op(lambda e, p2=p2: e.tensor_scalar(out=r_msk[p2][:], in0=r_cur[p2][:],
                                                      scalar1=r_m[p2][:], scalar2=None,
                                                      op0=ALU.is_ge), chain=True)
                v_op(lambda e, p2=p2: e.scalar_tensor_tensor(out=r_cur[p2][:],
                                                             in0=r_msk[p2][:],
                                                             scalar=-1e30, in1=r_cur[p2][:],
                                                             op0=ALU.mult, op1=ALU.add),
                     chain=True)
        v_op(lambda e, p2=p2: e.tensor_scalar(out=r_msk[p2][:], in0=r_e[p2][:],
                                              scalar1=r_m[p2][:], scalar2=None,
                                              op0=ALU.is_ge), chain=True)
        v_op(lambda e, p2=p2: e.tensor_tensor(out=r_cur[p2][:], in0=r_e[p2][:],
                                              in1=r_msk[p2][:], op=ALU.mult), chain=True)
        v_op(lambda e, p2=p2: e.reduce_sum(out=r_s1[p2][:], in_=r_cur[p2][:], axis=AX.X),
             chain=True)
        v_op(lambda e, p2=p2: e.tensor_scalar_max(out=r_s1[p2][:], in0=r_s1[p2][:],
                                                  scalar1=1e-12), chain=True)
        v_op(lambda e, p2=p2: e.reciprocal(out=r_s2[p2][:], in_=r_s1[p2][:]), chain=True)
        m_wq_arr[i] = v_op(lambda e, p2=p2: e.tensor_scalar(out=r_wb[p2][:],
                                                            in0=r_cur[p2][:],
                                                            scalar1=r_s2[p2][:], scalar2=None,
                                                            op0=ALU.mult),
                           waits=[(sP, m_wt_par[p2])], chain=True)

    def wt_part(i):
        p2 = i % 2
        m_wt = t_op(lambda e, p2=p2: e.transpose(wt_ps[p2][:], r_wb[p2][:], identb[:]),
                    waits=[(sV, m_wq_arr[i]), (sA, w_evac[p2])])
        m_wt_par[p2] = m_wt
        w_evac[p2] = a_op(lambda e, i=i, p2=p2: e.copy(out=wTb[:, i * 128:(i + 1) * 128],
                                                       in_=wt_ps[p2][:]),
                          waits=[(sP, m_wt)])

    prep_parts = {}

    def prep_act0(d):
        V = vt[d]
        lr, ldl = V["vecs"][:, 0:2], V["vecs"][:, 4:6]
        wv = [(dM, m_misc)]
        a_op(lambda e: e.activation(out=V["delta"][:], in_=ldl, func=AF.Exp), waits=wv)
        mA2 = a_op(lambda e: e.activation(out=V["elr"][:], in_=lr, func=AF.Exp), waits=wv)
        prep_parts[d] = {"mA2": mA2}

    def prep_v1(d):
        def vc(fn, waits=(), inc=True):
            return v_op(fn, waits=waits, inc=inc, chain=True)
        V = vt[d]
        im = V["vecs"][:, 2:4]
        mA2 = prep_parts[d]["mA2"]
        vc(lambda e: e.tensor_tensor(out=V["ngd"][:], in0=V["elr"][:], in1=V["delta"][:],
                                       op=ALU.mult), waits=[(sA, mA2)])
        mV1 = vc(lambda e: e.tensor_scalar(out=V["ngd"][:], in0=V["ngd"][:], scalar1=-1.0,
                                             scalar2=None, op0=ALU.mult))
        vc(lambda e: e.tensor_tensor(out=V["theta"][:], in0=im, in1=V["delta"][:],
                                       op=ALU.mult))
        MAGIC = 12582912.0
        vc(lambda e: e.tensor_scalar(out=V["kf"][:], in0=V["theta"][:],
                                     scalar1=1.0 / TWO_PI, scalar2=MAGIC,
                                     op0=ALU.mult, op1=ALU.add))
        vc(lambda e: e.tensor_scalar(out=V["kf"][:], in0=V["kf"][:], scalar1=-MAGIC,
                                     scalar2=None, op0=ALU.add))
        vc(lambda e: e.scalar_tensor_tensor(out=V["thr"][:], in0=V["kf"][:], scalar=-TWO_PI,
                                            in1=V["theta"][:], op0=ALU.mult, op1=ALU.add))
        m_half = vc(lambda e: e.tensor_scalar(out=V["half"][:], in0=V["thr"][:], scalar1=0.5,
                                                scalar2=None, op0=ALU.mult))
        prep_parts[d]["mV1"] = mV1
        prep_parts[d]["m_half"] = m_half

    def prep_act1(d):
        V = vt[d]
        m_r = a_op(lambda e: e.activation(out=V["rr"][:], in_=V["ngd"][:], func=AF.Exp),
                   waits=[(sV, prep_parts[d]["mV1"])])
        a_op(lambda e: e.activation(out=V["sh"][:], in_=V["half"][:], func=AF.Sin),
             waits=[(sV, prep_parts[d]["m_half"])])
        m_ch = a_op(lambda e: e.activation(out=V["ch"][:], in_=V["half"][:], func=AF.Sin,
                                           bias=c_hpi[:]), waits=[(sV, m_hpi)])
        prep_parts[d]["m_r"] = m_r
        prep_parts[d]["m_ch"] = m_ch

    def prep_v2(d):
        def vc(fn, waits=(), inc=True):
            return v_op(fn, waits=waits, inc=inc, chain=True)
        V = vt[d]
        im = V["vecs"][:, 2:4]
        m_ch = prep_parts[d]["m_ch"]
        m_r = prep_parts[d]["m_r"]
        vc(lambda e: e.tensor_tensor(out=V["sinth"][:], in0=V["sh"][:], in1=V["ch"][:],
                                       op=ALU.mult), waits=[(sA, m_ch)])
        vc(lambda e: e.tensor_scalar(out=V["sinth"][:], in0=V["sinth"][:], scalar1=2.0,
                                       scalar2=None, op0=ALU.mult))
        vc(lambda e: e.tensor_tensor(out=V["costh"][:], in0=V["sh"][:], in1=V["sh"][:],
                                       op=ALU.mult))
        m_cs = vc(lambda e: e.tensor_scalar(out=V["costh"][:], in0=V["costh"][:],
                                              scalar1=-2.0, scalar2=1.0,
                                              op0=ALU.mult, op1=ALU.add))
        vc(lambda e: e.tensor_tensor(out=V["lbr"][:], in0=V["rr"][:], in1=V["costh"][:],
                                       op=ALU.mult), waits=[(sA, m_r)])
        vc(lambda e: e.tensor_tensor(out=V["lbi"][:], in0=V["rr"][:], in1=V["sinth"][:],
                                       op=ALU.mult))
        vc(lambda e: e.tensor_tensor(out=V["l2m"][:], in0=V["elr"][:], in1=V["elr"][:],
                                       op=ALU.mult))
        vc(lambda e: e.tensor_tensor(out=V["tmp"][:], in0=im, in1=im, op=ALU.mult))
        vc(lambda e: e.tensor_tensor(out=V["l2m"][:], in0=V["l2m"][:], in1=V["tmp"][:],
                                       op=ALU.add))
        vc(lambda e: e.tensor_scalar(out=V["l2m"][:], in0=V["l2m"][:], scalar1=1e-12,
                                       scalar2=None, op0=ALU.is_lt))
        vc(lambda e: e.tensor_scalar(out=V["onem"][:], in0=V["l2m"][:], scalar1=-1.0,
                                       scalar2=1.0, op0=ALU.mult, op1=ALU.add))
        vc(lambda e: e.tensor_scalar(out=V["lsr"][:], in0=V["elr"][:], scalar1=-1.0,
                                       scalar2=None, op0=ALU.mult))
        vc(lambda e: e.tensor_tensor(out=V["lsr"][:], in0=V["lsr"][:], in1=V["onem"][:],
                                       op=ALU.mult))
        vc(lambda e: e.scalar_tensor_tensor(out=V["lsr"][:], in0=V["l2m"][:], scalar=1e-6,
                                              in1=V["lsr"][:], op0=ALU.mult, op1=ALU.add))
        vc(lambda e: e.tensor_tensor(out=V["lsi"][:], in0=im, in1=V["onem"][:],
                                       op=ALU.mult))
        vc(lambda e: e.tensor_tensor(out=V["dinv"][:], in0=V["lsr"][:], in1=V["lsr"][:],
                                       op=ALU.mult))
        vc(lambda e: e.tensor_tensor(out=V["tmp"][:], in0=V["lsi"][:], in1=V["lsi"][:],
                                       op=ALU.mult))
        vc(lambda e: e.tensor_tensor(out=V["dinv"][:], in0=V["dinv"][:], in1=V["tmp"][:],
                                       op=ALU.add))
        vc(lambda e: e.reciprocal(out=V["dinv"][:], in_=V["dinv"][:]))
        vc(lambda e: e.tensor_scalar(out=V["nr"][:], in0=V["lbr"][:], scalar1=-1.0,
                                       scalar2=None, op0=ALU.add))
        vc(lambda e: e.tensor_tensor(out=V["scr"][:], in0=V["nr"][:], in1=V["lsr"][:],
                                       op=ALU.mult))
        vc(lambda e: e.tensor_tensor(out=V["tmp"][:], in0=V["lbi"][:], in1=V["lsi"][:],
                                       op=ALU.mult))
        vc(lambda e: e.tensor_tensor(out=V["scr"][:], in0=V["scr"][:], in1=V["tmp"][:],
                                       op=ALU.add))
        vc(lambda e: e.tensor_tensor(out=V["scr"][:], in0=V["scr"][:], in1=V["dinv"][:],
                                       op=ALU.mult))
        vc(lambda e: e.tensor_tensor(out=V["sci"][:], in0=V["lbi"][:], in1=V["lsr"][:],
                                       op=ALU.mult))
        vc(lambda e: e.tensor_tensor(out=V["tmp"][:], in0=V["nr"][:], in1=V["lsi"][:],
                                       op=ALU.mult))
        vc(lambda e: e.tensor_tensor(out=V["sci"][:], in0=V["sci"][:], in1=V["tmp"][:],
                                       op=ALU.subtract))
        m_sc = vc(lambda e: e.tensor_tensor(out=V["sci"][:], in0=V["sci"][:],
                                              in1=V["dinv"][:], op=ALU.mult))
        prep_parts[d]["m_sc"] = m_sc
        prep_parts[d]["m_cs"] = m_cs

    def prep_v2b(d, half):
        def vc(fn, waits=(), inc=True):
            return v_op(fn, waits=waits, inc=inc, chain=True)
        V = vt[d]
        if half == 0:
            m_cs = prep_parts[d]["m_cs"]
            vc(lambda e: e.tensor_copy(out=V["AB"][:, 0:2], in_=V["costh"][:]),
                 waits=[(sV, m_cs)])
            vc(lambda e: e.tensor_copy(out=V["AB"][:, 2:4], in_=V["sinth"][:]))
        for k in (range(5) if half == 0 else range(5, 10)):
            a0, b0 = V["AB"][:, 4 * k:4 * k + 2], V["AB"][:, 4 * k + 2:4 * k + 4]
            a1 = V["AB"][:, 4 * (k + 1):4 * (k + 1) + 2]
            b1 = V["AB"][:, 4 * (k + 1) + 2:4 * (k + 1) + 4]
            vc(lambda e, a0=a0, a1=a1: e.tensor_tensor(out=a1, in0=a0, in1=a0, op=ALU.mult))
            vc(lambda e, b0=b0, b1=b1: e.tensor_tensor(out=b1, in0=b0, in1=b0, op=ALU.mult))
            vc(lambda e, a1=a1, b1=b1: e.tensor_tensor(out=a1, in0=a1, in1=b1,
                                                         op=ALU.subtract))
            vc(lambda e, a0=a0, b0=b0, b1=b1: e.tensor_tensor(out=b1, in0=a0, in1=b0,
                                                                op=ALU.mult))
            vc(lambda e, b1=b1: e.tensor_scalar(out=b1, in0=b1, scalar1=2.0, scalar2=None,
                                                  op0=ALU.mult))
        if half == 0:
            return
        m_ab = sV.n
        for jc in range(NDC):
            vc(lambda e, d=d, jc=jc: e.tensor_scalar(
                out=diagD[d][:, jc * 128:(jc + 1) * 128], in0=identf[:],
                scalar1=vt[d]["Dt"][:, jc:jc + 1], scalar2=None, op0=ALU.mult),
                waits=[(dM, m_misc)])
        prep_marks[d] = (prep_parts[d]["m_sc"], m_ab, prep_parts[d]["m_r"])

    prep_marks = {}

    m_ct_cast = {}
    ct_quarters = [(d, ri_i, hf) for d in "fb" for ri_i in range(2) for hf in range(2)]
    ct_load_marks = {}
    ct_cast_marks = {}
    stageb = [stage, stage2]
    dSTp = [dST, dST2]

    def ct_loads(ks):
        for k in ks:
            d, ri_i, hf = ct_quarters[k]
            war = [(sV, ct_cast_marks[k - 2])] if k - 2 in ct_cast_marks else []
            ct_load_marks[k] = g_dma(
                dSTp[k % 2], stageb[k % 2][:],
                dCT[d].rearrange("p (r h x) -> p r h x", r=2, h=2)[:, ri_i, hf, :],
                waits=war)

    def ct_casts(ks):
        for k in ks:
            d, ri_i, hf = ct_quarters[k]
            ri = "ri"[ri_i]
            sc = -1.0 if ri == "i" else 1.0
            ct_cast_marks[k] = v_op(lambda e, d=d, ri=ri, sc=sc, hf=hf, k=k: e.tensor_scalar(
                out=CTb[(d, ri)][:, hf * 1024:(hf + 1) * 1024], in0=stageb[k % 2][:],
                scalar1=sc, scalar2=None, op0=ALU.mult),
                waits=[(dSTp[k % 2], ct_load_marks[k])])
            m_ct_cast[(d, ri)] = ct_cast_marks[k]

    tr_evac = sA.n
    bb_state = {"cmb_v": 0, "cmb_g": 0, "tr": sA.n}

    def bbar_products_f():
        V = vt["f"]
        m_sc = prep_marks["f"][0]
        m_lbr = load(dBU[0], cmb["r"][:], dB["f"].rearrange("p (r x) -> p r x", r=2)[:, 0, :])
        m_lbi = load(dBU[1], cmb["i"][:], dB["f"].rearrange("p (r x) -> p r x", r=2)[:, 1, :])
        for ri in "ri":
            dst = Bbar1 if ri == "r" else Bbar2
            scr = bq["r"] if ri == "r" else uu["r"]
            for cc in range(2):
                sl = slice(cc * 1024, (cc + 1) * 1024)
                scr_c = V["scr"][:, cc:cc + 1]
                sci_c = V["sci"][:, cc:cc + 1]
                oth = "i" if ri == "r" else "r"
                sgn = ALU.subtract if ri == "r" else ALU.add
                v_op(lambda e, sl=sl, sci_c=sci_c, oth=oth, scr=scr: e.tensor_scalar(
                    out=scr[:, sl], in0=cmb[oth][:, sl], scalar1=sci_c, scalar2=None,
                    op0=ALU.mult),
                    waits=[(dBU[1], m_lbi), (dBU[0], m_lbr), (sV, m_sc)])
                v_op(lambda e, sl=sl, scr_c=scr_c, ri=ri, sgn=sgn, dst=dst, scr=scr:
                     e.scalar_tensor_tensor(
                    out=dst[:, sl], in0=cmb[ri][:, sl], scalar=scr_c, in1=scr[:, sl],
                    op0=ALU.mult, op1=sgn), chain=True)
        bb_state["m_bbar_f"] = sV.n
        bb_state["cmb_v"] = sV.n

    def bbar_transposes_f():
        m_bbar = bb_state["m_bbar_f"]
        tr = bb_state["tr"]
        for ri in "ri":
            src = Bbar1 if ri == "r" else Bbar2
            for cc in range(2):
                for half in range(2):
                    def tfn(e, cc=cc, half=half, src=src):
                        last = None
                        for k in range(4):
                            dc = half * 4 + k
                            last = e.transpose(
                                tp[:, k * 1024:k * 1024 + 128],
                                src[:, cc * 1024 + dc * 128:
                                    cc * 1024 + dc * 128 + 128],
                                identb[:])
                        return last
                    m_tp = t_op(tfn, waits=[(sV, m_bbar), (sA, tr),
                                            (sA, tp_evac[0]), (sA, tp_evac[1])])
                    def efn(e, ri=ri, cc=cc, half=half):
                        return e.copy(out=BTbv[ri][:, half * 4:(half + 1) * 4, cc, :],
                                      in_=tpv[:, 0:4, 0:128])
                    tr = a_op(efn, waits=[(sP, m_tp)])
                    tp_evac[0] = tr
                    tp_evac[1] = tr
        bb_state["tr"] = tr
        return tr

    def plan_bbar(d, use_tp=True):
        V = vt[d]
        m_sc, m_ab, m_r = prep_marks[d]
        m_lbr = load(dBU[0], cmb["r"][:], dB[d].rearrange("p (r x) -> p r x", r=2)[:, 0, :],
                     waits=[(sV, bb_state["cmb_v"]), (sG, bb_state["cmb_g"])])
        m_lbi = load(dBU[1], cmb["i"][:], dB[d].rearrange("p (r x) -> p r x", r=2)[:, 1, :],
                     waits=[(sV, bb_state["cmb_v"]), (sG, bb_state["cmb_g"])])
        m_tp_last = 0
        tr = bb_state["tr"]
        for ri in "ri":
            for cc in range(2):
                sl = slice(cc * 1024, (cc + 1) * 1024)
                scr_c = V["scr"][:, cc:cc + 1]
                sci_c = V["sci"][:, cc:cc + 1]
                oth = "i" if ri == "r" else "r"
                sgn = ALU.subtract if ri == "r" else ALU.add
                v_op(lambda e, sl=sl, sci_c=sci_c, oth=oth: e.tensor_scalar(
                    out=bq["r"][:, sl], in0=cmb[oth][:, sl], scalar1=sci_c, scalar2=None,
                    op0=ALU.mult),
                    waits=[(dBU[1], m_lbi), (dBU[0], m_lbr), (sV, m_sc),
                           (sP, m_tp_last)])
                v_op(lambda e, sl=sl, scr_c=scr_c, ri=ri, sgn=sgn: e.scalar_tensor_tensor(
                    out=Bbar1[:, sl], in0=cmb[ri][:, sl], scalar=scr_c, in1=bq["r"][:, sl],
                    op0=ALU.mult, op1=sgn), chain=True)
            m_bbar = sV.n
            if use_tp:
                for cc in range(2):
                    for half in range(2):
                        def tfn(e, cc=cc, half=half):
                            last = None
                            for k in range(4):
                                dc = half * 4 + k
                                last = e.transpose(
                                    tp[:, k * 1024:k * 1024 + 128],
                                    Bbar1[:, cc * 1024 + dc * 128:
                                          cc * 1024 + dc * 128 + 128],
                                    identb[:])
                            return last
                        m_tp = t_op(tfn, waits=[(sV, m_bbar), (sA, tr),
                                                (sA, tp_evac[0]), (sA, tp_evac[1])])
                        m_tp_last = m_tp
                        def efn(e, ri=ri, cc=cc, half=half):
                            return e.copy(out=BTbv[ri][:, half * 4:(half + 1) * 4, cc, :],
                                          in_=tpv[:, 0:4, 0:128])
                        tr = a_op(efn, waits=[(sP, m_tp)])
                        tp_evac[0] = tr
                        tp_evac[1] = tr
            else:
                bviews = [bu_ps[i].bitcast(BF16) for i in range(2)]
                for cc in range(2):
                    for half in range(4):
                        def tfn(e, cc=cc, half=half):
                            last = None
                            for k in range(2):
                                dc = half * 2 + k
                                last = e.transpose(
                                    bviews[k][:, 0:128],
                                    Bbar1[:, cc * 1024 + dc * 128:
                                          cc * 1024 + dc * 128 + 128],
                                    identb[:])
                            return last
                        m_tp = t_op(tfn, waits=[(sV, m_bbar), (sA, tr),
                                                (sV, prod_done[0]),
                                                (sV, prod_done[1])])
                        m_tp_last = m_tp
                        def efn(e, ri=ri, cc=cc, half=half):
                            e.copy(out=BTbv[ri][:, 2 * half, cc, :],
                                   in_=bviews[0][:, 0:128])
                            return e.copy(out=BTbv[ri][:, 2 * half + 1, cc, :],
                                          in_=bviews[1][:, 0:128])
                        tr = a_op(efn, waits=[(sP, m_tp)])
        bb_state["cmb_v"] = sV.n
        bb_state["tr"] = tr
        return tr


    prep_act0("f")
    prep_act0("b")
    ln_bn_part(0)
    ln_xnf_part(0)
    ln_bn_part(1)
    ln_xnf_part(1)
    ln_tp_part(0)
    m_btb = None
    m_A_v = None
    for i in range(NTOK):
        if i + 2 < NTOK:
            ln_bn_part(i + 2)
        router_mm_part(i)
        router_head_part(i)
        if i + 1 < NTOK:
            ln_tp_part(i + 1)
        if i >= 1:
            wt_part(i - 1)
        router_tail_part(i)
        if i + 2 < NTOK:
            ln_xnf_part(i + 2)
        if i == 0:
            prep_v1("f")
        elif i == 1:
            prep_act1("f")
            prep_v1("b")
        elif i == 2:
            prep_v2("f")
            prep_act1("b")
        elif i == 3:
            prep_v2b("f", 0)
        elif i == 4:
            prep_v2b("f", 1)
            prep_v2("b")
            ct_loads([0, 1])
        elif i == 5:
            prep_v2b("b", 0)
            ct_casts([0, 1])
            ct_loads([2, 3])
        elif i == 6:
            prep_v2b("b", 1)
            ct_casts([2, 3])
            ct_loads([4, 5])
        elif i == 7:
            ct_casts([4, 5])
            ct_loads([6, 7])
        elif i == 8:
            ct_casts([6, 7])
            bbar_products_f()
        elif i == 10:
            m_btb = bbar_transposes_f()
    wt_part(NTOK - 1)
    m_A_v = sV.n
    m_xnT_a = sA.n
    m_w_done = max(w_evac)




    es_aonly.close()
    m_aonly = [(sP, sP.n), (sV, sV.n), (sA, sA.n), (sG, sG.n)]



    es_psAB.close()
    es_psBH = contextlib.ExitStack()
    bu_ps = [es_psBH.enter_context(nc.psum_tensor(f"bu{i}", [128, 512], F32)).ap()
             for i in range(3)]
    g_ps = [es_psBH.enter_context(nc.psum_tensor(f"g_ps{i}", [128, 512], F32)).ap()
            for i in range(2)]
    u_ps = [es_psBH.enter_context(nc.psum_tensor(f"u_ps{i}", [128, 512], F32)).ap()
            for i in range(2)]
    w_ps = [es_psBH.enter_context(nc.psum_tensor("w_ps0", [128, 512], F32)).ap()]

    es_c1 = contextlib.ExitStack()
    gupstage = [sbr(es_c1, f"gupstage{i}", [128, 1024], F32) for i in range(2)]
    gupb = [sbr(es_c1, f"gupb{i}", [128, 2048], BF16) for i in range(2)]
    hstage = [sbr(es_c1, f"hstage{i}", [128, 512], BF16) for i in range(2)]
    t1s = [sbr(es_c1, f"t1s{i}", [128, 512], F32) for i in range(2)]

    gup_cast = [0] * NEXP
    gup_pe_done = [0] * NEXP
    h_silu = {}
    h_t2 = {}
    h_h = {}
    h_w = {}
    h_store = {}
    hstage_war = [0, 0]
    gup_halfcast = []

    def plan_gup_load(e_i):
        for hf in range(2):
            st_i = (2 * e_i + hf) % 2
            idx = 2 * e_i + hf
            war = [(sG, gup_halfcast[idx - 2])] if idx >= 2 else list(m_aonly)
            m_l = load(dGU[st_i], gupstage[st_i][:],
                       dgup[e_i].rearrange("p (h x) -> p h x", h=2)[:, hf, :], waits=war)
            wv = [(dGU[st_i], m_l)]
            if e_i >= 2:
                wv.append((sP, gup_pe_done[e_i - 2]))
            m_c = g_op(lambda e, st_i=st_i, e_i=e_i, hf=hf: e.tensor_copy(
                out=gupb[e_i % 2][:, hf * 1024:(hf + 1) * 1024], in_=gupstage[st_i][:]),
                waits=wv)
            gup_halfcast.append(m_c)
        gup_cast[e_i] = gup_halfcast[-1]

    plan_gup_load(0)
    plan_gup_load(1)
    h_state = {"n": 0}

    def emit_h_units(count):
        for _ in range(count):
            idx = h_state["n"]
            if idx >= NEXP * 4:
                return
            h_state["n"] += 1
            e_i, th = idx // 4, idx % 4
            is_moe = e_i < 32
            k2 = th % 2
            pg, pu, pw = g_ps[k2], u_ps[k2], w_ps[0]
            war_g = [(sA, h_silu.get(idx - 2, 0))]
            war_u = [(sV, h_t2.get(idx - 2, 0))]
            war_w = [(sV, h_h.get(idx - 1, 0))]
            if idx == 0:
                war_g += [(sA, m_btb), (sA, m_w_done), (sV, m_A_v)]
            def gmm(e, e_i=e_i, th=th, pg=pg):
                last = None
                for kc in range(NDC):
                    last = e.matmul(pg[:], gupb[e_i % 2][:, kc * 128:(kc + 1) * 128],
                                    xnT[:, kc * T + th * 512: kc * T + th * 512 + 512],
                                    start=(kc == 0), stop=(kc == NDC - 1))
                return last
            t_op(gmm, waits=[(sG, gup_cast[e_i]), (sA, m_xnT_a)] + war_g, inc=False)
            def umm(e, e_i=e_i, th=th, pu=pu):
                last = None
                for kc in range(NDC):
                    last = e.matmul(pu[:],
                                    gupb[e_i % 2][:, 1024 + kc * 128: 1024 + (kc + 1) * 128],
                                    xnT[:, kc * T + th * 512: kc * T + th * 512 + 512],
                                    start=(kc == 0), stop=(kc == NDC - 1))
                return last
            m_u = t_op(umm, waits=war_u)
            if is_moe:
                m_wb = t_op(lambda e, e_i=e_i, th=th, pw=pw: e.matmul(
                    pw[:], oh32b[:, e_i:e_i + 1].broadcast_to([32, 128]),
                    wTb[:, th * 512:(th + 1) * 512], start=True, stop=True),
                    waits=[(sA, m_w_done), (sV, m_oh)] + war_w)
            else:
                m_wb = 0
            m_si = a_op(lambda e, pg=pg, k=idx % 2: e.activation(
                out=t1s[k][:], in_=pg[:], func=AF.Silu),
                waits=[(sP, m_u), (sV, h_t2.get(idx - 2, 0))])
            h_silu[idx] = m_si
            m_t2 = v_op(lambda e, pu=pu, k=idx % 2: e.tensor_tensor(
                out=t1s[k][:], in0=t1s[k][:], in1=pu[:], op=ALU.mult),
                waits=[(sA, m_si)])
            h_t2[idx] = m_t2
            if is_moe:
                m_h = v_op(lambda e, pw=pw, k=idx % 2: e.tensor_tensor(
                    out=hstage[k][:], in0=t1s[k][:], in1=pw[:], op=ALU.mult),
                    waits=[(sP, m_wb), (dHS[idx % 2], hstage_war[idx % 2])])
            else:
                m_h = v_op(lambda e, k=idx % 2: e.tensor_copy(
                    out=hstage[k][:], in_=t1s[k][:]),
                    waits=[(dHS[idx % 2], hstage_war[idx % 2])])
            h_h[idx] = m_h
            h_w[idx] = m_h
            m_st = st_dma(dHS[idx % 2], hscr[e_i, th], hstage[idx % 2][:],
                          waits=[(sV, m_h)])
            h_store[idx] = m_st
            hstage_war[idx % 2] = m_st
            if th == 3:
                gup_pe_done[e_i] = sP.n
                if e_i + 2 < NEXP:
                    plan_gup_load(e_i + 2)

    m_xs_all = 0
    last_scan_read_v = 0
    last_pool_rw = 0
    prod_done = [0, 0, 0]
    first_bu = True
    cur_btb = m_btb
    for d in "fb":
        V = vt[d]
        m_sc, m_ab, m_r = prep_marks[d]
        rev = (d == "b")
        if d == "b":
            bb_state["cmb_v"] = last_scan_read_v
            bb_state["cmb_g"] = last_pool_rw
            cur_btb = plan_bbar("b", use_tp=False)
        for cc in range(2):
            v_op(lambda e: e.memset(cosT[:, 0:1], 1.0),
                 waits=[(sV, last_scan_read_v), (sG, last_pool_rw)])
            v_op(lambda e: e.memset(sinT[:, 0:1], 0.0))
            for k in range(11):
                m = 1 << k
                mm = min(m, T - m)
                ch_ = (m <= 512)
                am = V["AB"][:, 4 * k + cc:4 * k + cc + 1]
                bm = V["AB"][:, 4 * k + 2 + cc:4 * k + 2 + cc + 1]
                v_op(lambda e, mm=mm, bm=bm: e.tensor_scalar(
                    out=bq["i"][:, 0:mm], in0=sinT[:, 0:mm], scalar1=bm, scalar2=None,
                    op0=ALU.mult), chain=ch_,
                    waits=[(sV, m_ab)] if k == 0 else ())
                v_op(lambda e, m=m, mm=mm, am=am: e.scalar_tensor_tensor(
                    out=cosT[:, m:m + mm], in0=cosT[:, 0:mm], scalar=am, in1=bq["i"][:, 0:mm],
                    op0=ALU.mult, op1=ALU.subtract), chain=ch_)
                v_op(lambda e, mm=mm, bm=bm: e.tensor_scalar(
                    out=bq["i"][:, 0:mm], in0=cosT[:, 0:mm], scalar1=bm, scalar2=None,
                    op0=ALU.mult), chain=ch_)
                v_op(lambda e, m=m, mm=mm, am=am: e.scalar_tensor_tensor(
                    out=sinT[:, m:m + mm], in0=sinT[:, 0:mm], scalar=am, in1=bq["i"][:, 0:mm],
                    op0=ALU.mult, op1=ALU.add), chain=ch_)
            m_ph = sV.n

            for tt in range(4):
                bk_r = (2 * tt) % 3
                bk_i = (2 * tt + 1) % 3
                wv_r = [(sV, prod_done[bk_r])]
                wv_i = [(sV, prod_done[bk_i])]
                if first_bu:
                    wv_r += [(sA, m_w_done), (sV, m_A_v), (sA, m_xnT_a)]
                    first_bu = False
                wv_r.append((sA, cur_btb))
                wv_i.append((sA, cur_btb))
                def bfnr(e, d=d, cc=cc, tt=tt, pp=bu_ps[bk_r]):
                    last = None
                    for dc in range(NDC):
                        last = e.matmul(pp[:], BTbv["r"][:, dc, cc, :],
                                        xnT[:, dc * T + tt * 512: dc * T + tt * 512 + 512],
                                        start=(dc == 0), stop=(dc == NDC - 1))
                    return last
                m_mr = t_op(bfnr, waits=wv_r)
                def bfni(e, d=d, cc=cc, tt=tt, pp=bu_ps[bk_i]):
                    last = None
                    for dc in range(NDC):
                        last = e.matmul(pp[:], BTbv["i"][:, dc, cc, :],
                                        xnT[:, dc * T + tt * 512: dc * T + tt * 512 + 512],
                                        start=(dc == 0), stop=(dc == NDC - 1))
                    return last
                m_mi = t_op(bfni, waits=wv_i)
                ts_ = slice(tt * 512, (tt + 1) * 512)
                wv_p = [(sP, m_mi), (sV, m_ph), (sG, last_pool_rw)]
                v_op(lambda e, ts_=ts_, pp=bu_ps[bk_r]: e.tensor_tensor(
                    out=bq["r"][:, ts_], in0=cosT[:, ts_], in1=pp[:], op=ALU.mult),
                    waits=wv_p)
                v_op(lambda e, ts_=ts_, pp=bu_ps[bk_i]: e.tensor_tensor(
                    out=uu["r"][:, ts_], in0=sinT[:, ts_], in1=pp[:], op=ALU.mult))
                v_op(lambda e, ts_=ts_, pp=bu_ps[bk_i]: e.tensor_tensor(
                    out=bq["i"][:, ts_], in0=cosT[:, ts_], in1=pp[:], op=ALU.mult))
                m_p4 = v_op(lambda e, ts_=ts_, pp=bu_ps[bk_r]: e.tensor_tensor(
                    out=uu["i"][:, ts_], in0=sinT[:, ts_], in1=pp[:], op=ALU.mult))
                prod_done[bk_r] = m_p4
                prod_done[bk_i] = m_p4
                emit_h_units(1)
            A_ = (lambda ap: ap[:, ::-1]) if rev else (lambda ap: ap)
            sgn_r = ALU.subtract if rev else ALU.add
            sgn_i = ALU.add if rev else ALU.subtract
            m_c1 = v_op(lambda e, A_=A_, sgn_r=sgn_r: e.tensor_tensor(
                out=cmb["r"][:], in0=A_(bq["r"][:]), in1=A_(uu["r"][:]), op=sgn_r),
                waits=[(sV, m_p4)])
            m_c2 = g_op(lambda e, A_=A_, sgn_i=sgn_i: e.tensor_tensor(
                out=cmb["i"][:], in0=A_(bq["i"][:]), in1=A_(uu["i"][:]), op=sgn_i),
                waits=[(sV, m_p4)])
            rbc = V["rr"][:, cc:cc + 1].broadcast_to([128, T])
            m_s1 = v_op(lambda e, rbc=rbc: e.tensor_tensor_scan(
                out=bq["r"][:], data0=rbc, data1=cmb["r"][:], initial=0.0,
                op0=ALU.mult, op1=ALU.add), waits=[(sV, m_c1), (sA, m_r)])
            m_s2 = v_op(lambda e, rbc=rbc: e.tensor_tensor_scan(
                out=bq["i"][:], data0=rbc, data1=cmb["i"][:], initial=0.0,
                op0=ALU.mult, op1=ALU.add), waits=[(sG, m_c2)])
            emit_h_units(2)
            B_ = (lambda ap: ap[:, ::-1]) if rev else (lambda ap: ap)
            usgn_r = ALU.add if rev else ALU.subtract
            usgn_i = ALU.subtract if rev else ALU.add
            wv2 = [(sV, m_s1), (sV, m_s2)]
            mA = v_op(lambda e, B_=B_: e.tensor_tensor(out=uu["r"][:], in0=cosT[:],
                                                       in1=B_(bq["r"][:]), op=ALU.mult),
                      waits=wv2)
            mB = g_op(lambda e, B_=B_: e.tensor_tensor(out=uu["i"][:], in0=sinT[:],
                                                       in1=B_(bq["i"][:]), op=ALU.mult),
                      waits=wv2)
            mC = v_op(lambda e, d=d, cc=cc, usgn_r=usgn_r: e.tensor_tensor(
                out=xsT[(d, "r")][:, cc * T:(cc + 1) * T], in0=uu["r"][:], in1=uu["i"][:],
                op=usgn_r), waits=[(sG, mB)])
            mD = g_op(lambda e, B_=B_: e.tensor_tensor(out=uu["r"][:], in0=cosT[:],
                                                       in1=B_(bq["i"][:]), op=ALU.mult),
                      waits=wv2 + [(sV, mC)])
            mE = v_op(lambda e, B_=B_: e.tensor_tensor(out=uu["i"][:], in0=sinT[:],
                                                       in1=B_(bq["r"][:]), op=ALU.mult),
                      waits=[(sV, mC)])
            mF = v_op(lambda e, d=d, cc=cc, usgn_i=usgn_i: e.tensor_tensor(
                out=xsT[(d, "i")][:, cc * T:(cc + 1) * T], in0=uu["r"][:], in1=uu["i"][:],
                op=usgn_i), waits=[(sG, mD), (sV, mE)])
            last_scan_read_v = mF
            last_pool_rw = mD
            m_xs_all = mF
            emit_h_units(3)

    emit_h_units(NEXP * 4)
    m_hprod_a = sA.n
    m_hprod_v = sV.n

    es_psBH.close()
    es_ab.close()
    es_c1.close()
    es_c = contextlib.ExitStack()
    xrl_extra = [sbr(es_c, f"xrl{i + 2}", [128, 1024], F32) for i in range(2)]
    xrl_all = xrl + xrl_extra
    downstage = [sbt(es_c, f"downstage{i}", [128, 2048], F32) for i in range(3)]
    downb = [sbt(es_c, f"downb{i}", [128, 2048], BF16) for i in range(3)]
    hld = [sbt(es_c, f"hld{i}", [128, 2048], BF16) for i in range(2)]
    gt = [sbt(es_c, f"gt{i}", [128, 512], F32) for i in range(8)]
    stl = [sbt(es_c, f"stl{i}", [128, 512], F32) for i in range(8)]
    sadd = [sbt(es_c, f"sadd{i}", [128, 512], F32) for i in range(2)]
    es_psD = contextlib.ExitStack()
    d_ps = [es_psD.enter_context(nc.psum_tensor(f"d_ps{i}", [128, 512], F32)).ap()
            for i in range(8)]

    down_cast = [0, 0, 0]
    down_pe = [0, 0, 0]
    hld_pe = [0, 0]
    prev_ys_evac = 0
    prev_comb = 0
    xrl_war_v = [ln_v_marks[-2], ln_v_marks[-1], m_hprod_v, m_hprod_v]
    xrl_war_d = [0, 0, 0, 0]
    overlay_guard = {2: [(sA, m_hprod_a), (sG, sG.n),
                         (dHS[0], dHS[0].n), (dHS[1], dHS[1].n)],
                     3: [(sA, m_hprod_a), (sG, sG.n),
                         (dHS[0], dHS[0].n), (dHS[1], dHS[1].n)]}
    cur_xrl_ld = 0
    cur_xrl_sem = dX[0]
    sadd_war = [0, 0]
    dd_r = ddown.rearrange("e p j -> p e j")
    hs_r = hscr.rearrange("e t p c -> t p e c")
    dcast_all = {}
    hld_all = {}
    prev_geb_last = 0

    def plan_down_unit(s, u):
        k = u % 3
        m_dl = load(dDW[k], downstage[k].rearrange("p (e j) -> p e j", e=2),
                    dd_r[:, 2 * u:2 * u + 2, :],
                    waits=[(sV, down_cast[k])])
        m_dc = v_op(lambda e, k=k: e.tensor_copy(out=downb[k][:], in_=downstage[k][:]),
                    waits=[(dDW[k], m_dl), (sP, down_pe[k])])
        down_cast[k] = m_dc
        dcast_all[(s, u)] = m_dc

    def plan_hld_unit(s, v):
        kv = v % 2
        e0 = 4 * v
        wv_sp = [(dHS[((e0 + j) * 4 + s) % 2], h_store[(e0 + j) * 4 + s])
                 for j in range(4)]
        hld_all[(s, v)] = load(dHL[kv], hld[kv].rearrange("p (e c) -> p e c", e=4),
                               hs_r[s][:, e0:e0 + 4, :],
                               waits=wv_sp + [(sP, hld_pe[kv])])

    for s in range(4):
        m_last_mm = 0
        if s == 0:
            plan_down_unit(0, 0)
            plan_down_unit(0, 1)
            plan_down_unit(0, 2)
            plan_hld_unit(0, 0)
            plan_hld_unit(0, 1)
        for e_i in range(NEXP):
            u = e_i // 2
            k = u % 3
            v = e_i // 4
            kv = v % 2
            wv = [(sV, dcast_all[(s, u)]), (dHL[kv], hld_all[(s, v)])]
            if e_i == 0:
                wv += [(sA, prev_geb_last)]
                if s == 0:
                    wv += [(sA, m_hprod_a), (sV, m_hprod_v)]
            def dmm(e, e_i=e_i, k=k, kv=kv):
                last = None
                eo = (e_i % 2) * 1024
                ho = (e_i % 4) * 512
                for tc in range(4):
                    for jg in range(2):
                        last = e.matmul(d_ps[tc * 2 + jg][:],
                                        hld[kv][:, ho + tc * 128: ho + (tc + 1) * 128],
                                        downb[k][:, eo + jg * 512: eo + (jg + 1) * 512],
                                        start=(e_i == 0), stop=(e_i == NEXP - 1))
                return last
            m_mm = t_op(dmm, waits=wv)
            if e_i % 2 == 1:
                down_pe[k] = m_mm
                if u + 3 < 18:
                    plan_down_unit(s, u + 3)
            if e_i % 4 == 3:
                hld_pe[kv] = m_mm
                if v + 2 < 9:
                    plan_hld_unit(s, v + 2)
            m_last_mm = m_mm
        if s + 1 < 4:
            plan_down_unit(s + 1, 0)
            plan_down_unit(s + 1, 1)
            plan_down_unit(s + 1, 2)
            plan_hld_unit(s + 1, 0)
            plan_hld_unit(s + 1, 1)
        silu_marks = []
        for gi in range(8):
            m_si = a_op(lambda e, gi=gi: e.activation(out=gt[gi][:], in_=d_ps[gi][:],
                                                      func=AF.Silu),
                        waits=[(sP, m_last_mm), (sV, prev_comb)])
            silu_marks.append(m_si)
        ge_f = [0] * 8
        ysf_m = [0] * 8
        def ymm_mk(dname, tok, jg, pf):
            def ymm(e, d=dname, tok=tok, jg=jg, pf=pf):
                last = None
                first = True
                for cc in range(2):
                    for ri in "ri":
                        last = e.matmul(
                            pf[:],
                            xsT[(d, ri)][:, cc * T + tok: cc * T + tok + 128],
                            CTb[(d, ri)][:, cc * 1024 + jg * 512:
                                         cc * 1024 + jg * 512 + 512],
                            start=first, stop=False)
                        first = False
                for k4 in range(4):
                    dc = jg * 4 + k4
                    last = e.matmul(pf[:, k4 * 128:(k4 + 1) * 128],
                                    xnT[:, dc * T + tok: dc * T + tok + 128],
                                    diagD[d][:, dc * 128:(dc + 1) * 128],
                                    start=False, stop=(k4 == 3))
                return last
            return ymm
        for gi in range(8):
            tc, jg = gi // 2, gi % 2
            tok = s * 512 + tc * 128
            pf = d_ps[gi]
            wv = [(sV, m_xs_all), (sV, m_ct_cast[("f", "i")]),
                  (sV, m_ct_cast[("f", "r")]), (sA, silu_marks[gi])]
            ysf_m[gi] = t_op(ymm_mk("f", tok, jg, pf), waits=wv)
            ge_f[gi] = a_op(lambda e, gi=gi, pf=pf: e.activation(
                out=stl[gi][:], in_=pf[:], func=AF.Gelu),
                waits=[(sP, ysf_m[gi]), (sV, prev_comb)])
        ysb_info = []
        for gi in range(8):
            tc, jg = gi // 2, gi % 2
            tok = s * 512 + tc * 128
            tcg = s * 4 + tc
            kx = tcg % 4
            pf = d_ps[gi]
            if jg == 0:
                wv_x = [(sV, xrl_war_v[kx]), (dOS[kx], xrl_war_d[kx])]
                wv_x += overlay_guard.pop(kx, [])
                cur_xrl_ld = g_dma(dX[kx], xrl_all[kx][:],
                                   dx[tcg * 128:(tcg + 1) * 128, :], waits=wv_x)
                cur_xrl_sem = dX[kx]
            wv = [(sV, m_ct_cast[("b", "i")]), (sV, m_ct_cast[("b", "r")]),
                  (sA, ge_f[gi])]
            m_ysb = t_op(ymm_mk("b", tok, jg, pf), waits=wv)
            m_geb = a_op(lambda e, gi=gi, pf=pf: e.activation(
                out=sadd[gi % 2][:], in_=pf[:], func=AF.Gelu),
                waits=[(sP, m_ysb), (sV, sadd_war[gi % 2])])
            prev_geb_last = m_geb
            m_sadd = v_op(lambda e, gi=gi: e.tensor_tensor(
                out=stl[gi][:], in0=stl[gi][:], in1=sadd[gi % 2][:], op=ALU.add),
                waits=[(sA, m_geb)])
            sadd_war[gi % 2] = m_sadd
            ysb_info.append((gi, jg, kx, tcg, cur_xrl_ld, cur_xrl_sem))
        for gi, jg, kx, tcg, xl, xl_sem in ysb_info:
            m_og = v_op(lambda e, gi=gi: e.tensor_tensor(
                out=gt[gi][:], in0=stl[gi][:], in1=gt[gi][:], op=ALU.mult))
            m_out = v_op(lambda e, gi=gi, kx=kx, jg=jg: e.tensor_tensor(
                out=xrl_all[kx][:, jg * 512:(jg + 1) * 512], in0=gt[gi][:],
                in1=xrl_all[kx][:, jg * 512:(jg + 1) * 512], op=ALU.add),
                waits=[(sV, m_og), (xl_sem, xl)])
            prev_comb = m_out
            if jg == 1:
                m_so = g_dma(dOS[kx], dout[tcg * 128:(tcg + 1) * 128, :], xrl_all[kx][:],
                             waits=[(sV, m_out)])
                xrl_war_v[kx] = m_out
                xrl_war_d[kx] = m_so

    fin_waits = [(dOS[i], dOS[i].n) for i in range(4)] + \
                [(dHS[0], dHS[0].n), (dHS[1], dHS[1].n)]

    def fin(e, fin_waits=fin_waits):
        for s_, v_ in fin_waits:
            e.wait_ge(s_.h, v_)
    q("gpsimd", fin)

    es_psD.close()
    es_c.close()

    with nc.Block() as block:
        @block.sync
        def _(e):
            for fn in PL["sync"]:
                fn(e)

        @block.scalar
        def _(e):
            for fn in PL["scalar"]:
                fn(e)

        @block.vector
        def _(e):
            for fn in PL["vector"]:
                fn(e)

        @block.tensor
        def _(e):
            for fn in PL["tensor"]:
                fn(e)

        @block.gpsimd
        def _(e):
            for fn in PL["gpsimd"]:
                fn(e)

    return nc


def _pack_weights(i):
    f32 = np.float32
    gup = np.empty((NEXP, 128, 2048), dtype=f32)
    down = np.empty((NEXP, 128, 1024), dtype=f32)
    gate_w = np.asarray(i["L0_ffn_gate_w"], dtype=f32)
    up_w = np.asarray(i["L0_ffn_up_w"], dtype=f32)
    down_w = np.asarray(i["L0_ffn_down_w"], dtype=f32)
    for e in range(32):
        for gu, w in ((0, gate_w[e]), (1, up_w[e])):
            wT = w.T.reshape(8, 128, 128)
            gup[e, :, gu * 1024:(gu + 1) * 1024] = wT.transpose(1, 0, 2).reshape(128, 1024)
        down[e] = down_w[e].T
    sgT = np.asarray(i["L0_ffn_sgate_w"], dtype=f32).T
    suT = np.asarray(i["L0_ffn_sup_w"], dtype=f32).T
    sdT = np.asarray(i["L0_ffn_sdown_w"], dtype=f32).T
    for hc in range(4):
        for gu, wT_full in ((0, sgT), (1, suT)):
            wT = wT_full[:, hc * 128:(hc + 1) * 128].reshape(8, 128, 128)
            gup[32 + hc, :, gu * 1024:(gu + 1) * 1024] = \
                wT.transpose(1, 0, 2).reshape(128, 1024)
        down[32 + hc] = sdT[hc * 128:(hc + 1) * 128, :]
    routerT = np.ascontiguousarray(
        np.asarray(i["L0_ffn_router_w"], dtype=f32).T.reshape(8, 128, 32)
        .transpose(1, 0, 2).reshape(128, 256))
    rbias = np.ascontiguousarray(np.asarray(i["L0_ffn_router_bias"], dtype=f32).reshape(1, 32))
    out = {"gupT": np.ascontiguousarray(gup), "downT": np.ascontiguousarray(down),
           "routerT": routerT, "rbias": rbias}
    for d, pfx in (("f", "fwd"), ("b", "bwd")):
        Br = np.asarray(i[f"L0_attn_{pfx}_B_real"], dtype=f32).reshape(2, 128, 1024)
        Bi = np.asarray(i[f"L0_attn_{pfx}_B_imag"], dtype=f32).reshape(2, 128, 1024)
        out[f"B_{d}"] = np.ascontiguousarray(
            np.stack([Br, Bi]).transpose(2, 0, 1, 3).reshape(128, 4096))
        Cr = np.asarray(i[f"L0_attn_{pfx}_C_real"], dtype=f32).T.reshape(2, 128, 1024)
        Ci = np.asarray(i[f"L0_attn_{pfx}_C_imag"], dtype=f32).T.reshape(2, 128, 1024)
        out[f"CT_{d}"] = np.ascontiguousarray(
            np.stack([Cr, Ci]).transpose(2, 0, 1, 3).reshape(128, 4096))
        lr = np.asarray(i[f"L0_attn_{pfx}_log_real"], dtype=f32).reshape(2, 128)
        im = np.asarray(i[f"L0_attn_{pfx}_imag"], dtype=f32).reshape(2, 128)
        ld = np.asarray(i[f"L0_attn_{pfx}_log_Delta"], dtype=f32).reshape(2, 128)
        out[f"vec_{d}"] = np.ascontiguousarray(
            np.stack([lr, im, ld]).transpose(2, 0, 1).reshape(128, 6))
        out[f"D_{d}"] = np.ascontiguousarray(
            np.asarray(i[f"L0_attn_{pfx}_D"], dtype=f32).reshape(8, 128).T)
    return out


def kernel(**inputs):
    shared = _pack_weights(inputs)
    x = np.asarray(inputs["L0_attn_x"], dtype=np.float32)
    nc = build_program()
    in_maps = []
    for b in range(8):
        m = dict(shared)
        m["x"] = np.ascontiguousarray(x[b])
        in_maps.append(m)
    res = run_bass_kernel_spmd(nc, in_maps, core_ids=list(range(8)))
    return np.stack([r["out"] for r in res.results]).astype(np.float32)
